# version 01
# baseline (speedup 1.0000x reference)
import numpy as np
import ml_dtypes
from contextlib import ExitStack
import concourse.bass as bass
import concourse.mybir as mybir
from concourse.bass_utils import run_bass_kernel_spmd

F32 = mybir.dt.float32
BF16 = mybir.dt.bfloat16
AF = mybir.ActivationFunctionType
ALU = mybir.AluOpType
AX = mybir.AxisListType.X

T = 4096
D = 1024
NH = 8
NI = 2816
NIC = NI // 128
NKC = D // 128
TILE = 512
NT = T // TILE
NTC = T // 128
TG = 1024
NTG = T // TG
TWO23 = 12582912.0
QMAX = 127.0
LN_EPS = 1e-5
RMS_EPS = 1e-6
FORGET_LB = 0.9


class Sem:
    def __init__(self, handle):
        self.h = handle
        self.n = 0

    def inc(self, instr, k=1):
        self.n += k
        instr.then_inc(self.h, k)
        return self.n

    def dma(self, instr):
        return self.inc(instr, 16)


class Chain:

    def __init__(self, eng, sem):
        self.eng = eng
        self.sem = sem

    def __call__(self, instr):
        self.sem.inc(instr)
        self.eng.wait_ge(self.sem.h, self.sem.n)
        return instr


def build_program(dbg=False, stage=7, nt_run=NT):
    nc = bass.Bass("TRN2", target_bir_lowering=False, debug=False)

    x_d = nc.dram_tensor("x", [T, D], F32, kind="ExternalInput").ap()
    wf_d = nc.dram_tensor("wf", [D, D], F32, kind="ExternalInput").ap()
    wi_d = nc.dram_tensor("wi", [D, D], F32, kind="ExternalInput").ap()
    wo_d = nc.dram_tensor("wo", [D, D], F32, kind="ExternalInput").ap()
    nw_d = nc.dram_tensor("nw", [D], F32, kind="ExternalInput").ap()
    cx_d = nc.dram_tensor("cx", [1], F32, kind="ExternalInput").ap()
    cd_d = nc.dram_tensor("cd", [1], F32, kind="ExternalInput").ap()
    wg_d = nc.dram_tensor("wg", [NI, D], F32, kind="ExternalInput").ap()
    wu_d = nc.dram_tensor("wu", [NI, D], F32, kind="ExternalInput").ap()
    wd_d = nc.dram_tensor("wd", [D, NI], F32, kind="ExternalInput").ap()
    y_d = nc.dram_tensor("y", [T, D], F32, kind="ExternalOutput").ap()
    if dbg:
        y2_d = nc.dram_tensor("y2", [T, D], F32, kind="ExternalOutput").ap()
        y2_t4 = y2_d.rearrange("(n c p) d -> n p c d", n=NT, c=4, p=128)

    x2_d = nc.dram_tensor("x2buf", [T, D], F32).ap()
    h_d = nc.dram_tensor("hbuf", [NI, T], F32).ap()
    ccxi_d = nc.dram_tensor("ccxi", [1, 128], F32)
    ccxo_d = nc.dram_tensor("ccxo", [1, 128], F32)
    cchi_d = nc.dram_tensor("cchi", [1, 128], F32)
    ccho_d = nc.dram_tensor("ccho", [1, 128], F32)

    x_t4 = x_d.rearrange("(n c p) d -> n p c d", n=NT, c=4, p=128)
    x2_t4 = x2_d.rearrange("(n c p) d -> n p c d", n=NT, c=4, p=128)
    x2_tc = x2_d.rearrange("(n p) d -> n p d", n=NTC, p=128)
    y_tc = y_d.rearrange("(n p) d -> n p d", n=NTC, p=128)
    h_ic = h_d.rearrange("(a p) t -> a p t", a=NIC, p=128)

    eye32 = nc.inline_tensor(np.eye(128, dtype=np.float32), "ceye32").ap()
    eyeb = nc.inline_tensor(np.eye(128, dtype=ml_dtypes.bfloat16), "ceyeb").ap()
    hones_np = np.zeros((128, NH * NH), dtype=np.float32)
    for hh in range(NH):
        hones_np[:, hh * NH + hh] = 1.0
    hones = nc.inline_tensor(hones_np, "chones").ap()
    selb_np = np.zeros((NH, NH * 128), dtype=np.float32)
    for hh in range(NH):
        selb_np[hh, hh * 128:(hh + 1) * 128] = 1.0
    selb = nc.inline_tensor(selb_np, "cselb").ap()
    ones1 = nc.inline_tensor(np.ones((1, 128), dtype=np.float32), "cones1").ap()

    es = ExitStack()
    with es:
        e = es.enter_context

        qs = Sem(e(nc.semaphore("qs")))
        qg = Sem(e(nc.semaphore("qg")))
        sv = Sem(e(nc.semaphore("sv")))
        ss = Sem(e(nc.semaphore("ss")))
        st = Sem(e(nc.semaphore("st")))
        sg = Sem(e(nc.semaphore("sg")))
        scc = Sem(e(nc.semaphore("scc")))

        eyet = e(nc.sbuf_tensor("eyet", [128, 128], F32))
        eyebt = e(nc.sbuf_tensor("eyebt", [128, 128], BF16))
        honet = e(nc.sbuf_tensor("honet", [128, NH * NH], F32))
        selbt = e(nc.sbuf_tensor("selbt", [NH, NH * 128], F32))
        ones1t = e(nc.sbuf_tensor("ones1t", [1, 128], F32))
        nwt = e(nc.sbuf_tensor("nwt", [128, NKC], F32))
        m2t = e(nc.sbuf_tensor("m2t", [128, NTC], F32))
        rs2t = e(nc.sbuf_tensor("rs2t", [128, NTC], F32))
        qsc = e(nc.sbuf_tensor("qsc", [128, NTC], F32))
        qbi = e(nc.sbuf_tensor("qbi", [128, NTC], F32))
        carry = e(nc.sbuf_tensor("carry", [128, NH], F32))
        lnmax = e(nc.sbuf_tensor("lnmax", [128, 1], F32))
        hmax = e(nc.sbuf_tensor("hmax", [128, 1], F32))
        sxv = e(nc.sbuf_tensor("sxv", [128, 4], F32))
        shv = e(nc.sbuf_tensor("shv", [128, 4], F32))
        sgc = e(nc.sbuf_tensor("sgc", [1, 8], F32))
        ccrow = e(nc.sbuf_tensor("ccrow", [1, 128], F32))
        swgt = e(nc.sbuf_tensor("swgt", [128, NIC], F32))
        swut = e(nc.sbuf_tensor("swut", [128, NIC], F32))
        swdt = e(nc.sbuf_tensor("swdt", [128, NKC], F32))
        stat = e(nc.sbuf_tensor("stat", [128, 40], F32))
        statS = e(nc.sbuf_tensor("statS", [128, 8], F32))
        c23t = e(nc.sbuf_tensor("c23t", [128, 1], F32))
        open_stacks = []
        p1w = ExitStack()
        open_stacks.append(p1w)
        p1we = p1w.enter_context
        x_t = p1we(nc.sbuf_tensor("x_t", [128, 2 * 4096], F32))
        wfThi = p1we(nc.sbuf_tensor("wfThi", [128, NKC * D], BF16))
        wfTlo = p1we(nc.sbuf_tensor("wfTlo", [128, NKC * D], BF16))
        woThi = p1we(nc.sbuf_tensor("woThi", [128, NKC * D], BF16))
        woTlo = p1we(nc.sbuf_tensor("woTlo", [128, NKC * D], BF16))
        uThi = p1we(nc.sbuf_tensor("uThi", [128, NKC * NH], BF16))
        uTlo = p1we(nc.sbuf_tensor("uTlo", [128, NKC * NH], BF16))

        def x_sl(p):
            return x_t[:, p * 4096:(p + 1) * 4096]

        block = e(nc.Block())

        pro = ExitStack()
        pe_ = pro.enter_context
        wtile = pe_(nc.sbuf_tensor("wtile", [128, 2 * D], F32))
        u_sb = pe_(nc.sbuf_tensor("u_sb", [NH, D], F32))
        ps_w = pe_(nc.psum_tensor("ps_w", [128, 1024], F32))
        ps_u = pe_(nc.psum_tensor("ps_u", [NH, 1024], F32))

        @block.sync
        def _(sync):
            for dst, src in ((eyet, eye32), (eyebt, eyeb), (honet, hones),
                             (selbt, selb), (ones1t, ones1)):
                qs.dma(sync.dma_start(out=dst[:, :], in_=src))
            qs.dma(sync.dma_start(out=nwt[:, :],
                                  in_=nw_d.rearrange("(c p) -> p c", p=128),
                                  allow_slow_non_contiguous=True))
            qs.dma(sync.dma_start(out=sgc[0:1, 0:1], in_=cx_d.rearrange("(a o) -> a o", a=1)))
            qs.dma(sync.dma_start(out=sgc[0:1, 1:2], in_=cd_d.rearrange("(a o) -> a o", a=1)))
        ev_const = qs.n

        ev_xload = []
        @block.sync
        def _(sync):
            qs.dma(sync.dma_start(out=x_sl(0), in_=x_t4[0]))
        ev_xload.append(qs.n)
        @block.sync
        def _(sync):
            qs.dma(sync.dma_start(out=x_sl(1), in_=x_t4[1]))
        ev_xload.append(qs.n)

        ev_wi = []
        ev_umm_h = []
        for hh in range(NH):
            @block.sync
            def _(sync, hh=hh):
                if hh >= 2:
                    sync.wait_ge(st.h, ev_umm_h[hh - 2])
                qs.dma(sync.dma_start(out=wtile[:, (hh % 2) * D:(hh % 2) * D + D],
                                      in_=wi_d[hh * 128:(hh + 1) * 128, :]))
            ev_wi.append(qs.n)

            @block.tensor
            def _(tensor, hh=hh):
                if hh == 0:
                    tensor.wait_ge(qs.h, ev_const)
                tensor.wait_ge(qs.h, ev_wi[hh])
                for half in range(2):
                    st.inc(tensor.matmul(
                        ps_u[:, half * 512:(half + 1) * 512],
                        honet[:, hh * NH:(hh + 1) * NH],
                        wtile[:, (hh % 2) * D + half * 512:(hh % 2) * D + (half + 1) * 512],
                        start=(hh == 0), stop=(hh == NH - 1)))
            ev_umm_h.append(st.n)
        ev_umm = st.n

        @block.vector
        def _(vector):
            vector.wait_ge(st.h, ev_umm)
            sv.inc(vector.tensor_copy(u_sb[:, :], ps_u[:, :]))
        ev_usb = sv.n

        @block.tensor
        def _(tensor):
            tensor.wait_ge(sv.h, ev_usb)
            for kc in range(NKC):
                st.inc(tensor.transpose(ps_w[:, kc * NH:(kc + 1) * NH],
                                        u_sb[:, kc * 128:(kc + 1) * 128],
                                        eyet[0:NH, 0:NH]))
        ev_ut = st.n

        @block.vector
        def _(vector):
            ch = Chain(vector, sv)
            vector.wait_ge(st.h, ev_ut)
            for kc in range(NKC):
                ch(vector.tensor_scalar(
                    uThi[:, kc * NH:(kc + 1) * NH], ps_w[:, kc * NH:(kc + 1) * NH],
                    nwt[:, kc:kc + 1], None, ALU.mult))
                ch(vector.scalar_tensor_tensor(
                    uTlo[:, kc * NH:(kc + 1) * NH], ps_w[:, kc * NH:(kc + 1) * NH],
                    nwt[:, kc:kc + 1], uThi[:, kc * NH:(kc + 1) * NH],
                    ALU.mult, ALU.subtract))
        ev_utsplit = sv.n

        def prep_weight(w_dram, whi, wlo, init_v, init_s, init_tr):
            ev_ld = []
            evac_v = [init_v]
            evac_s = [init_s]
            ev_tr_all = {-2: init_tr[0], -1: init_tr[1]}
            for oc in range(NKC):
                @block.sync
                def _(sync, oc=oc):
                    sync.wait_ge(st.h, ev_tr_all[oc - 2])
                    qs.dma(sync.dma_start(
                        out=wtile[:, (oc % 2) * D:(oc % 2) * D + D],
                        in_=w_dram[oc * 128:(oc + 1) * 128, :]))
                ev_ld.append(qs.n)
                ev_tr = []
                @block.tensor
                def _(tensor, oc=oc):
                    tensor.wait_ge(qs.h, ev_ld[oc])
                    tensor.wait_ge(sv.h, evac_v[-1])
                    tensor.wait_ge(ss.h, evac_s[-1])
                    for kc in range(NKC):
                        ev_tr.append(st.inc(tensor.transpose(
                            ps_w[:, kc * 128:(kc + 1) * 128],
                            wtile[:, (oc % 2) * D + kc * 128:(oc % 2) * D + (kc + 1) * 128],
                            eyet[:, :])))
                ev_tr_all[oc] = st.n
                ev_hi = []
                @block.scalar
                def _(scalar, oc=oc):
                    for kc in range(NKC):
                        scalar.wait_ge(st.h, ev_tr[3 if kc < 4 else 7])
                        ev_hi.append(ss.inc(scalar.activation(
                            whi[:, kc * D + oc * 128:kc * D + (oc + 1) * 128],
                            ps_w[:, kc * 128:(kc + 1) * 128],
                            AF.Identity, scale=nwt[:, kc:kc + 1])))
                @block.vector
                def _(vector, oc=oc):
                    for kc in range(NKC):
                        vector.wait_ge(ss.h, ev_hi[3 if kc < 4 else 7])
                        sv.inc(vector.scalar_tensor_tensor(
                            wlo[:, kc * D + oc * 128:kc * D + (oc + 1) * 128],
                            ps_w[:, kc * 128:(kc + 1) * 128],
                            nwt[:, kc:kc + 1],
                            whi[:, kc * D + oc * 128:kc * D + (oc + 1) * 128],
                            ALU.mult, ALU.subtract))
                evac_v.append(sv.n)
                evac_s.append(ss.n)
            return evac_v[-1], evac_s[-1], (ev_tr_all[NKC - 2], ev_tr_all[NKC - 1])

        wf_v, wf_s, wf_tr = prep_weight(wf_d, wfThi, wfTlo, ev_utsplit, ss.n,
                                        (ev_umm, ev_umm))
        ev_pro_v, ev_pro_s, _ = prep_weight(wo_d, woThi, woTlo, wf_v, wf_s, wf_tr)
        pro.close()

        def _finish():
            for stk in reversed(open_stacks):
                stk.close()
            @block.sync
            def _(sync):
                sync.wait_ge(qs.h, qs.n)
                sync.wait_ge(sv.h, sv.n)
                sync.wait_ge(ss.h, ss.n)
                sync.wait_ge(st.h, st.n)
            @block.gpsimd
            def _(g):
                g.wait_ge(qg.h, qg.n)
                g.wait_ge(sg.h, sg.n)

        if stage <= 0:
            _finish()
            return nc

        p1 = ExitStack()
        open_stacks.append(p1)
        p1e = p1.enter_context
        z_t = p1e(nc.sbuf_tensor("z_t", [128, 2 * 1024], F32))
        r_t = p1e(nc.sbuf_tensor("r_t", [128, 4096], F32))
        x2_t = p1e(nc.sbuf_tensor("x2_t", [128, 4096], F32))
        zThi = p1e(nc.sbuf_tensor("zThi", [128, NKC * TILE], BF16))
        zTlo = p1e(nc.sbuf_tensor("zTlo", [128, NKC * TILE], BF16))
        y_tile = p1e(nc.sbuf_tensor("y_tile", [128, 4096], F32))
        f_b = p1e(nc.sbuf_tensor("f_b", [128, 2 * TILE], F32))
        i_b = p1e(nc.sbuf_tensor("i_b", [128, 2 * TILE], F32))
        o_b = p1e(nc.sbuf_tensor("o_b", [128, 2 * TILE], F32))
        g_b = p1e(nc.sbuf_tensor("g_b", [128, 2 * TILE], F32))
        v_sb = p1e(nc.sbuf_tensor("v_sb", [NH, TILE], F32))
        tmp_t = p1e(nc.sbuf_tensor("tmp_t", [128, 1024], F32))
        trash = p1e(nc.sbuf_tensor("trash", [128, 1024], F32))
        ps_rot = [p1e(nc.psum_tensor("ps_rotA", [128, 1024], F32)),
                  p1e(nc.psum_tensor("ps_rotB", [128, 1024], F32))]
        ps_f = p1e(nc.psum_tensor("ps_f", [128, TILE], F32))
        ps_o = p1e(nc.psum_tensor("ps_o", [128, TILE], F32))
        ps_v = p1e(nc.psum_tensor("ps_v", [NH, TILE], F32))
        ps_vb = p1e(nc.psum_tensor("ps_vb", [128, TILE], F32))

        @block.vector
        def _(vector):
            vector.wait_ge(qs.h, ev_const)
            vector.memset(carry[:, :], 0.0)
            vector.memset(lnmax[:, :], 0.0)
            vector.memset(c23t[:, :], TWO23)
            sv.inc(vector.memset(hmax[:, :], 0.0))

        rot_guard = [None, None]
        rot_idx = [0]

        def rot_next():
            s_ = rot_idx[0] % 2
            rot_idx[0] += 1
            return s_

        MEAN = slice(0, 4); SQ = slice(4, 8); VARP = slice(8, 12)
        A2 = slice(12, 16); R0 = slice(16, 20); WRK = slice(20, 24)
        MEAN2 = slice(24, 28); WRK2 = slice(28, 32)

        ev_z = {}; ev_r = {}; ev_zt = {}; ev_zhi = {}; ev_zlo = {}
        ev_vsb = {}; ev_fsig = {}; ev_osig = {}; ev_g = {}; ev_ymul = {}
        ev_ytile = {}; ev_yt = {}; ev_x2add = {}; ev_x2st = {}
        ev_sq2 = {}; ev_amax4 = {}; ev_cxn = {}; ev_xdone = {}; ev_rs2 = {}
        ev_fproj = {}; ev_oproj = {}; ev_vb = {}; ev_vmm = {}

        for i in range(nt_run + 1):
            par = i % 2
            la = i - 1

            if i + 1 < nt_run:
                @block.sync
                def _(sync, i=i):
                    if i - 1 >= 0:
                        sync.wait_ge(ss.h, ev_xdone[i - 1])
                    qs.dma(sync.dma_start(out=x_sl((i + 1) % 2), in_=x_t4[i + 1]))
                ev_xload.append(qs.n)

            if la >= 0:
                for tc in range(4):
                    slot = rot_next()
                    @block.tensor
                    def _(tensor, la=la, tc=tc, slot=slot):
                        tensor.wait_ge(sv.h, ev_ytile[la])
                        if rot_guard[slot] is not None:
                            kind, val = rot_guard[slot]
                            tensor.wait_ge(kind.h, val)
                        for oc in range(NH):
                            st.inc(tensor.transpose(
                                ps_rot[slot][:, oc * 128:(oc + 1) * 128],
                                y_tile[:, oc * TILE + tc * 128:oc * TILE + (tc + 1) * 128],
                                eyet[:, :]))
                    ev_yt[(la, tc)] = st.n

                    @block.vector
                    def _(vector, la=la, tc=tc, slot=slot):
                        vector.wait_ge(st.h, ev_yt[(la, tc)])
                        vector.wait_ge(ss.h, ev_r[la])
                        if la >= 1 and tc == 0:
                            vector.wait_ge(ss.h, ev_sq2[la - 1])
                            vector.wait_ge(qs.h, ev_x2st[la - 1])
                        ev_x2add[(la, tc)] = sv.inc(vector.tensor_tensor(
                            x2_t[:, tc * 1024:(tc + 1) * 1024],
                            r_t[:, tc * 1024:(tc + 1) * 1024],
                            ps_rot[slot][:, :], ALU.add))
                    rot_guard[slot] = (sv, ev_x2add[(la, tc)])

                @block.vector
                def _(vector, la=la):
                    ch = Chain(vector, sv)
                    ch(vector.tensor_reduce(
                        stat[:, MEAN2],
                        x2_t[:, :].rearrange("p (c d) -> p c d", c=4, d=1024),
                        AX, ALU.add))
                    ch(vector.tensor_scalar(m2t[:, la * 4:(la + 1) * 4],
                                            stat[:, MEAN2], 1.0 / D, None,
                                            ALU.mult))
                    for tc in range(4):
                        vector.tensor_scalar(
                            tmp_t[:, :], x2_t[:, tc * 1024:(tc + 1) * 1024],
                            m2t[:, la * 4 + tc:la * 4 + tc + 1], None, ALU.subtract)
                        last = vector.tensor_reduce(
                            stat[:, 32 + tc:33 + tc], tmp_t[:, :], AX, ALU.max,
                            apply_absolute_value=True)
                    sv.inc(last)
                ev_amax4[la] = sv.n

                @block.scalar
                def _(scalar, la=la):
                    scalar.wait_ge(sv.h, ev_x2add[(la, 3)])
                    if la >= 1:
                        scalar.wait_ge(sv.h, ev_rs2[la - 1])
                    scalar.memzero(statS[:, 4:8])
                    for tc in range(4):
                        last = scalar.activation(
                            trash[:, :], x2_t[:, tc * 1024:(tc + 1) * 1024],
                            AF.Square, accum_out=statS[:, 4 + tc:5 + tc])
                    ss.inc(last)
                ev_sq2[la] = ss.n

            if i < nt_run:
                @block.scalar
                def _(scalar, i=i, par=par):
                    scalar.wait_ge(qs.h, ev_xload[i])
                    if i >= 1:
                        scalar.wait_ge(sv.h, ev_cxn[i - 1])
                    scalar.memzero(statS[:, 0:4])
                    for tc in range(4):
                        last = scalar.activation(
                            trash[:, :], x_sl(par)[:, tc * 1024:(tc + 1) * 1024],
                            AF.Square, accum_out=statS[:, tc:tc + 1])
                    ss.inc(last)
                ev_sqx = ss.n

                @block.vector
                def _(vector, i=i, par=par, la=la):
                    ch = Chain(vector, sv)
                    vector.wait_ge(qs.h, ev_xload[i])
                    ch(vector.tensor_reduce(
                        stat[:, MEAN],
                        x_sl(par)[:, :].rearrange("p (c d) -> p c d", c=4, d=1024),
                        AX, ALU.add))
                    ch(vector.tensor_scalar(stat[:, MEAN], stat[:, MEAN],
                                            1.0 / D, None, ALU.mult))
                    vector.wait_ge(ss.h, ev_sqx)
                    ch(vector.tensor_scalar(stat[:, VARP], statS[:, 0:4],
                                            1.0 / D, None, ALU.mult))
                    ch(vector.tensor_tensor(stat[:, WRK2], stat[:, MEAN],
                                            stat[:, MEAN], ALU.mult))
                    ch(vector.tensor_tensor(stat[:, VARP], stat[:, VARP],
                                            stat[:, WRK2], ALU.subtract))
                    ch(vector.tensor_scalar(stat[:, VARP], stat[:, VARP],
                                            LN_EPS, None, ALU.add))
                    ch(vector.tensor_scalar(stat[:, A2], stat[:, VARP],
                                            1.0 + RMS_EPS, -LN_EPS, ALU.mult, ALU.add))
                    ch(vector.reciprocal(stat[:, R0], stat[:, VARP]))
                    ch(vector.reciprocal(stat[:, WRK], stat[:, A2]))
                    if la >= 0:
                        vector.wait_ge(ss.h, ev_sq2[la])
                        ch(vector.tensor_scalar(stat[:, MEAN2], statS[:, 4:8],
                                                1.0 / D, None, ALU.mult))
                        ch(vector.tensor_tensor(stat[:, SQ],
                                                m2t[:, la * 4:(la + 1) * 4],
                                                m2t[:, la * 4:(la + 1) * 4], ALU.mult))
                        ch(vector.tensor_tensor(stat[:, MEAN2], stat[:, MEAN2],
                                                stat[:, SQ], ALU.subtract))
                        ch(vector.tensor_scalar(stat[:, MEAN2], stat[:, MEAN2],
                                                LN_EPS, None, ALU.add))
                        ch(vector.reciprocal(stat[:, SQ], stat[:, MEAN2]))
                ev_recip = sv.n

                @block.scalar
                def _(scalar, la=la):
                    scalar.wait_ge(sv.h, ev_recip)
                    scalar.activation(stat[:, R0], stat[:, R0], AF.Sqrt)
                    last = scalar.activation(stat[:, WRK], stat[:, WRK], AF.Sqrt)
                    if la >= 0:
                        last = scalar.activation(stat[:, SQ], stat[:, SQ], AF.Sqrt)
                    ss.inc(last)
                ev_sqrt = ss.n

                @block.vector
                def _(vector, i=i, la=la):
                    ch = Chain(vector, sv)
                    vector.wait_ge(ss.h, ev_sqrt)
                    if la >= 0:
                        ch(vector.tensor_tensor(stat[:, WRK2], stat[:, SQ],
                                                stat[:, SQ], ALU.mult))
                        ch(vector.tensor_tensor(stat[:, WRK2], stat[:, WRK2],
                                                stat[:, MEAN2], ALU.mult))
                        ch(vector.tensor_scalar(stat[:, WRK2], stat[:, WRK2],
                                                -0.5, 1.5, ALU.mult, ALU.add))
                        ch(vector.tensor_tensor(stat[:, SQ], stat[:, SQ],
                                                stat[:, WRK2], ALU.mult))
                        ch(vector.tensor_copy(rs2t[:, la * 4:(la + 1) * 4],
                                              stat[:, SQ]))
                        vector.wait_ge(sv.h, ev_amax4[la])
                        ch(vector.tensor_tensor(stat[:, 32:36], stat[:, 32:36],
                                                stat[:, SQ], ALU.mult))
                        ch(vector.tensor_reduce(stat[:, 36:37], stat[:, 32:36],
                                                AX, ALU.max))
                        ch(vector.tensor_tensor(lnmax[:, :], lnmax[:, :],
                                                stat[:, 36:37], ALU.max))
                        ev_rs2[la] = sv.n
                    ch(vector.tensor_tensor(stat[:, WRK2], stat[:, R0],
                                            stat[:, R0], ALU.mult))
                    ch(vector.tensor_tensor(stat[:, WRK2], stat[:, WRK2],
                                            stat[:, VARP], ALU.mult))
                    ch(vector.tensor_scalar(stat[:, WRK2], stat[:, WRK2],
                                            -0.5, 1.5, ALU.mult, ALU.add))
                    ch(vector.tensor_tensor(stat[:, R0], stat[:, R0],
                                            stat[:, WRK2], ALU.mult))
                    ch(vector.tensor_tensor(stat[:, WRK2], stat[:, WRK],
                                            stat[:, WRK], ALU.mult))
                    ch(vector.tensor_tensor(stat[:, WRK2], stat[:, WRK2],
                                            stat[:, A2], ALU.mult))
                    ch(vector.tensor_scalar(stat[:, WRK2], stat[:, WRK2],
                                            -0.5, 1.5, ALU.mult, ALU.add))
                    ch(vector.tensor_tensor(stat[:, WRK], stat[:, WRK],
                                            stat[:, WRK2], ALU.mult))
                    ch(vector.tensor_tensor(stat[:, A2], stat[:, MEAN],
                                            stat[:, WRK], ALU.mult))
                    ch(vector.tensor_scalar(stat[:, A2], stat[:, A2],
                                            -1.0, None, ALU.mult))
                    ch(vector.tensor_scalar(stat[:, VARP], stat[:, R0],
                                            1.0, None, ALU.add))
                    ch(vector.tensor_tensor(stat[:, WRK2], stat[:, MEAN],
                                            stat[:, R0], ALU.mult))
                    ch(vector.tensor_scalar(stat[:, WRK2], stat[:, WRK2],
                                            -1.0, None, ALU.mult))
                ev_cxn[i] = sv.n

                for tc in range(4):
                    slot = rot_next()
                    @block.scalar
                    def _(scalar, i=i, tc=tc, par=par):
                        scalar.wait_ge(sv.h, ev_cxn[i])
                        zt_prev = ev_zt.get((i, tc - 2)) if tc >= 2 else \
                            (ev_zt.get((i - 1, tc + 2)) if i >= 1 else None)
                        if zt_prev is not None:
                            scalar.wait_ge(st.h, zt_prev)
                        if ev_x2add.get((i - 1, tc)) is not None:
                            scalar.wait_ge(sv.h, ev_x2add[(i - 1, tc)])
                        ss.inc(scalar.activation(
                            z_t[:, (tc % 2) * 1024:(tc % 2) * 1024 + 1024],
                            x_sl(par)[:, tc * 1024:(tc + 1) * 1024],
                            AF.Identity, scale=stat[:, WRK][:, tc:tc + 1],
                            bias=stat[:, A2][:, tc:tc + 1]))
                        ev_z[(i, tc)] = ss.n
                        ss.inc(scalar.activation(
                            r_t[:, tc * 1024:(tc + 1) * 1024],
                            x_sl(par)[:, tc * 1024:(tc + 1) * 1024],
                            AF.Identity, scale=stat[:, VARP][:, tc:tc + 1],
                            bias=stat[:, WRK2][:, tc:tc + 1]))
                        if tc == 3:
                            ev_r[i] = ss.n
                            ev_xdone[i] = ss.n

                    @block.tensor
                    def _(tensor, i=i, tc=tc, slot=slot):
                        if i == 0 and tc == 0:
                            tensor.wait_ge(sv.h, ev_pro_v)
                            tensor.wait_ge(ss.h, ev_pro_s)
                            tensor.wait_ge(sv.h, ev_utsplit)
                        tensor.wait_ge(ss.h, ev_z[(i, tc)])
                        if rot_guard[slot] is not None:
                            kind, val = rot_guard[slot]
                            tensor.wait_ge(kind.h, val)
                        for kc in range(NKC):
                            st.inc(tensor.transpose(
                                ps_rot[slot][:, kc * 128:(kc + 1) * 128],
                                z_t[:, (tc % 2) * 1024 + kc * 128:(tc % 2) * 1024 + (kc + 1) * 128],
                                eyet[:, :]))
                    ev_zt[(i, tc)] = st.n

                    @block.scalar
                    def _(scalar, i=i, tc=tc, slot=slot):
                        scalar.wait_ge(st.h, ev_zt[(i, tc)])
                        if i >= 1 and tc == 0:
                            scalar.wait_ge(st.h, ev_oproj[(i - 1, NH - 1)])
                        ss.inc(scalar.copy(
                            zThi[:, :].rearrange("p (k t) -> p k t", k=NKC, t=TILE)[:, :, tc * 128:(tc + 1) * 128],
                            ps_rot[slot][:, :].rearrange("p (k b) -> p k b", k=NKC, b=128)))
                    ev_zhi[(i, tc)] = ss.n

                    @block.vector
                    def _(vector, i=i, tc=tc, slot=slot):
                        vector.wait_ge(st.h, ev_zt[(i, tc)])
                        vector.wait_ge(ss.h, ev_zhi[(i, tc)])
                        if i >= 1 and tc == 0:
                            vector.wait_ge(st.h, ev_oproj[(i - 1, NH - 1)])
                        sv.inc(vector.tensor_tensor(
                            zTlo[:, :].rearrange("p (k t) -> p k t", k=NKC, t=TILE)[:, :, tc * 128:(tc + 1) * 128],
                            ps_rot[slot][:, :].rearrange("p (k b) -> p k b", k=NKC, b=128),
                            zThi[:, :].rearrange("p (k t) -> p k t", k=NKC, t=TILE)[:, :, tc * 128:(tc + 1) * 128],
                            ALU.subtract))
                    ev_zlo[(i, tc)] = sv.n
                    rot_guard[slot] = (sv, ev_zlo[(i, tc)])

                    @block.tensor
                    def _(tensor, i=i, tc=tc):
                        tensor.wait_ge(ss.h, ev_zhi[(i, tc)])
                        tensor.wait_ge(sv.h, ev_zlo[(i, tc)])
                        if tc == 0 and i > 0:
                            tensor.wait_ge(sv.h, ev_vsb[i - 1])
                        n = 0
                        for kc in range(NKC):
                            zh = zThi[:, kc * TILE + tc * 128:kc * TILE + (tc + 1) * 128]
                            zl = zTlo[:, kc * TILE + tc * 128:kc * TILE + (tc + 1) * 128]
                            uh = uThi[:, kc * NH:(kc + 1) * NH]
                            ul = uTlo[:, kc * NH:(kc + 1) * NH]
                            for (wop, zop) in ((uh, zh), (ul, zh), (uh, zl)):
                                st.inc(tensor.matmul(
                                    ps_v[:, tc * 128:(tc + 1) * 128], wop, zop,
                                    start=(n == 0), stop=(n == 3 * NKC - 1)))
                                n += 1
                    ev_vmm[(i, tc)] = st.n

                @block.vector
                def _(vector, i=i):
                    vector.wait_ge(st.h, ev_vmm[(i, 3)])
                    sv.inc(vector.tensor_copy(v_sb[:, :], ps_v[:, :]))
                ev_vsb[i] = sv.n

                for oc in range(NH):
                    op = oc % 2

                    @block.tensor
                    def _(tensor, i=i, oc=oc):
                        prev = ev_fsig.get((i, oc - 1)) if oc > 0 else \
                            ev_fsig.get((i - 1, NH - 1))
                        if prev is not None:
                            tensor.wait_ge(ss.h, prev)
                        n = 0
                        for kc in range(NKC):
                            zh = zThi[:, kc * TILE:(kc + 1) * TILE]
                            zl = zTlo[:, kc * TILE:(kc + 1) * TILE]
                            wh = wfThi[:, kc * D + oc * 128:kc * D + (oc + 1) * 128]
                            wl = wfTlo[:, kc * D + oc * 128:kc * D + (oc + 1) * 128]
                            for (wop, zop) in ((wh, zh), (wl, zh), (wh, zl)):
                                st.inc(tensor.matmul(
                                    ps_f[:, :], wop, zop,
                                    start=(n == 0), stop=(n == 3 * NKC - 1)))
                                n += 1
                    ev_fproj[(i, oc)] = st.n

                    @block.scalar
                    def _(scalar, i=i, oc=oc, op=op):
                        scalar.wait_ge(st.h, ev_fproj[(i, oc)])
                        prevy = ev_ymul.get((i, oc - 2)) if oc >= 2 else \
                            (ev_ymul.get((i - 1, NH - 2 + oc)) if i >= 1 else None)
                        if prevy is not None:
                            scalar.wait_ge(sv.h, prevy)
                        ss.inc(scalar.activation(
                            f_b[:, op * TILE:(op + 1) * TILE], ps_f[:, :], AF.Sigmoid))
                    ev_fsig[(i, oc)] = ss.n

                    @block.tensor
                    def _(tensor, i=i, oc=oc):
                        prev = ev_osig.get((i, oc - 1)) if oc > 0 else \
                            ev_osig.get((i - 1, NH - 1))
                        if prev is not None:
                            tensor.wait_ge(ss.h, prev)
                        n = 0
                        for kc in range(NKC):
                            zh = zThi[:, kc * TILE:(kc + 1) * TILE]
                            zl = zTlo[:, kc * TILE:(kc + 1) * TILE]
                            wh = woThi[:, kc * D + oc * 128:kc * D + (oc + 1) * 128]
                            wl = woTlo[:, kc * D + oc * 128:kc * D + (oc + 1) * 128]
                            for (wop, zop) in ((wh, zh), (wl, zh), (wh, zl)):
                                st.inc(tensor.matmul(
                                    ps_o[:, :], wop, zop,
                                    start=(n == 0), stop=(n == 3 * NKC - 1)))
                                n += 1
                        ev_oproj[(i, oc)] = st.n
                        tensor.wait_ge(sv.h, ev_vsb[i])
                        prevg = ev_g.get((i, oc - 1)) if oc > 0 else \
                            ev_g.get((i - 1, NH - 1))
                        if prevg is not None:
                            tensor.wait_ge(sv.h, prevg)
                        st.inc(tensor.matmul(
                            ps_vb[:, :], selbt[:, oc * 128:(oc + 1) * 128],
                            v_sb[:, :], start=True, stop=True))
                    ev_vb[(i, oc)] = st.n

                    @block.scalar
                    def _(scalar, i=i, oc=oc, op=op):
                        scalar.wait_ge(st.h, ev_oproj[(i, oc)])
                        prevy = ev_ymul.get((i, oc - 2)) if oc >= 2 else \
                            (ev_ymul.get((i - 1, NH - 2 + oc)) if i >= 1 else None)
                        if prevy is not None:
                            scalar.wait_ge(sv.h, prevy)
                        ss.inc(scalar.activation(
                            o_b[:, op * TILE:(op + 1) * TILE], ps_o[:, :], AF.Sigmoid))
                    ev_osig[(i, oc)] = ss.n

                    @block.vector
                    def _(vector, i=i, oc=oc, op=op):
                        fsl = f_b[:, op * TILE:(op + 1) * TILE]
                        isl = i_b[:, op * TILE:(op + 1) * TILE]
                        gsl = g_b[:, op * TILE:(op + 1) * TILE]
                        osl = o_b[:, op * TILE:(op + 1) * TILE]
                        if oc == 0 and i >= 1 and (i - 1, 3) in ev_yt:
                            vector.wait_ge(st.h, ev_yt[(i - 1, 3)])
                        vector.wait_ge(ss.h, ev_fsig[(i, oc)])
                        vector.tensor_scalar(fsl, fsl, FORGET_LB, None, ALU.max)
                        vector.tensor_scalar(isl, fsl, -1.0, 1.0, ALU.mult, ALU.add)
                        vector.wait_ge(st.h, ev_vb[(i, oc)])
                        ev_g[(i, oc)] = sv.inc(vector.tensor_tensor(
                            gsl, isl, ps_vb[:, :], ALU.mult))
                        sv.inc(vector.tensor_tensor_scan(
                            fsl, fsl, gsl, carry[:, oc:oc + 1], ALU.mult, ALU.add))
                        vector.wait_ge(sv.h, sv.n)
                        vector.tensor_copy(carry[:, oc:oc + 1],
                                           fsl[:, TILE - 1:TILE])
                        vector.wait_ge(ss.h, ev_osig[(i, oc)])
                        sv.inc(vector.tensor_tensor(
                            y_tile[:, oc * TILE:(oc + 1) * TILE], osl, fsl,
                            ALU.mult))
                    ev_ymul[(i, oc)] = sv.n
                ev_ytile[i] = ev_ymul[(i, NH - 1)]

            if i == nt_run and la >= 0:
                @block.vector
                def _(vector, la=la):
                    ch = Chain(vector, sv)
                    vector.wait_ge(ss.h, ev_sq2[la])
                    ch(vector.tensor_scalar(stat[:, MEAN2], statS[:, 4:8],
                                            1.0 / D, None, ALU.mult))
                    ch(vector.tensor_tensor(stat[:, SQ],
                                            m2t[:, la * 4:(la + 1) * 4],
                                            m2t[:, la * 4:(la + 1) * 4], ALU.mult))
                    ch(vector.tensor_tensor(stat[:, MEAN2], stat[:, MEAN2],
                                            stat[:, SQ], ALU.subtract))
                    ch(vector.tensor_scalar(stat[:, MEAN2], stat[:, MEAN2],
                                            LN_EPS, None, ALU.add))
                    ch(vector.reciprocal(stat[:, SQ], stat[:, MEAN2]))
                ev_rc2 = sv.n

                @block.scalar
                def _(scalar):
                    scalar.wait_ge(sv.h, ev_rc2)
                    ss.inc(scalar.activation(stat[:, SQ], stat[:, SQ], AF.Sqrt))
                ev_sq2s = ss.n

                @block.vector
                def _(vector, la=la):
                    ch = Chain(vector, sv)
                    vector.wait_ge(ss.h, ev_sq2s)
                    ch(vector.tensor_tensor(stat[:, WRK2], stat[:, SQ],
                                            stat[:, SQ], ALU.mult))
                    ch(vector.tensor_tensor(stat[:, WRK2], stat[:, WRK2],
                                            stat[:, MEAN2], ALU.mult))
                    ch(vector.tensor_scalar(stat[:, WRK2], stat[:, WRK2],
                                            -0.5, 1.5, ALU.mult, ALU.add))
                    ch(vector.tensor_tensor(stat[:, SQ], stat[:, SQ],
                                            stat[:, WRK2], ALU.mult))
                    ch(vector.tensor_copy(rs2t[:, la * 4:(la + 1) * 4], stat[:, SQ]))
                    vector.wait_ge(sv.h, ev_amax4[la])
                    ch(vector.tensor_tensor(stat[:, 32:36], stat[:, 32:36],
                                            stat[:, SQ], ALU.mult))
                    ch(vector.tensor_reduce(stat[:, 36:37], stat[:, 32:36],
                                            AX, ALU.max))
                    ch(vector.tensor_tensor(lnmax[:, :], lnmax[:, :],
                                            stat[:, 36:37], ALU.max))
                ev_rs2[la] = sv.n

            if la >= 0:
                @block.sync
                def _(sync, la=la):
                    sync.wait_ge(sv.h, ev_rs2[la])
                    sync.wait_ge(ss.h, ev_sq2[la])
                    qs.dma(sync.dma_start(out=x2_t4[la], in_=x2_t[:, :]))
                    if dbg:
                        qs.dma(sync.dma_start(out=y2_t4[la], in_=x2_t[:, :]))
                ev_x2st[la] = qs.n

        if stage <= 1:
            if dbg == 2:
                dmp = {}
                for nm, tile_ in (("d_r", r_t), ("d_y", y_tile), ("d_x2", x2_t),
                                  ("d_f", f_b), ("d_o", o_b), ("d_g", g_b),
                                  ("d_stat", stat), ("d_m2", m2t), ("d_rs2", rs2t),
                                  ("d_carry", carry), ("d_zhi", zThi),
                                  ("d_uhi", uThi), ("d_ulo", uTlo)):
                    shp = [tile_.shape[0], tile_.shape[1]]
                    dt_ = tile_.dtype
                    dmp[nm] = (nc.dram_tensor(nm, shp, dt_, kind="ExternalOutput").ap(), tile_)
                vdmp = nc.dram_tensor("d_v", [NH, TILE], F32, kind="ExternalOutput").ap()

                @block.sync
                def _(sync):
                    sync.wait_ge(sv.h, sv.n)
                    sync.wait_ge(ss.h, ss.n)
                    sync.wait_ge(st.h, st.n)
                    for nm, (ap_, tile_) in dmp.items():
                        qs.dma(sync.dma_start(out=ap_, in_=tile_[:, :]))
                    qs.dma(sync.dma_start(out=vdmp, in_=v_sb[:, :]))
            _finish()
            return nc

        @block.tensor
        def _(tensor):
            tensor.wait_ge(sv.h, ev_rs2[nt_run - 1])
            st.inc(tensor.transpose(ps_v[0:1, 0:128], lnmax[:, 0:1], eyet[:, :]))
        ev_lt = st.n

        @block.vector
        def _(vector):
            vector.wait_ge(st.h, ev_lt)
            sv.inc(vector.tensor_copy(ccrow[0:1, :], ps_v[0:1, 0:128]))
        ev_lrow = sv.n

        @block.scalar
        def _(scalar):
            scalar.wait_ge(qs.h, ev_const)
            scalar.activation(sgc[0:1, 2:3], sgc[0:1, 0:1], AF.Exp, scale=-1.0)
            ss.inc(scalar.activation(sgc[0:1, 3:4], sgc[0:1, 1:2], AF.Exp, scale=-1.0))
        ev_exp = ss.n

        @block.gpsimd
        def _(g):
            g.wait_ge(sv.h, ev_lrow)
            qg.dma(g.dma_start(out=ccxi_d.ap(), in_=ccrow[0:1, :]))
            g.wait_ge(qg.h, qg.n)
            scc.inc(g.collective_compute(
                "AllReduce", ALU.max, replica_groups=[list(range(8))],
                ins=[ccxi_d.ap()], outs=[ccxo_d.ap()]))
            g.wait_ge(scc.h, scc.n)
            qg.dma(g.dma_start(out=ccrow[0:1, :], in_=ccxo_d.ap()))
        ev_ccx = qg.n

        @block.vector
        def _(vector):
            ch = Chain(vector, sv)
            vector.wait_ge(ss.h, ev_exp)
            ch(vector.tensor_scalar(sgc[0:1, 2:4], sgc[0:1, 2:4], 1.0, None, ALU.add))
            ch(vector.reciprocal(sgc[0:1, 2:4], sgc[0:1, 2:4]))
            vector.wait_ge(qg.h, ev_ccx)
            ch(vector.tensor_reduce(sgc[0:1, 4:5], ccrow[0:1, :], AX, ALU.max))
            ch(vector.tensor_tensor(sgc[0:1, 4:5], sgc[0:1, 4:5],
                                    sgc[0:1, 2:3], ALU.mult))
        ev_gm = sv.n

        @block.tensor
        def _(tensor):
            tensor.wait_ge(sv.h, ev_gm)
            st.inc(tensor.matmul(ps_vb[:, 0:1], ones1t[:, :], sgc[0:1, 4:5],
                                 start=True, stop=True))
        ev_gb = st.n

        @block.vector
        def _(vector):
            ch = Chain(vector, sv)
            vector.wait_ge(st.h, ev_gb)
            ch(vector.tensor_scalar(sxv[:, 1:2], ps_vb[:, 0:1], 1.0 / QMAX,
                                    None, ALU.mult))
            ch(vector.reciprocal(sxv[:, 0:1], sxv[:, 1:2]))
            ch(vector.tensor_scalar(qsc[:, :], rs2t[:, :], sxv[:, 0:1], None,
                                    ALU.mult))
            ch(vector.tensor_tensor(qbi[:, :], m2t[:, :], qsc[:, :], ALU.mult))
            ch(vector.tensor_scalar(qbi[:, :], qbi[:, :], -1.0, None, ALU.mult))
        ev_qsc = sv.n

        p1.close(); open_stacks.remove(p1)
        p1w.close(); open_stacks.remove(p1w)

        if stage <= 2:
            _finish()
            return nc

        p2c = ExitStack()
        open_stacks.append(p2c)
        wdT = p2c.enter_context(nc.sbuf_tensor("wdT", [128, NIC * 1024], BF16))

        p2 = ExitStack()
        open_stacks.append(p2)
        p2e = p2.enter_context
        xqT = p2e(nc.sbuf_tensor("xqT", [128, NKC * T], BF16))
        wdst = p2e(nc.sbuf_tensor("wdst", [128, NI], F32))
        wdqn = p2e(nc.sbuf_tensor("wdqn", [128, NI], BF16))
        wgst = p2e(nc.sbuf_tensor("wgst", [128, 2 * D], F32))
        wust = p2e(nc.sbuf_tensor("wust", [128, 2 * D], F32))
        wgqn = p2e(nc.sbuf_tensor("wgqn", [128, 2 * D], BF16))
        wuqn = p2e(nc.sbuf_tensor("wuqn", [128, 2 * D], BF16))
        wgqT = p2e(nc.sbuf_tensor("wgqT", [128, 2 * D], BF16))
        wuqT = p2e(nc.sbuf_tensor("wuqT", [128, 2 * D], BF16))
        q_s = p2e(nc.sbuf_tensor("q_s", [128, 2 * D], F32))
        xqn = p2e(nc.sbuf_tensor("xqn", [128, 2 * D], BF16))
        gsil = p2e(nc.sbuf_tensor("gsil", [128, 2 * TG], F32))
        h_t = p2e(nc.sbuf_tensor("h_t", [128, 2 * TG], F32))
        mgt = p2e(nc.sbuf_tensor("mgt", [128, NIC], F32))
        mut = p2e(nc.sbuf_tensor("mut", [128, NIC], F32))
        ps_gate = p2e(nc.psum_tensor("ps_gate", [128, TG], F32))
        ps_up = p2e(nc.psum_tensor("ps_up", [128, TG], F32))
        ps_wt = p2e(nc.psum_tensor("ps_wt", [128, 1024], BF16))

        PH1_SS, PH1_SV, PH1_ST, PH1_SG = ss.n, sv.n, st.n, sg.n

        ev_x2ld = {}; ev_aff = {}; ev_xq = {}; ev_xqtr = {}; ev_xqevac = {}
        for tc in range(NTC):
            parq = tc % 2
            @block.sync
            def _(sync, tc=tc, parq=parq):
                if tc == 0:
                    sync.wait_ge(sv.h, ev_qsc)
                    sync.wait_ge(ss.h, PH1_SS)
                    sync.wait_ge(sg.h, PH1_SG)
                    sync.wait_ge(st.h, PH1_ST)
                if tc >= 2:
                    sync.wait_ge(sv.h, ev_xq[tc - 2])
                qs.dma(sync.dma_start(out=q_s[:, parq * D:(parq + 1) * D],
                                      in_=x2_tc[tc]))
            ev_x2ld[tc] = qs.n

            @block.scalar
            def _(scalar, tc=tc, parq=parq):
                scalar.wait_ge(qs.h, ev_x2ld[tc])
                ss.inc(scalar.activation(
                    q_s[:, parq * D:(parq + 1) * D],
                    q_s[:, parq * D:(parq + 1) * D],
                    AF.Identity, scale=qsc[:, tc:tc + 1], bias=qbi[:, tc:tc + 1]))
            ev_aff[tc] = ss.n

            @block.vector
            def _(vector, tc=tc, parq=parq):
                vector.wait_ge(ss.h, ev_aff[tc])
                if tc >= 2:
                    vector.wait_ge(st.h, ev_xqtr[tc - 2])
                vector.tensor_scalar(q_s[:, parq * D:(parq + 1) * D],
                                     q_s[:, parq * D:(parq + 1) * D],
                                     TWO23, -TWO23, ALU.add, ALU.add)
                sv.inc(vector.tensor_scalar(
                    xqn[:, parq * D:(parq + 1) * D],
                    q_s[:, parq * D:(parq + 1) * D],
                    QMAX, -QMAX - 1.0, ALU.min, ALU.max))
            ev_xq[tc] = sv.n

            @block.tensor
            def _(tensor, tc=tc, parq=parq):
                tensor.wait_ge(sv.h, ev_xq[tc])
                if tc >= 1:
                    tensor.wait_ge(ss.h, ev_xqevac[tc - 1])
                for kc in range(NKC):
                    st.inc(tensor.transpose(
                        ps_wt[:, kc * 128:(kc + 1) * 128],
                        xqn[:, parq * D + kc * 128:parq * D + (kc + 1) * 128],
                        eyebt[:, :]))
            ev_xqtr[tc] = st.n

            @block.scalar
            def _(scalar, tc=tc):
                scalar.wait_ge(st.h, ev_xqtr[tc])
                ss.inc(scalar.copy(
                    xqT[:, :].rearrange("p (k t) -> p k t", k=NKC, t=T)[:, :, tc * 128:(tc + 1) * 128],
                    ps_wt[:, :].rearrange("p (k b) -> p k b", k=NKC, b=128)))
            ev_xqevac[tc] = ss.n
        ev_xqT_done = ev_xqevac[NTC - 1]

        if stage <= 3:
            _finish()
            return nc

        ev_wdld = {}; ev_wdam = {}; ev_wdr = {}; ev_wdq = {}
        ev_wdtr = {}; ev_wdevac = {}
        for hc in range(NKC):
            @block.sync
            def _(sync, hc=hc):
                if hc >= 1:
                    sync.wait_ge(sg.h, ev_wdq[hc - 1])
                qs.dma(sync.dma_start(out=wdst[:, :],
                                      in_=wd_d[hc * 128:(hc + 1) * 128, :]))
            ev_wdld[hc] = qs.n

            @block.vector
            def _(vector, hc=hc):
                ch = Chain(vector, sv)
                vector.wait_ge(qs.h, ev_wdld[hc])
                if hc >= 1:
                    vector.wait_ge(sg.h, ev_wdq[hc - 1])
                ch(vector.tensor_reduce(stat[:, 0:1], wdst[:, :], AX, ALU.max,
                                        apply_absolute_value=True))
                ch(vector.tensor_scalar(swdt[:, hc:hc + 1], stat[:, 0:1],
                                        1.0 / QMAX, None, ALU.mult))
                ch(vector.reciprocal(stat[:, 1:2], swdt[:, hc:hc + 1]))
            ev_wdr[hc] = sv.n

            @block.gpsimd
            def _(g, hc=hc):
                g.wait_ge(sv.h, ev_wdr[hc])
                if hc >= 1:
                    g.wait_ge(st.h, ev_wdtr[hc - 1])
                g.tensor_scalar(wdst[:, :], wdst[:, :], stat[:, 1:2], TWO23,
                                ALU.mult, ALU.add)
                g.tensor_scalar(wdst[:, :], wdst[:, :], -TWO23, QMAX,
                                ALU.add, ALU.min)
                sg.inc(g.tensor_scalar(wdqn[:, :], wdst[:, :], -QMAX - 1.0,
                                       None, ALU.max))
            ev_wdq[hc] = sg.n

            for grp in range(3):
                lo = grp * 8
                hi_ = min(lo + 8, NIC)
                @block.tensor
                def _(tensor, hc=hc, lo=lo, hi_=hi_, grp=grp):
                    tensor.wait_ge(sg.h, ev_wdq[hc])
                    if hc == 0 and grp == 0:
                        tensor.wait_ge(ss.h, ev_xqT_done)
                    if grp >= 1:
                        tensor.wait_ge(ss.h, ev_wdevac[(hc, grp - 1)])
                    elif hc >= 1:
                        tensor.wait_ge(ss.h, ev_wdevac[(hc - 1, 2)])
                    for icc in range(lo, hi_):
                        st.inc(tensor.transpose(
                            ps_wt[:, (icc - lo) * 128:(icc - lo + 1) * 128],
                            wdqn[:, icc * 128:(icc + 1) * 128],
                            eyebt[:, :]))
                ev_wdtr_g = st.n

                @block.scalar
                def _(scalar, hc=hc, lo=lo, hi_=hi_, ev=ev_wdtr_g):
                    scalar.wait_ge(st.h, ev)
                    for icc in range(lo, hi_):
                        last = scalar.copy(
                            wdT[:, icc * 1024 + hc * 128:icc * 1024 + (hc + 1) * 128],
                            ps_wt[:, (icc - lo) * 128:(icc - lo + 1) * 128])
                    ss.inc(last)
                ev_wdevac[(hc, grp)] = ss.n
            ev_wdtr[hc] = st.n
        ev_wd_done = ev_wdevac[(NKC - 1, 2)]

        if stage <= 4:
            _finish()
            return nc

        ev_wgld = {}; ev_am2 = {}; ev_r2 = {}; ev_wgq = {}
        ev_wgtr = {}; ev_wgte1 = {}; ev_wutr = {}; ev_wgte = {}
        ev_gmm = {}; ev_umm = {}; ev_sil = {}; ev_hmul = {}
        ev_hmax = {}; ev_hst = {}
        for ic in range(NIC):
            parw = ic % 2
            @block.sync
            def _(sync, ic=ic, parw=parw):
                if ic >= 2:
                    sync.wait_ge(sg.h, ev_wgq[ic - 2])
                qs.dma(sync.dma_start(out=wgst[:, parw * D:(parw + 1) * D],
                                      in_=wg_d[ic * 128:(ic + 1) * 128, :]))
                qs.dma(sync.dma_start(out=wust[:, parw * D:(parw + 1) * D],
                                      in_=wu_d[ic * 128:(ic + 1) * 128, :]))
            ev_wgld[ic] = qs.n

            @block.vector
            def _(vector, ic=ic, parw=parw):
                ch = Chain(vector, sv)
                vector.wait_ge(qs.h, ev_wgld[ic])
                if ic >= 1:
                    vector.wait_ge(sg.h, ev_wgq[ic - 1])
                ch(vector.tensor_reduce(stat[:, 2:3],
                                        wgst[:, parw * D:(parw + 1) * D],
                                        AX, ALU.max, apply_absolute_value=True))
                ch(vector.tensor_reduce(stat[:, 3:4],
                                        wust[:, parw * D:(parw + 1) * D],
                                        AX, ALU.max, apply_absolute_value=True))
                ch(vector.tensor_scalar(swgt[:, ic:ic + 1], stat[:, 2:3],
                                        1.0 / QMAX, None, ALU.mult))
                ch(vector.tensor_scalar(swut[:, ic:ic + 1], stat[:, 3:4],
                                        1.0 / QMAX, None, ALU.mult))
                ch(vector.reciprocal(stat[:, 4:5], swgt[:, ic:ic + 1]))
                ch(vector.reciprocal(stat[:, 5:6], swut[:, ic:ic + 1]))
                ch(vector.tensor_scalar(mgt[:, ic:ic + 1], swgt[:, ic:ic + 1],
                                        sxv[:, 1:2], None, ALU.mult))
                ch(vector.tensor_scalar(mut[:, ic:ic + 1],
                                        swut[:, ic:ic + 1],
                                        sxv[:, 1:2], None, ALU.mult))
            ev_r2[ic] = sv.n

            @block.gpsimd
            def _(g, ic=ic, parw=parw):
                g.wait_ge(sv.h, ev_r2[ic])
                if ic >= 2:
                    g.wait_ge(st.h, ev_wutr[ic - 2])
                for (wst, wqn, rcol) in ((wgst, wgqn, 4), (wust, wuqn, 5)):
                    g.tensor_scalar(wst[:, parw * D:(parw + 1) * D],
                                    wst[:, parw * D:(parw + 1) * D],
                                    stat[:, rcol:rcol + 1], TWO23,
                                    ALU.mult, ALU.add)
                    g.tensor_scalar(wst[:, parw * D:(parw + 1) * D],
                                    wst[:, parw * D:(parw + 1) * D],
                                    -TWO23, QMAX, ALU.add, ALU.min)
                    sg.inc(g.tensor_scalar(wqn[:, parw * D:(parw + 1) * D],
                                           wst[:, parw * D:(parw + 1) * D],
                                           -QMAX - 1.0, None, ALU.max))
            ev_wgq[ic] = sg.n

            @block.tensor
            def _(tensor, ic=ic, parw=parw):
                tensor.wait_ge(sg.h, ev_wgq[ic])
                if ic == 0:
                    tensor.wait_ge(ss.h, ev_wd_done)
                if ic >= 2:
                    tensor.wait_ge(ss.h, ev_wgte[ic - 2])
                for kc in range(NKC):
                    st.inc(tensor.transpose(
                        ps_wt[:, kc * 128:(kc + 1) * 128],
                        wgqn[:, parw * D + kc * 128:parw * D + (kc + 1) * 128],
                        eyebt[:, :]))
            ev_wgtr[ic] = st.n

            @block.scalar
            def _(scalar, ic=ic, parw=parw):
                scalar.wait_ge(st.h, ev_wgtr[ic])
                ss.inc(scalar.copy(wgqT[:, parw * D:(parw + 1) * D],
                                          ps_wt[:, :]))
            ev_wgte1[ic] = ss.n

            @block.tensor
            def _(tensor, ic=ic, parw=parw):
                tensor.wait_ge(ss.h, ev_wgte1[ic])
                for kc in range(NKC):
                    st.inc(tensor.transpose(
                        ps_wt[:, kc * 128:(kc + 1) * 128],
                        wuqn[:, parw * D + kc * 128:parw * D + (kc + 1) * 128],
                        eyebt[:, :]))
            ev_wutr[ic] = st.n

            @block.scalar
            def _(scalar, ic=ic, parw=parw):
                scalar.wait_ge(st.h, ev_wutr[ic])
                ss.inc(scalar.copy(wuqT[:, parw * D:(parw + 1) * D],
                                          ps_wt[:, :]))
            ev_wgte[ic] = ss.n

            for ts_ in range(NTG):
                parh = ts_ % 2

                @block.tensor
                def _(tensor, ic=ic, ts_=ts_, parw=parw):
                    tensor.wait_ge(ss.h, ev_wgte[ic])
                    pk = (ic, ts_ - 1) if ts_ > 0 else (ic - 1, NTG - 1)
                    if pk in ev_sil:
                        tensor.wait_ge(ss.h, ev_sil[pk])
                    for kc in range(NKC):
                        for half in range(2):
                            st.inc(tensor.matmul(
                                ps_gate[:, half * 512:(half + 1) * 512],
                                wgqT[:, parw * D + kc * 128:parw * D + (kc + 1) * 128],
                                xqT[:, kc * T + ts_ * TG + half * 512:kc * T + ts_ * TG + (half + 1) * 512],
                                start=(kc == 0), stop=(kc == NKC - 1)))
                    ev_gmm[(ic, ts_)] = st.n
                    if pk in ev_hmul:
                        tensor.wait_ge(sv.h, ev_hmul[pk])
                    for kc in range(NKC):
                        for half in range(2):
                            st.inc(tensor.matmul(
                                ps_up[:, half * 512:(half + 1) * 512],
                                wuqT[:, parw * D + kc * 128:parw * D + (kc + 1) * 128],
                                xqT[:, kc * T + ts_ * TG + half * 512:kc * T + ts_ * TG + (half + 1) * 512],
                                start=(kc == 0), stop=(kc == NKC - 1)))
                ev_umm[(ic, ts_)] = st.n

                @block.scalar
                def _(scalar, ic=ic, ts_=ts_, parh=parh):
                    scalar.wait_ge(st.h, ev_gmm[(ic, ts_)])
                    pk = (ic, ts_ - 2) if ts_ >= 2 else \
                        ((ic - 1, NTG - 2 + ts_) if ic > 0 else None)
                    if pk is not None and pk in ev_hmul:
                        scalar.wait_ge(sv.h, ev_hmul[pk])
                    ss.inc(scalar.activation(
                        gsil[:, parh * TG:(parh + 1) * TG], ps_gate[:, :],
                        AF.Silu, scale=mgt[:, ic:ic + 1]))
                ev_sil[(ic, ts_)] = ss.n

                @block.vector
                def _(vector, ic=ic, ts_=ts_, parh=parh):
                    vector.wait_ge(st.h, ev_umm[(ic, ts_)])
                    vector.wait_ge(ss.h, ev_sil[(ic, ts_)])
                    pk = (ic, ts_ - 2) if ts_ >= 2 else \
                        ((ic - 1, NTG - 2 + ts_) if ic > 0 else None)
                    if pk is not None and pk in ev_hst:
                        vector.wait_ge(qs.h, ev_hst[pk])
                    sv.inc(vector.scalar_tensor_tensor(
                        h_t[:, parh * TG:(parh + 1) * TG], ps_up[:, :],
                        mut[:, ic:ic + 1], gsil[:, parh * TG:(parh + 1) * TG],
                        ALU.mult, ALU.mult))
                ev_hmul[(ic, ts_)] = sv.n

                @block.vector
                def _(vector, ic=ic, ts_=ts_, parh=parh):
                    ch = Chain(vector, sv)
                    ch(vector.tensor_reduce(stat[:, 6:7],
                                            h_t[:, parh * TG:(parh + 1) * TG],
                                            AX, ALU.max,
                                            apply_absolute_value=True))
                    ch(vector.tensor_tensor(hmax[:, :], hmax[:, :],
                                            stat[:, 6:7], ALU.max))
                ev_hmax[(ic, ts_)] = sv.n

                @block.sync
                def _(sync, ic=ic, ts_=ts_, parh=parh):
                    sync.wait_ge(sv.h, ev_hmul[(ic, ts_)])
                    qs.dma(sync.dma_start(
                        out=h_ic[ic][:, ts_ * TG:(ts_ + 1) * TG],
                        in_=h_t[:, parh * TG:(parh + 1) * TG]))
                ev_hst[(ic, ts_)] = qs.n

        if stage <= 5:
            _finish()
            return nc

        @block.tensor
        def _(tensor):
            tensor.wait_ge(sv.h, ev_hmax[(NIC - 1, NTG - 1)])
            st.inc(tensor.transpose(ps_up[0:1, 0:128], hmax[:, 0:1], eyet[:, :]))
        ev_ht = st.n

        @block.vector
        def _(vector):
            vector.wait_ge(st.h, ev_ht)
            sv.inc(vector.tensor_copy(ccrow[0:1, :], ps_up[0:1, 0:128]))
        ev_hrow = sv.n

        @block.gpsimd
        def _(g):
            g.wait_ge(sv.h, ev_hrow)
            qg.dma(g.dma_start(out=cchi_d.ap(), in_=ccrow[0:1, :]))
            g.wait_ge(qg.h, qg.n)
            scc.inc(g.collective_compute(
                "AllReduce", ALU.max, replica_groups=[list(range(8))],
                ins=[cchi_d.ap()], outs=[ccho_d.ap()]))
            g.wait_ge(scc.h, scc.n)
            qg.dma(g.dma_start(out=ccrow[0:1, :], in_=ccho_d.ap()))
        ev_cch = qg.n

        @block.vector
        def _(vector):
            ch = Chain(vector, sv)
            vector.wait_ge(qg.h, ev_cch)
            ch(vector.tensor_reduce(sgc[0:1, 5:6], ccrow[0:1, :], AX, ALU.max))
            ch(vector.tensor_tensor(sgc[0:1, 5:6], sgc[0:1, 5:6],
                                    sgc[0:1, 3:4], ALU.mult))
        ev_gh = sv.n

        @block.tensor
        def _(tensor):
            tensor.wait_ge(sv.h, ev_gh)
            st.inc(tensor.matmul(ps_up[:, 0:1], ones1t[:, :], sgc[0:1, 5:6],
                                 start=True, stop=True))
        ev_ghb = st.n

        @block.vector
        def _(vector):
            ch = Chain(vector, sv)
            vector.wait_ge(st.h, ev_ghb)
            ch(vector.tensor_scalar(shv[:, 1:2], ps_up[:, 0:1], 1.0 / QMAX,
                                    None, ALU.mult))
            ch(vector.reciprocal(shv[:, 0:1], shv[:, 1:2]))
            ch(vector.tensor_scalar(swdt[:, :], swdt[:, :], shv[:, 1:2],
                                    None, ALU.mult))
        ev_md = sv.n

        if dbg == 2:
            dmp2 = {}
            for nm, tile_ in (("e_sxv", sxv), ("e_shv", shv), ("e_qsc", qsc),
                              ("e_qbi", qbi), ("e_swg", swgt), ("e_swu", swut),
                              ("e_swd", swdt), ("e_mgt", mgt), ("e_mut", mut),
                              ("e_xqT", xqT), ("e_lnm", lnmax),
                              ("e_ccr", ccrow), ("e_sgc", sgc),
                              ("e_hmx", hmax), ("e_m2", m2t), ("e_rs2", rs2t)):
                shp = [tile_.shape[0], tile_.shape[1]]
                dmp2[nm] = (nc.dram_tensor(nm, shp, tile_.dtype,
                                           kind="ExternalOutput").ap(), tile_)
            hdbg = nc.dram_tensor("e_h", [NI, T], F32, kind="ExternalOutput").ap()

            @block.sync
            def _(sync):
                sync.wait_ge(sv.h, sv.n)
                for nm, (ap_, tile_) in dmp2.items():
                    qs.dma(sync.dma_start(out=ap_, in_=tile_[:, :]))
                qs.dma(sync.dma_start(out=hdbg, in_=h_d))

        p2.close(); open_stacks.remove(p2)

        if stage <= 6:
            _finish()
            return nc

        p3 = ExitStack()
        open_stacks.append(p3)
        p3e = p3.enter_context
        hq = p3e(nc.sbuf_tensor("hq", [128, 2 * NIC * TG], BF16))
        hst_b = p3e(nc.sbuf_tensor("hst_b", [128, 2 * TG], F32))
        dsc = p3e(nc.sbuf_tensor("dsc", [128, NKC * TG], F32))
        x2b = p3e(nc.sbuf_tensor("x2b", [128, 2 * D], F32))
        outb = p3e(nc.sbuf_tensor("outb", [128, 2 * D], F32))
        ps_d = [p3e(nc.psum_tensor("ps_dA", [128, TG], F32)),
                p3e(nc.psum_tensor("ps_dB", [128, TG], F32))]
        ps_t2 = [p3e(nc.psum_tensor("ps_t2A", [128, 1024], F32)),
                 p3e(nc.psum_tensor("ps_t2B", [128, 1024], F32))]

        PH2_SS, PH2_SV, PH2_ST, PH2_SG = ss.n, sv.n, st.n, sg.n

        ev_hld = {}; ev_hq = {}; ev_hqs = {}; ev_dmm = {}; ev_dsc = {}
        ev_x2bld = {}; ev_dtr = {}; ev_oadd = {}; ev_ost = {}
        for tg in range(NTG):
            ptg = tg % 2
            for icc in range(NIC):
                pl = (tg * NIC + icc) % 2
                @block.sync
                def _(sync, tg=tg, icc=icc, pl=pl):
                    if tg == 0 and icc == 0:
                        sync.wait_ge(sv.h, ev_md)
                        sync.wait_ge(ss.h, PH2_SS)
                        sync.wait_ge(sg.h, PH2_SG)
                        sync.wait_ge(st.h, PH2_ST)
                    k = tg * NIC + icc
                    if k >= 2:
                        pk = divmod(k - 2, NIC)
                        sync.wait_ge(sv.h, ev_hqs[(pk[0], pk[1])])
                    qs.dma(sync.dma_start(
                        out=hst_b[:, pl * TG:(pl + 1) * TG],
                        in_=h_ic[icc][:, tg * TG:(tg + 1) * TG]))
                ev_hld[(tg, icc)] = qs.n

                @block.scalar
                def _(scalar, tg=tg, icc=icc, pl=pl, ptg=ptg):
                    scalar.wait_ge(qs.h, ev_hld[(tg, icc)])
                    if tg >= 2 and icc == 0:
                        scalar.wait_ge(st.h, ev_dmm[(tg - 2, NKC - 1)])
                    ss.inc(scalar.activation(
                        hst_b[:, pl * TG:(pl + 1) * TG],
                        hst_b[:, pl * TG:(pl + 1) * TG],
                        AF.Identity, scale=shv[:, 0:1], bias=c23t[:, 0:1]))
                ev_hq1 = ss.n

                @block.vector
                def _(vector, tg=tg, icc=icc, pl=pl, ptg=ptg, ev=ev_hq1):
                    vector.wait_ge(ss.h, ev)
                    vector.tensor_scalar(
                        hst_b[:, pl * TG:(pl + 1) * TG],
                        hst_b[:, pl * TG:(pl + 1) * TG],
                        -TWO23, QMAX, ALU.add, ALU.min)
                    sv.inc(vector.tensor_scalar(
                        hq[:, (ptg * NIC + icc) * TG:(ptg * NIC + icc + 1) * TG],
                        hst_b[:, pl * TG:(pl + 1) * TG],
                        -QMAX - 1.0, None, ALU.max))
                ev_hq[(tg, icc)] = sv.n

                ev_hqs[(tg, icc)] = ev_hq[(tg, icc)]

            for hc in range(NKC):
                pd = hc % 2
                @block.tensor
                def _(tensor, tg=tg, hc=hc, pd=pd, ptg=ptg):
                    if tg == 0 and hc == 0:
                        tensor.wait_ge(sv.h, ev_md)
                    tensor.wait_ge(sv.h, ev_hq[(tg, NIC - 1)])
                    pk = (tg, hc - 2) if hc >= 2 else \
                        ((tg - 1, NKC - 2 + hc) if tg > 0 else None)
                    if pk is not None and pk in ev_dsc:
                        tensor.wait_ge(sv.h, ev_dsc[pk])
                    for icc in range(NIC):
                        for half in range(2):
                            st.inc(tensor.matmul(
                                ps_d[pd][:, half * 512:(half + 1) * 512],
                                wdT[:, icc * 1024 + hc * 128:icc * 1024 + (hc + 1) * 128],
                                hq[:, (ptg * NIC + icc) * TG + half * 512:(ptg * NIC + icc) * TG + (half + 1) * 512],
                                start=(icc == 0), stop=(icc == NIC - 1)))
                ev_dmm[(tg, hc)] = st.n

                @block.vector
                def _(vector, tg=tg, hc=hc, pd=pd):
                    vector.wait_ge(st.h, ev_dmm[(tg, hc)])
                    if tg >= 1:
                        vector.wait_ge(st.h, ev_dtr[(tg - 1, 7)])
                    sv.inc(vector.tensor_scalar(
                        dsc[:, hc * TG:(hc + 1) * TG], ps_d[pd][:, :],
                        swdt[:, hc:hc + 1], None, ALU.mult))
                ev_dsc[(tg, hc)] = sv.n

            for tcg in range(NKC):
                pt2 = tcg % 2
                tc_abs = tg * NKC + tcg
                @block.sync
                def _(sync, tg=tg, tcg=tcg, tc_abs=tc_abs):
                    k = tc_abs
                    if k >= 2:
                        sync.wait_ge(sv.h, ev_oadd[k - 2])
                    qs.dma(sync.dma_start(out=x2b[:, (k % 2) * D:(k % 2) * D + D],
                                          in_=x2_tc[tc_abs]))
                ev_x2bld[tc_abs] = qs.n

                @block.tensor
                def _(tensor, tg=tg, tcg=tcg, pt2=pt2):
                    tensor.wait_ge(sv.h, ev_dsc[(tg, NKC - 1)])
                    k = tg * NKC + tcg
                    if k >= 2:
                        tensor.wait_ge(sv.h, ev_oadd[k - 2])
                    for hc in range(NKC):
                        st.inc(tensor.transpose(
                            ps_t2[pt2][:, hc * 128:(hc + 1) * 128],
                            dsc[:, hc * TG + tcg * 128:hc * TG + (tcg + 1) * 128],
                            eyet[:, :]))
                ev_dtr[(tg, tcg)] = st.n

                @block.vector
                def _(vector, tg=tg, tcg=tcg, pt2=pt2, tc_abs=tc_abs):
                    vector.wait_ge(st.h, ev_dtr[(tg, tcg)])
                    vector.wait_ge(qs.h, ev_x2bld[tc_abs])
                    if tc_abs >= 2:
                        vector.wait_ge(qs.h, ev_ost[tc_abs - 2])
                    sv.inc(vector.tensor_tensor(
                        outb[:, (tc_abs % 2) * D:(tc_abs % 2) * D + D],
                        x2b[:, (tc_abs % 2) * D:(tc_abs % 2) * D + D],
                        ps_t2[pt2][:, :], ALU.add))
                ev_oadd[tc_abs] = sv.n

                @block.sync
                def _(sync, tc_abs=tc_abs):
                    sync.wait_ge(sv.h, ev_oadd[tc_abs])
                    qs.dma(sync.dma_start(out=y_tc[tc_abs],
                                          in_=outb[:, (tc_abs % 2) * D:(tc_abs % 2) * D + D]))
                ev_ost[tc_abs] = qs.n

        @block.sync
        def _(sync):
            sync.wait_ge(qs.h, qs.n)
        @block.gpsimd
        def _(g):
            g.wait_ge(qg.h, qg.n)

        p3.close(); open_stacks.remove(p3)
        p2c.close(); open_stacks.remove(p2c)

    return nc


def kernel(L0_attn_x, L0_attn_norm_w, L0_attn_w_forget, L0_attn_w_input,
           L0_attn_w_output, L0_ffn_clip_x, L0_ffn_clip_down, L0_ffn_w_gate,
           L0_ffn_w_up, L0_ffn_w_down):
    nc = build_program(dbg=False)
    x = np.ascontiguousarray(np.asarray(L0_attn_x, dtype=np.float32))
    shared = {
        "wf": np.ascontiguousarray(np.asarray(L0_attn_w_forget, np.float32)),
        "wi": np.ascontiguousarray(np.asarray(L0_attn_w_input, np.float32)),
        "wo": np.ascontiguousarray(np.asarray(L0_attn_w_output, np.float32)),
        "nw": np.ascontiguousarray(np.asarray(L0_attn_norm_w, np.float32)),
        "cx": np.ascontiguousarray(np.asarray(L0_ffn_clip_x, np.float32)),
        "cd": np.ascontiguousarray(np.asarray(L0_ffn_clip_down, np.float32)),
        "wg": np.ascontiguousarray(np.asarray(L0_ffn_w_gate, np.float32)),
        "wu": np.ascontiguousarray(np.asarray(L0_ffn_w_up, np.float32)),
        "wd": np.ascontiguousarray(np.asarray(L0_ffn_w_down, np.float32)),
    }
    in_maps = [dict(shared, x=np.ascontiguousarray(x[c])) for c in range(8)]
    res = run_bass_kernel_spmd(nc, in_maps, core_ids=list(range(8)))
    return np.stack([np.asarray(res.results[c]["y"]) for c in range(8)], axis=0)


# version 04
# speedup vs baseline: 1.3914x; 1.3914x over previous
import numpy as np
import ml_dtypes
from contextlib import ExitStack
import concourse.bass as bass
import concourse.mybir as mybir
from concourse.bass_utils import run_bass_kernel_spmd

F32 = mybir.dt.float32
BF16 = mybir.dt.bfloat16
AF = mybir.ActivationFunctionType
ALU = mybir.AluOpType
AX = mybir.AxisListType.X

T = 4096
D = 1024
NH = 8
NI = 2816
NIC = NI // 128
NKC = D // 128
TILE = 512
NT = T // TILE
NTC = T // 128
TG = 1024
NTG = T // TG
TWO23 = 12582912.0
QMAX = 127.0
LN_EPS = 1e-5
RMS_EPS = 1e-6
FORGET_LB = 0.9


class Sem:
    def __init__(self, handle):
        self.h = handle
        self.n = 0

    def inc(self, instr, k=1):
        self.n += k
        instr.then_inc(self.h, k)
        return self.n

    def dma(self, instr):
        return self.inc(instr, 16)


class Chain:

    def __init__(self, eng, sem):
        self.eng = eng
        self.sem = sem

    def __call__(self, instr):
        self.sem.inc(instr)
        self.eng.wait_ge(self.sem.h, self.sem.n)
        return instr


def build_program(dbg=False, stage=7, nt_run=NT):
    nc = bass.Bass("TRN2", target_bir_lowering=False, debug=False)

    x_d = nc.dram_tensor("x", [T, D], F32, kind="ExternalInput").ap()
    wf_d = nc.dram_tensor("wf", [D, D], F32, kind="ExternalInput").ap()
    wi_d = nc.dram_tensor("wi", [D, D], F32, kind="ExternalInput").ap()
    wo_d = nc.dram_tensor("wo", [D, D], F32, kind="ExternalInput").ap()
    nw_d = nc.dram_tensor("nw", [D], F32, kind="ExternalInput").ap()
    cx_d = nc.dram_tensor("cx", [1], F32, kind="ExternalInput").ap()
    cd_d = nc.dram_tensor("cd", [1], F32, kind="ExternalInput").ap()
    wg_d = nc.dram_tensor("wg", [NI, D], F32, kind="ExternalInput").ap()
    wu_d = nc.dram_tensor("wu", [NI, D], F32, kind="ExternalInput").ap()
    wd_d = nc.dram_tensor("wd", [D, NI], F32, kind="ExternalInput").ap()
    y_d = nc.dram_tensor("y", [T, D], F32, kind="ExternalOutput").ap()
    if dbg:
        y2_d = nc.dram_tensor("y2", [T, D], F32, kind="ExternalOutput").ap()
        y2_t4 = y2_d.rearrange("(n c p) d -> n p c d", n=NT, c=4, p=128)

    x2_d = nc.dram_tensor("x2buf", [T, D], F32).ap()
    h_d = nc.dram_tensor("hbuf", [NI, T], F32).ap()
    ccxi_d = nc.dram_tensor("ccxi", [1, 128], F32)
    ccxo_d = nc.dram_tensor("ccxo", [1, 128], F32)
    cchi_d = nc.dram_tensor("cchi", [1, 128], F32)
    ccho_d = nc.dram_tensor("ccho", [1, 128], F32)

    x_t4 = x_d.rearrange("(n c p) d -> n p c d", n=NT, c=4, p=128)
    x2_t4 = x2_d.rearrange("(n c p) d -> n p c d", n=NT, c=4, p=128)
    x2_tc = x2_d.rearrange("(n p) d -> n p d", n=NTC, p=128)
    y_tc = y_d.rearrange("(n p) d -> n p d", n=NTC, p=128)
    h_ic = h_d.rearrange("(a p) t -> a p t", a=NIC, p=128)

    eye32 = nc.inline_tensor(np.eye(128, dtype=np.float32), "ceye32").ap()
    eyeb = nc.inline_tensor(np.eye(128, dtype=ml_dtypes.bfloat16), "ceyeb").ap()
    hones_np = np.zeros((128, NH * NH), dtype=np.float32)
    for hh in range(NH):
        hones_np[:, hh * NH + hh] = 1.0
    hones = nc.inline_tensor(hones_np, "chones").ap()
    selb_np = np.zeros((NH, NH * 128), dtype=np.float32)
    for hh in range(NH):
        selb_np[hh, hh * 128:(hh + 1) * 128] = 1.0
    selb = nc.inline_tensor(selb_np, "cselb").ap()
    ones1 = nc.inline_tensor(np.ones((1, 128), dtype=np.float32), "cones1").ap()

    es = ExitStack()
    with es:
        e = es.enter_context

        qs = Sem(e(nc.semaphore("qs")))
        qg = Sem(e(nc.semaphore("qg")))
        sv = Sem(e(nc.semaphore("sv")))
        ss = Sem(e(nc.semaphore("ss")))
        st = Sem(e(nc.semaphore("st")))
        sg = Sem(e(nc.semaphore("sg")))
        scc = Sem(e(nc.semaphore("scc")))

        eyet = e(nc.sbuf_tensor("eyet", [128, 128], F32))
        eyebt = e(nc.sbuf_tensor("eyebt", [128, 128], BF16))
        honet = e(nc.sbuf_tensor("honet", [128, NH * NH], F32))
        selbt = e(nc.sbuf_tensor("selbt", [NH, NH * 128], F32))
        ones1t = e(nc.sbuf_tensor("ones1t", [1, 128], F32))
        nwt = e(nc.sbuf_tensor("nwt", [128, NKC], F32))
        m2t = e(nc.sbuf_tensor("m2t", [128, NTC], F32))
        rs2t = e(nc.sbuf_tensor("rs2t", [128, NTC], F32))
        qsc = e(nc.sbuf_tensor("qsc", [128, NTC], F32))
        qbi = e(nc.sbuf_tensor("qbi", [128, NTC], F32))
        carry = e(nc.sbuf_tensor("carry", [128, NH], F32))
        lnmax = e(nc.sbuf_tensor("lnmax", [128, 1], F32))
        lnmc = e(nc.sbuf_tensor("lnmc", [128, NT], F32))
        hmax = e(nc.sbuf_tensor("hmax", [128, 1], F32))
        hmaxc = e(nc.sbuf_tensor("hmaxc", [128, NIC * NTG], F32))
        sxv = e(nc.sbuf_tensor("sxv", [128, 4], F32))
        shv = e(nc.sbuf_tensor("shv", [128, 4], F32))
        sgc = e(nc.sbuf_tensor("sgc", [1, 8], F32))
        ccrow = e(nc.sbuf_tensor("ccrow", [1, 128], F32))
        swgt = e(nc.sbuf_tensor("swgt", [128, NIC], F32))
        swut = e(nc.sbuf_tensor("swut", [128, NIC], F32))
        swdt = e(nc.sbuf_tensor("swdt", [128, NKC], F32))
        stat2 = e(nc.sbuf_tensor("stat2", [128, 10 * 128], F32))
        statS = e(nc.sbuf_tensor("statS", [128, 256], F32))

        def S2(r):
            return stat2[:, r * 128:(r + 1) * 128]

        def S2c(r, c0, c1=None):
            c1 = c0 + 1 if c1 is None else c1
            return stat2[:, r * 128 + c0:r * 128 + c1]

        R_MEAN, R_VARP, R_A2, R_R0, R_WRK, R_M2, R_W2, R_SQ2, R_AMAX, R_TMP = range(10)
        WG_AM, WU_AM, WG_R, WU_R, WG_SW, WU_SW, WD_AM, WD_R, WD_SW, WG_MU = range(10)
        c23t = e(nc.sbuf_tensor("c23t", [128, 1], F32))
        open_stacks = []
        p1w = ExitStack()
        open_stacks.append(p1w)
        p1we = p1w.enter_context
        x_t = p1we(nc.sbuf_tensor("x_t", [128, 2 * 4096], F32))
        wfThi = p1we(nc.sbuf_tensor("wfThi", [128, NKC * D], BF16))
        wfTlo = p1we(nc.sbuf_tensor("wfTlo", [128, NKC * D], BF16))
        woThi = p1we(nc.sbuf_tensor("woThi", [128, NKC * D], BF16))
        woTlo = p1we(nc.sbuf_tensor("woTlo", [128, NKC * D], BF16))
        uThi = p1we(nc.sbuf_tensor("uThi", [128, NKC * NH], BF16))
        uTlo = p1we(nc.sbuf_tensor("uTlo", [128, NKC * NH], BF16))

        def x_sl(p):
            return x_t[:, p * 4096:(p + 1) * 4096]

        block = e(nc.Block())

        pro = ExitStack()
        pe_ = pro.enter_context
        wtile = pe_(nc.sbuf_tensor("wtile", [128, 2 * D], F32))
        u_sb = pe_(nc.sbuf_tensor("u_sb", [NH, D], F32))
        ps_w = pe_(nc.psum_tensor("ps_w", [128, 1024], F32))
        ps_u = pe_(nc.psum_tensor("ps_u", [NH, 1024], F32))

        @block.sync
        def _(sync):
            for dst, src in ((eyet, eye32), (eyebt, eyeb), (honet, hones),
                             (selbt, selb), (ones1t, ones1)):
                qs.dma(sync.dma_start(out=dst[:, :], in_=src))
            qs.dma(sync.dma_start(out=nwt[:, :],
                                  in_=nw_d.rearrange("(c p) -> p c", p=128),
                                  allow_slow_non_contiguous=True))
            qs.dma(sync.dma_start(out=sgc[0:1, 0:1], in_=cx_d.rearrange("(a o) -> a o", a=1)))
            qs.dma(sync.dma_start(out=sgc[0:1, 1:2], in_=cd_d.rearrange("(a o) -> a o", a=1)))
        ev_const = qs.n

        ev_xload = []
        @block.sync
        def _(sync):
            qs.dma(sync.dma_start(out=x_sl(0), in_=x_t4[0]))
        ev_xload.append(qs.n)
        @block.sync
        def _(sync):
            qs.dma(sync.dma_start(out=x_sl(1), in_=x_t4[1]))
        ev_xload.append(qs.n)

        ev_wi = []
        ev_umm_h = []
        for hh in range(NH):
            @block.sync
            def _(sync, hh=hh):
                if hh >= 2:
                    sync.wait_ge(st.h, ev_umm_h[hh - 2])
                qs.dma(sync.dma_start(out=wtile[:, (hh % 2) * D:(hh % 2) * D + D],
                                      in_=wi_d[hh * 128:(hh + 1) * 128, :]))
            ev_wi.append(qs.n)

            @block.tensor
            def _(tensor, hh=hh):
                if hh == 0:
                    tensor.wait_ge(qs.h, ev_const)
                tensor.wait_ge(qs.h, ev_wi[hh])
                for half in range(2):
                    st.inc(tensor.matmul(
                        ps_u[:, half * 512:(half + 1) * 512],
                        honet[:, hh * NH:(hh + 1) * NH],
                        wtile[:, (hh % 2) * D + half * 512:(hh % 2) * D + (half + 1) * 512],
                        start=(hh == 0), stop=(hh == NH - 1)))
            ev_umm_h.append(st.n)
        ev_umm = st.n

        @block.vector
        def _(vector):
            vector.wait_ge(st.h, ev_umm)
            sv.inc(vector.tensor_copy(u_sb[:, :], ps_u[:, :]))
        ev_usb = sv.n

        @block.tensor
        def _(tensor):
            tensor.wait_ge(sv.h, ev_usb)
            for kc in range(NKC):
                st.inc(tensor.transpose(ps_w[:, kc * NH:(kc + 1) * NH],
                                        u_sb[:, kc * 128:(kc + 1) * 128],
                                        eyet[0:NH, 0:NH]))
        ev_ut = st.n

        @block.vector
        def _(vector):
            ch = Chain(vector, sv)
            vector.wait_ge(st.h, ev_ut)
            for kc in range(NKC):
                ch(vector.tensor_scalar(
                    uThi[:, kc * NH:(kc + 1) * NH], ps_w[:, kc * NH:(kc + 1) * NH],
                    nwt[:, kc:kc + 1], None, ALU.mult))
                ch(vector.scalar_tensor_tensor(
                    uTlo[:, kc * NH:(kc + 1) * NH], ps_w[:, kc * NH:(kc + 1) * NH],
                    nwt[:, kc:kc + 1], uThi[:, kc * NH:(kc + 1) * NH],
                    ALU.mult, ALU.subtract))
        ev_utsplit = sv.n

        def prep_weight(w_dram, whi, wlo, init_v, init_s, init_tr):
            ev_ld = []
            evac_v = [init_v]
            evac_s = [init_s]
            ev_tr_all = {-2: init_tr[0], -1: init_tr[1]}
            for oc in range(NKC):
                @block.sync
                def _(sync, oc=oc):
                    sync.wait_ge(st.h, ev_tr_all[oc - 2])
                    qs.dma(sync.dma_start(
                        out=wtile[:, (oc % 2) * D:(oc % 2) * D + D],
                        in_=w_dram[oc * 128:(oc + 1) * 128, :]))
                ev_ld.append(qs.n)
                ev_tr = []
                @block.tensor
                def _(tensor, oc=oc):
                    tensor.wait_ge(qs.h, ev_ld[oc])
                    tensor.wait_ge(sv.h, evac_v[-1])
                    tensor.wait_ge(ss.h, evac_s[-1])
                    for kc in range(NKC):
                        ev_tr.append(st.inc(tensor.transpose(
                            ps_w[:, kc * 128:(kc + 1) * 128],
                            wtile[:, (oc % 2) * D + kc * 128:(oc % 2) * D + (kc + 1) * 128],
                            eyet[:, :])))
                ev_tr_all[oc] = st.n
                ev_hi = []
                @block.scalar
                def _(scalar, oc=oc):
                    for kc in range(NKC):
                        scalar.wait_ge(st.h, ev_tr[3 if kc < 4 else 7])
                        ev_hi.append(ss.inc(scalar.activation(
                            whi[:, kc * D + oc * 128:kc * D + (oc + 1) * 128],
                            ps_w[:, kc * 128:(kc + 1) * 128],
                            AF.Identity, scale=nwt[:, kc:kc + 1])))
                @block.vector
                def _(vector, oc=oc):
                    for kc in range(NKC):
                        vector.wait_ge(ss.h, ev_hi[3 if kc < 4 else 7])
                        sv.inc(vector.scalar_tensor_tensor(
                            wlo[:, kc * D + oc * 128:kc * D + (oc + 1) * 128],
                            ps_w[:, kc * 128:(kc + 1) * 128],
                            nwt[:, kc:kc + 1],
                            whi[:, kc * D + oc * 128:kc * D + (oc + 1) * 128],
                            ALU.mult, ALU.subtract))
                evac_v.append(sv.n)
                evac_s.append(ss.n)
            return evac_v[-1], evac_s[-1], (ev_tr_all[NKC - 2], ev_tr_all[NKC - 1])

        wf_v, wf_s, wf_tr = prep_weight(wf_d, wfThi, wfTlo, ev_utsplit, ss.n,
                                        (ev_umm, ev_umm))
        ev_pro_v, ev_pro_s, _ = prep_weight(wo_d, woThi, woTlo, wf_v, wf_s, wf_tr)
        pro.close()

        def _finish():
            for stk in reversed(open_stacks):
                stk.close()
            @block.sync
            def _(sync):
                sync.wait_ge(qs.h, qs.n)
                sync.wait_ge(sv.h, sv.n)
                sync.wait_ge(ss.h, ss.n)
                sync.wait_ge(st.h, st.n)
            @block.gpsimd
            def _(g):
                g.wait_ge(qg.h, qg.n)
                g.wait_ge(sg.h, sg.n)

        if stage <= 0:
            _finish()
            return nc

        p1 = ExitStack()
        open_stacks.append(p1)
        p1e = p1.enter_context
        z_t = p1e(nc.sbuf_tensor("z_t", [128, 2 * 1024], F32))
        r_t = p1e(nc.sbuf_tensor("r_t", [128, 4096], F32))
        x2_t = p1e(nc.sbuf_tensor("x2_t", [128, 4096], F32))
        zThi = p1e(nc.sbuf_tensor("zThi", [128, NKC * TILE], BF16))
        zTlo = p1e(nc.sbuf_tensor("zTlo", [128, NKC * TILE], BF16))
        y_tile = p1e(nc.sbuf_tensor("y_tile", [128, 4096], F32))
        f_b = p1e(nc.sbuf_tensor("f_b", [128, 2 * TILE], F32))
        i_b = p1e(nc.sbuf_tensor("i_b", [128, 2 * TILE], F32))
        o_b = p1e(nc.sbuf_tensor("o_b", [128, 2 * TILE], F32))
        g_b = p1e(nc.sbuf_tensor("g_b", [128, 2 * TILE], F32))
        v_sb = p1e(nc.sbuf_tensor("v_sb", [NH, TILE], F32))
        tmp_t = p1e(nc.sbuf_tensor("tmp_t", [128, 1024], F32))
        trash = p1e(nc.sbuf_tensor("trash", [128, 1024], F32))
        ps_rot = [p1e(nc.psum_tensor("ps_rotA", [128, 1024], F32)),
                  p1e(nc.psum_tensor("ps_rotB", [128, 1024], F32))]
        ps_f = p1e(nc.psum_tensor("ps_f", [128, TILE], F32))
        ps_o = p1e(nc.psum_tensor("ps_o", [128, TILE], F32))
        ps_v = p1e(nc.psum_tensor("ps_v", [NH, TILE], F32))
        ps_vb = p1e(nc.psum_tensor("ps_vb", [128, TILE], F32))

        @block.vector
        def _(vector):
            vector.wait_ge(qs.h, ev_const)
            vector.memset(carry[:, :], 0.0)
            vector.memset(lnmax[:, :], 0.0)
            vector.memset(lnmc[:, :], 0.0)
            vector.memset(hmaxc[:, :], 0.0)
            vector.memset(c23t[:, :], TWO23)
            sv.inc(vector.memset(hmax[:, :], 0.0))

        rot_guard = [None, None]
        rot_idx = [0]

        def rot_next():
            s_ = rot_idx[0] % 2
            rot_idx[0] += 1
            return s_


        ev_z = {}; ev_r = {}; ev_zt = {}; ev_zhi = {}; ev_zlo = {}
        ev_vsb = {}; ev_fsig = {}; ev_osig = {}; ev_g = {}; ev_ymul = {}
        ev_ytile = {}; ev_yt = {}; ev_x2add = {}; ev_x2st = {}
        ev_sq2 = {}; ev_amax4 = {}; ev_cxn = {}; ev_xdone = {}; ev_rs2 = {}
        ev_fproj = {}; ev_oproj = {}; ev_vb = {}; ev_vmm = {}

        for i in range(nt_run + 1):
            par = i % 2
            la = i - 1

            if i + 1 < nt_run:
                @block.sync
                def _(sync, i=i):
                    if i - 1 >= 0:
                        sync.wait_ge(ss.h, ev_xdone[i - 1])
                    qs.dma(sync.dma_start(out=x_sl((i + 1) % 2), in_=x_t4[i + 1]))
                ev_xload.append(qs.n)

            if la >= 0:
                for tc in range(4):
                    slot = rot_next()
                    @block.tensor
                    def _(tensor, la=la, tc=tc, slot=slot):
                        tensor.wait_ge(sv.h, ev_ytile[la])
                        if rot_guard[slot] is not None:
                            kind, val = rot_guard[slot]
                            tensor.wait_ge(kind.h, val)
                        for oc in range(NH):
                            st.inc(tensor.transpose(
                                ps_rot[slot][:, oc * 128:(oc + 1) * 128],
                                y_tile[:, oc * TILE + tc * 128:oc * TILE + (tc + 1) * 128],
                                eyet[:, :]))
                    ev_yt[(la, tc)] = st.n

                    @block.vector
                    def _(vector, la=la, tc=tc, slot=slot):
                        vector.wait_ge(st.h, ev_yt[(la, tc)])
                        vector.wait_ge(ss.h, ev_r[la])
                        if la >= 1 and tc == 0:
                            vector.wait_ge(ss.h, ev_sq2[la - 1])
                            vector.wait_ge(qs.h, ev_x2st[la - 1])
                        ev_x2add[(la, tc)] = sv.inc(vector.tensor_tensor(
                            x2_t[:, tc * 1024:(tc + 1) * 1024],
                            r_t[:, tc * 1024:(tc + 1) * 1024],
                            ps_rot[slot][:, :], ALU.add))
                    rot_guard[slot] = (sv, ev_x2add[(la, tc)])

                @block.vector
                def _(vector, la=la):
                    ch = Chain(vector, sv)
                    ch(vector.tensor_reduce(
                        S2c(R_M2, 0, 4),
                        x2_t[:, :].rearrange("p (c d) -> p c d", c=4, d=1024),
                        AX, ALU.add))
                    vector.tensor_scalar(S2(R_M2), S2(R_M2), 1.0 / D, None,
                                         ALU.mult)
                    vector.tensor_copy(m2t[:, la * 4:(la + 1) * 4],
                                       S2c(R_M2, 0, 4))
                    for tc in range(4):
                        vector.tensor_scalar(
                            tmp_t[:, :], x2_t[:, tc * 1024:(tc + 1) * 1024],
                            S2c(R_M2, tc), None, ALU.subtract)
                        last = vector.tensor_reduce(
                            S2c(R_AMAX, tc), tmp_t[:, :], AX, ALU.max,
                            apply_absolute_value=True)
                    sv.inc(last)
                ev_amax4[la] = sv.n

                @block.scalar
                def _(scalar, la=la):
                    scalar.wait_ge(sv.h, ev_x2add[(la, 3)])
                    if la >= 1:
                        scalar.wait_ge(sv.h, ev_rs2[la - 1])
                    scalar.memzero(statS[:, 128:132])
                    for tc in range(4):
                        last = scalar.activation(
                            trash[:, :], x2_t[:, tc * 1024:(tc + 1) * 1024],
                            AF.Square, accum_out=statS[:, 128 + tc:129 + tc])
                    ss.inc(last)
                ev_sq2[la] = ss.n

            if i < nt_run:
                @block.scalar
                def _(scalar, i=i, par=par):
                    scalar.wait_ge(qs.h, ev_xload[i])
                    if i >= 1:
                        scalar.wait_ge(sv.h, ev_cxn[i - 1])
                    scalar.memzero(statS[:, 0:4])
                    for tc in range(4):
                        last = scalar.activation(
                            trash[:, :], x_sl(par)[:, tc * 1024:(tc + 1) * 1024],
                            AF.Square, accum_out=statS[:, tc:tc + 1])
                    ss.inc(last)
                ev_sqx = ss.n

                @block.vector
                def _(vector, i=i, par=par, la=la):
                    ch = Chain(vector, sv)
                    vector.wait_ge(qs.h, ev_xload[i])
                    ch(vector.tensor_reduce(
                        S2c(R_MEAN, 0, 4),
                        x_sl(par)[:, :].rearrange("p (c d) -> p c d", c=4, d=1024),
                        AX, ALU.add))
                    vector.tensor_scalar(S2(R_MEAN), S2(R_MEAN),
                                         1.0 / D, None, ALU.mult)
                    vector.wait_ge(ss.h, ev_sqx)
                    vector.tensor_scalar(S2(R_VARP), statS[:, 0:128],
                                         1.0 / D, None, ALU.mult)
                    vector.tensor_tensor(S2(R_W2), S2(R_MEAN),
                                         S2(R_MEAN), ALU.mult)
                    vector.tensor_tensor(S2(R_VARP), S2(R_VARP),
                                         S2(R_W2), ALU.subtract)
                    vector.tensor_scalar(S2(R_VARP), S2(R_VARP),
                                         LN_EPS, None, ALU.add)
                    vector.tensor_scalar(S2(R_A2), S2(R_VARP),
                                         1.0 + RMS_EPS, -LN_EPS, ALU.mult, ALU.add)
                    vector.reciprocal(S2(R_R0), S2(R_VARP))
                    last = vector.reciprocal(S2(R_WRK), S2(R_A2))
                    if la >= 0:
                        vector.wait_ge(ss.h, ev_sq2[la])
                        vector.tensor_scalar(S2(R_SQ2), statS[:, 128:256],
                                             1.0 / D, None, ALU.mult)
                        vector.tensor_tensor(S2(R_W2), S2(R_M2),
                                             S2(R_M2), ALU.mult)
                        vector.tensor_tensor(S2(R_SQ2), S2(R_SQ2),
                                             S2(R_W2), ALU.subtract)
                        vector.tensor_scalar(S2(R_SQ2), S2(R_SQ2),
                                             LN_EPS, None, ALU.add)
                        last = vector.reciprocal(S2(R_TMP), S2(R_SQ2))
                    sv.inc(last)
                ev_recip = sv.n

                @block.scalar
                def _(scalar, la=la):
                    scalar.wait_ge(sv.h, ev_recip)
                    scalar.activation(S2(R_R0), S2(R_R0), AF.Sqrt)
                    last = scalar.activation(S2(R_WRK), S2(R_WRK), AF.Sqrt)
                    if la >= 0:
                        last = scalar.activation(S2(R_TMP), S2(R_TMP), AF.Sqrt)
                    ss.inc(last)
                ev_sqrt = ss.n

                @block.vector
                def _(vector, i=i, la=la):
                    ch = Chain(vector, sv)
                    vector.wait_ge(ss.h, ev_sqrt)
                    if la >= 0:
                        vector.tensor_tensor(S2(R_W2), S2(R_TMP),
                                             S2(R_TMP), ALU.mult)
                        vector.tensor_tensor(S2(R_W2), S2(R_W2),
                                             S2(R_SQ2), ALU.mult)
                        vector.tensor_scalar(S2(R_W2), S2(R_W2),
                                             -0.5, 1.5, ALU.mult, ALU.add)
                        vector.tensor_tensor(S2(R_TMP), S2(R_TMP),
                                             S2(R_W2), ALU.mult)
                        vector.tensor_copy(rs2t[:, la * 4:(la + 1) * 4],
                                           S2c(R_TMP, 0, 4))
                        vector.wait_ge(sv.h, ev_amax4[la])
                        vector.tensor_tensor(S2(R_AMAX), S2(R_AMAX),
                                             S2(R_TMP), ALU.mult)
                        sv.inc(vector.tensor_reduce(lnmc[:, la:la + 1],
                                                    S2c(R_AMAX, 0, 4),
                                                    AX, ALU.max))
                        ev_rs2[la] = sv.n
                    vector.tensor_tensor(S2(R_W2), S2(R_R0),
                                         S2(R_R0), ALU.mult)
                    vector.tensor_tensor(S2(R_W2), S2(R_W2),
                                         S2(R_VARP), ALU.mult)
                    vector.tensor_scalar(S2(R_W2), S2(R_W2),
                                         -0.5, 1.5, ALU.mult, ALU.add)
                    vector.tensor_tensor(S2(R_R0), S2(R_R0),
                                         S2(R_W2), ALU.mult)
                    vector.tensor_tensor(S2(R_W2), S2(R_WRK),
                                         S2(R_WRK), ALU.mult)
                    vector.tensor_tensor(S2(R_W2), S2(R_W2),
                                         S2(R_A2), ALU.mult)
                    vector.tensor_scalar(S2(R_W2), S2(R_W2),
                                         -0.5, 1.5, ALU.mult, ALU.add)
                    vector.tensor_tensor(S2(R_WRK), S2(R_WRK),
                                         S2(R_W2), ALU.mult)
                    vector.tensor_tensor(S2(R_A2), S2(R_MEAN),
                                         S2(R_WRK), ALU.mult)
                    vector.tensor_scalar(S2(R_A2), S2(R_A2),
                                         -1.0, None, ALU.mult)
                    vector.tensor_scalar(S2(R_VARP), S2(R_R0),
                                         1.0, None, ALU.add)
                    vector.tensor_tensor(S2(R_W2), S2(R_MEAN),
                                         S2(R_R0), ALU.mult)
                    sv.inc(vector.tensor_scalar(S2(R_W2), S2(R_W2),
                                                -1.0, None, ALU.mult))
                ev_cxn[i] = sv.n

                for tc in range(4):
                    slot = rot_next()
                    @block.scalar
                    def _(scalar, i=i, tc=tc, par=par):
                        scalar.wait_ge(sv.h, ev_cxn[i])
                        zt_prev = ev_zt.get((i, tc - 2)) if tc >= 2 else \
                            (ev_zt.get((i - 1, tc + 2)) if i >= 1 else None)
                        if zt_prev is not None:
                            scalar.wait_ge(st.h, zt_prev)
                        if ev_x2add.get((i - 1, tc)) is not None:
                            scalar.wait_ge(sv.h, ev_x2add[(i - 1, tc)])
                        ss.inc(scalar.activation(
                            z_t[:, (tc % 2) * 1024:(tc % 2) * 1024 + 1024],
                            x_sl(par)[:, tc * 1024:(tc + 1) * 1024],
                            AF.Identity, scale=S2c(R_WRK, tc),
                            bias=S2c(R_A2, tc)))
                        ev_z[(i, tc)] = ss.n
                        ss.inc(scalar.activation(
                            r_t[:, tc * 1024:(tc + 1) * 1024],
                            x_sl(par)[:, tc * 1024:(tc + 1) * 1024],
                            AF.Identity, scale=S2c(R_VARP, tc),
                            bias=S2c(R_W2, tc)))
                        if tc == 3:
                            ev_r[i] = ss.n
                            ev_xdone[i] = ss.n

                    @block.tensor
                    def _(tensor, i=i, tc=tc, slot=slot):
                        if i == 0 and tc == 0:
                            tensor.wait_ge(sv.h, ev_pro_v)
                            tensor.wait_ge(ss.h, ev_pro_s)
                            tensor.wait_ge(sv.h, ev_utsplit)
                        tensor.wait_ge(ss.h, ev_z[(i, tc)])
                        if rot_guard[slot] is not None:
                            kind, val = rot_guard[slot]
                            tensor.wait_ge(kind.h, val)
                        for kc in range(NKC):
                            st.inc(tensor.transpose(
                                ps_rot[slot][:, kc * 128:(kc + 1) * 128],
                                z_t[:, (tc % 2) * 1024 + kc * 128:(tc % 2) * 1024 + (kc + 1) * 128],
                                eyet[:, :]))
                    ev_zt[(i, tc)] = st.n

                    @block.scalar
                    def _(scalar, i=i, tc=tc, slot=slot):
                        scalar.wait_ge(st.h, ev_zt[(i, tc)])
                        if i >= 1 and tc == 0:
                            scalar.wait_ge(st.h, ev_oproj[(i - 1, NH - 1)])
                        ss.inc(scalar.copy(
                            zThi[:, :].rearrange("p (k t) -> p k t", k=NKC, t=TILE)[:, :, tc * 128:(tc + 1) * 128],
                            ps_rot[slot][:, :].rearrange("p (k b) -> p k b", k=NKC, b=128)))
                    ev_zhi[(i, tc)] = ss.n

                    @block.vector
                    def _(vector, i=i, tc=tc, slot=slot):
                        vector.wait_ge(st.h, ev_zt[(i, tc)])
                        vector.wait_ge(ss.h, ev_zhi[(i, tc)])
                        if i >= 1 and tc == 0:
                            vector.wait_ge(st.h, ev_oproj[(i - 1, NH - 1)])
                        sv.inc(vector.tensor_tensor(
                            zTlo[:, :].rearrange("p (k t) -> p k t", k=NKC, t=TILE)[:, :, tc * 128:(tc + 1) * 128],
                            ps_rot[slot][:, :].rearrange("p (k b) -> p k b", k=NKC, b=128),
                            zThi[:, :].rearrange("p (k t) -> p k t", k=NKC, t=TILE)[:, :, tc * 128:(tc + 1) * 128],
                            ALU.subtract))
                    ev_zlo[(i, tc)] = sv.n
                    rot_guard[slot] = (sv, ev_zlo[(i, tc)])

                    @block.tensor
                    def _(tensor, i=i, tc=tc):
                        tensor.wait_ge(ss.h, ev_zhi[(i, tc)])
                        tensor.wait_ge(sv.h, ev_zlo[(i, tc)])
                        if tc == 0 and i > 0:
                            tensor.wait_ge(sv.h, ev_vsb[i - 1])
                        n = 0
                        for kc in range(NKC):
                            zh = zThi[:, kc * TILE + tc * 128:kc * TILE + (tc + 1) * 128]
                            zl = zTlo[:, kc * TILE + tc * 128:kc * TILE + (tc + 1) * 128]
                            uh = uThi[:, kc * NH:(kc + 1) * NH]
                            ul = uTlo[:, kc * NH:(kc + 1) * NH]
                            for (wop, zop) in ((uh, zh), (ul, zh), (uh, zl)):
                                st.inc(tensor.matmul(
                                    ps_v[:, tc * 128:(tc + 1) * 128], wop, zop,
                                    start=(n == 0), stop=(n == 3 * NKC - 1)))
                                n += 1
                    ev_vmm[(i, tc)] = st.n

                @block.vector
                def _(vector, i=i):
                    vector.wait_ge(st.h, ev_vmm[(i, 3)])
                    sv.inc(vector.tensor_copy(v_sb[:, :], ps_v[:, :]))
                ev_vsb[i] = sv.n

                for oc in range(NH):
                    op = oc % 2

                    @block.tensor
                    def _(tensor, i=i, oc=oc):
                        prev = ev_fsig.get((i, oc - 1)) if oc > 0 else \
                            ev_fsig.get((i - 1, NH - 1))
                        if prev is not None:
                            tensor.wait_ge(ss.h, prev)
                        n = 0
                        for kc in range(NKC):
                            zh = zThi[:, kc * TILE:(kc + 1) * TILE]
                            zl = zTlo[:, kc * TILE:(kc + 1) * TILE]
                            wh = wfThi[:, kc * D + oc * 128:kc * D + (oc + 1) * 128]
                            wl = wfTlo[:, kc * D + oc * 128:kc * D + (oc + 1) * 128]
                            for (wop, zop) in ((wh, zh), (wl, zh), (wh, zl)):
                                st.inc(tensor.matmul(
                                    ps_f[:, :], wop, zop,
                                    start=(n == 0), stop=(n == 3 * NKC - 1)))
                                n += 1
                    ev_fproj[(i, oc)] = st.n

                    @block.scalar
                    def _(scalar, i=i, oc=oc, op=op):
                        scalar.wait_ge(st.h, ev_fproj[(i, oc)])
                        prevy = ev_ymul.get((i, oc - 2)) if oc >= 2 else \
                            (ev_ymul.get((i - 1, NH - 2 + oc)) if i >= 1 else None)
                        if prevy is not None:
                            scalar.wait_ge(sv.h, prevy)
                        ss.inc(scalar.activation(
                            f_b[:, op * TILE:(op + 1) * TILE], ps_f[:, :], AF.Sigmoid))
                    ev_fsig[(i, oc)] = ss.n

                    @block.tensor
                    def _(tensor, i=i, oc=oc):
                        prev = ev_osig.get((i, oc - 1)) if oc > 0 else \
                            ev_osig.get((i - 1, NH - 1))
                        if prev is not None:
                            tensor.wait_ge(ss.h, prev)
                        n = 0
                        for kc in range(NKC):
                            zh = zThi[:, kc * TILE:(kc + 1) * TILE]
                            zl = zTlo[:, kc * TILE:(kc + 1) * TILE]
                            wh = woThi[:, kc * D + oc * 128:kc * D + (oc + 1) * 128]
                            wl = woTlo[:, kc * D + oc * 128:kc * D + (oc + 1) * 128]
                            for (wop, zop) in ((wh, zh), (wl, zh), (wh, zl)):
                                st.inc(tensor.matmul(
                                    ps_o[:, :], wop, zop,
                                    start=(n == 0), stop=(n == 3 * NKC - 1)))
                                n += 1
                        ev_oproj[(i, oc)] = st.n
                        tensor.wait_ge(sv.h, ev_vsb[i])
                        prevg = ev_g.get((i, oc - 1)) if oc > 0 else \
                            ev_g.get((i - 1, NH - 1))
                        if prevg is not None:
                            tensor.wait_ge(sv.h, prevg)
                        st.inc(tensor.matmul(
                            ps_vb[:, :], selbt[:, oc * 128:(oc + 1) * 128],
                            v_sb[:, :], start=True, stop=True))
                    ev_vb[(i, oc)] = st.n

                    @block.scalar
                    def _(scalar, i=i, oc=oc, op=op):
                        scalar.wait_ge(st.h, ev_oproj[(i, oc)])
                        prevy = ev_ymul.get((i, oc - 2)) if oc >= 2 else \
                            (ev_ymul.get((i - 1, NH - 2 + oc)) if i >= 1 else None)
                        if prevy is not None:
                            scalar.wait_ge(sv.h, prevy)
                        ss.inc(scalar.activation(
                            o_b[:, op * TILE:(op + 1) * TILE], ps_o[:, :], AF.Sigmoid))
                    ev_osig[(i, oc)] = ss.n

                    @block.vector
                    def _(vector, i=i, oc=oc, op=op):
                        fsl = f_b[:, op * TILE:(op + 1) * TILE]
                        isl = i_b[:, op * TILE:(op + 1) * TILE]
                        gsl = g_b[:, op * TILE:(op + 1) * TILE]
                        osl = o_b[:, op * TILE:(op + 1) * TILE]
                        if oc == 0 and i >= 1 and (i - 1, 3) in ev_yt:
                            vector.wait_ge(st.h, ev_yt[(i - 1, 3)])
                        vector.wait_ge(ss.h, ev_fsig[(i, oc)])
                        vector.tensor_scalar(fsl, fsl, FORGET_LB, None, ALU.max)
                        vector.tensor_scalar(isl, fsl, -1.0, 1.0, ALU.mult, ALU.add)
                        vector.wait_ge(st.h, ev_vb[(i, oc)])
                        ev_g[(i, oc)] = sv.inc(vector.tensor_tensor(
                            gsl, isl, ps_vb[:, :], ALU.mult))
                        sv.inc(vector.tensor_tensor_scan(
                            fsl, fsl, gsl, carry[:, oc:oc + 1], ALU.mult, ALU.add))
                        vector.wait_ge(sv.h, sv.n)
                        vector.tensor_copy(carry[:, oc:oc + 1],
                                           fsl[:, TILE - 1:TILE])
                        vector.wait_ge(ss.h, ev_osig[(i, oc)])
                        sv.inc(vector.tensor_tensor(
                            y_tile[:, oc * TILE:(oc + 1) * TILE], osl, fsl,
                            ALU.mult))
                    ev_ymul[(i, oc)] = sv.n
                ev_ytile[i] = ev_ymul[(i, NH - 1)]

            if i == nt_run and la >= 0:
                @block.vector
                def _(vector, la=la):
                    vector.wait_ge(ss.h, ev_sq2[la])
                    vector.tensor_scalar(S2(R_SQ2), statS[:, 128:256],
                                         1.0 / D, None, ALU.mult)
                    vector.tensor_tensor(S2(R_W2), S2(R_M2),
                                         S2(R_M2), ALU.mult)
                    vector.tensor_tensor(S2(R_SQ2), S2(R_SQ2),
                                         S2(R_W2), ALU.subtract)
                    vector.tensor_scalar(S2(R_SQ2), S2(R_SQ2),
                                         LN_EPS, None, ALU.add)
                    sv.inc(vector.reciprocal(S2(R_TMP), S2(R_SQ2)))
                ev_rc2 = sv.n

                @block.scalar
                def _(scalar):
                    scalar.wait_ge(sv.h, ev_rc2)
                    ss.inc(scalar.activation(S2(R_TMP), S2(R_TMP), AF.Sqrt))
                ev_sq2s = ss.n

                @block.vector
                def _(vector, la=la):
                    ch = Chain(vector, sv)
                    vector.wait_ge(ss.h, ev_sq2s)
                    vector.tensor_tensor(S2(R_W2), S2(R_TMP),
                                         S2(R_TMP), ALU.mult)
                    vector.tensor_tensor(S2(R_W2), S2(R_W2),
                                         S2(R_SQ2), ALU.mult)
                    vector.tensor_scalar(S2(R_W2), S2(R_W2),
                                         -0.5, 1.5, ALU.mult, ALU.add)
                    vector.tensor_tensor(S2(R_TMP), S2(R_TMP),
                                         S2(R_W2), ALU.mult)
                    vector.tensor_copy(rs2t[:, la * 4:(la + 1) * 4],
                                       S2c(R_TMP, 0, 4))
                    vector.wait_ge(sv.h, ev_amax4[la])
                    vector.tensor_tensor(S2(R_AMAX), S2(R_AMAX),
                                         S2(R_TMP), ALU.mult)
                    sv.inc(vector.tensor_reduce(lnmc[:, la:la + 1],
                                                S2c(R_AMAX, 0, 4),
                                                AX, ALU.max))
                ev_rs2[la] = sv.n

            if la >= 0:
                @block.sync
                def _(sync, la=la):
                    sync.wait_ge(sv.h, ev_rs2[la])
                    sync.wait_ge(ss.h, ev_sq2[la])
                    qs.dma(sync.dma_start(out=x2_t4[la], in_=x2_t[:, :]))
                    if dbg:
                        qs.dma(sync.dma_start(out=y2_t4[la], in_=x2_t[:, :]))
                ev_x2st[la] = qs.n

        if stage <= 1:
            if dbg == 2:
                dmp = {}
                for nm, tile_ in (("d_r", r_t), ("d_y", y_tile), ("d_x2", x2_t),
                                  ("d_f", f_b), ("d_o", o_b), ("d_g", g_b),
                                  ("d_stat", stat), ("d_m2", m2t), ("d_rs2", rs2t),
                                  ("d_carry", carry), ("d_zhi", zThi),
                                  ("d_uhi", uThi), ("d_ulo", uTlo)):
                    shp = [tile_.shape[0], tile_.shape[1]]
                    dt_ = tile_.dtype
                    dmp[nm] = (nc.dram_tensor(nm, shp, dt_, kind="ExternalOutput").ap(), tile_)
                vdmp = nc.dram_tensor("d_v", [NH, TILE], F32, kind="ExternalOutput").ap()

                @block.sync
                def _(sync):
                    sync.wait_ge(sv.h, sv.n)
                    sync.wait_ge(ss.h, ss.n)
                    sync.wait_ge(st.h, st.n)
                    for nm, (ap_, tile_) in dmp.items():
                        qs.dma(sync.dma_start(out=ap_, in_=tile_[:, :]))
                    qs.dma(sync.dma_start(out=vdmp, in_=v_sb[:, :]))
            _finish()
            return nc

        @block.vector
        def _(vector):
            ch = Chain(vector, sv)
            vector.wait_ge(sv.h, ev_rs2[nt_run - 1])
            ch(vector.tensor_reduce(lnmax[:, 0:1], lnmc[:, 0:nt_run],
                                    AX, ALU.max))
        ev_lnf = sv.n

        @block.tensor
        def _(tensor):
            tensor.wait_ge(sv.h, ev_lnf)
            st.inc(tensor.transpose(ps_v[0:1, 0:128], lnmax[:, 0:1], eyet[:, :]))
        ev_lt = st.n

        @block.vector
        def _(vector):
            vector.wait_ge(st.h, ev_lt)
            sv.inc(vector.tensor_copy(ccrow[0:1, :], ps_v[0:1, 0:128]))
        ev_lrow = sv.n

        @block.scalar
        def _(scalar):
            scalar.wait_ge(qs.h, ev_const)
            scalar.activation(sgc[0:1, 2:3], sgc[0:1, 0:1], AF.Exp, scale=-1.0)
            ss.inc(scalar.activation(sgc[0:1, 3:4], sgc[0:1, 1:2], AF.Exp, scale=-1.0))
        ev_exp = ss.n

        @block.gpsimd
        def _(g):
            g.wait_ge(sv.h, ev_lrow)
            qg.dma(g.dma_start(out=ccxi_d.ap(), in_=ccrow[0:1, :]))
            g.wait_ge(qg.h, qg.n)
            scc.inc(g.collective_compute(
                "AllReduce", ALU.max, replica_groups=[list(range(8))],
                ins=[ccxi_d.ap()], outs=[ccxo_d.ap()]))
            g.wait_ge(scc.h, scc.n)
            qg.dma(g.dma_start(out=ccrow[0:1, :], in_=ccxo_d.ap()))
        ev_ccx = qg.n

        @block.vector
        def _(vector):
            ch = Chain(vector, sv)
            vector.wait_ge(ss.h, ev_exp)
            ch(vector.tensor_scalar(sgc[0:1, 2:4], sgc[0:1, 2:4], 1.0, None, ALU.add))
            ch(vector.reciprocal(sgc[0:1, 2:4], sgc[0:1, 2:4]))
            vector.wait_ge(qg.h, ev_ccx)
            ch(vector.tensor_reduce(sgc[0:1, 4:5], ccrow[0:1, :], AX, ALU.max))
            ch(vector.tensor_tensor(sgc[0:1, 4:5], sgc[0:1, 4:5],
                                    sgc[0:1, 2:3], ALU.mult))
        ev_gm = sv.n

        @block.tensor
        def _(tensor):
            tensor.wait_ge(sv.h, ev_gm)
            st.inc(tensor.matmul(ps_vb[:, 0:1], ones1t[:, :], sgc[0:1, 4:5],
                                 start=True, stop=True))
        ev_gb = st.n

        @block.vector
        def _(vector):
            ch = Chain(vector, sv)
            vector.wait_ge(st.h, ev_gb)
            ch(vector.tensor_scalar(sxv[:, 1:2], ps_vb[:, 0:1], 1.0 / QMAX,
                                    None, ALU.mult))
            ch(vector.reciprocal(sxv[:, 0:1], sxv[:, 1:2]))
            ch(vector.tensor_scalar(qsc[:, :], rs2t[:, :], sxv[:, 0:1], None,
                                    ALU.mult))
            ch(vector.tensor_tensor(qbi[:, :], m2t[:, :], qsc[:, :], ALU.mult))
            ch(vector.tensor_scalar(qbi[:, :], qbi[:, :], -1.0, None, ALU.mult))
        ev_qsc = sv.n

        p1.close(); open_stacks.remove(p1)
        p1w.close(); open_stacks.remove(p1w)

        if stage <= 2:
            _finish()
            return nc

        p2c = ExitStack()
        open_stacks.append(p2c)
        wdT = p2c.enter_context(nc.sbuf_tensor("wdT", [128, NIC * 1024], BF16))

        p2 = ExitStack()
        open_stacks.append(p2)
        p2e = p2.enter_context
        xqT = p2e(nc.sbuf_tensor("xqT", [128, NKC * T], BF16))
        wdst = p2e(nc.sbuf_tensor("wdst", [128, NI], F32))
        wdqn = p2e(nc.sbuf_tensor("wdqn", [128, NI], BF16))
        wgst = p2e(nc.sbuf_tensor("wgst", [128, 2 * D], F32))
        wust = p2e(nc.sbuf_tensor("wust", [128, 2 * D], F32))
        wgqn = p2e(nc.sbuf_tensor("wgqn", [128, 2 * D], BF16))
        wuqn = p2e(nc.sbuf_tensor("wuqn", [128, 2 * D], BF16))
        wgqT = p2e(nc.sbuf_tensor("wgqT", [128, 2 * D], BF16))
        wuqT = p2e(nc.sbuf_tensor("wuqT", [128, 2 * D], BF16))
        q_s = p2e(nc.sbuf_tensor("q_s", [128, 2 * D], F32))
        xqn = p2e(nc.sbuf_tensor("xqn", [128, 2 * D], BF16))
        wrnd = p2e(nc.sbuf_tensor("wrnd", [128, 2 * D], F32))
        gsil = p2e(nc.sbuf_tensor("gsil", [128, 2 * TG], F32))
        h_t = p2e(nc.sbuf_tensor("h_t", [128, 2 * TG], F32))
        mgt = p2e(nc.sbuf_tensor("mgt", [128, NIC], F32))
        mut = p2e(nc.sbuf_tensor("mut", [128, NIC], F32))
        ps_gate = p2e(nc.psum_tensor("ps_gate", [128, TG], F32))
        ps_up = p2e(nc.psum_tensor("ps_up", [128, TG], F32))
        ps_wt = p2e(nc.psum_tensor("ps_wt", [128, 1024], BF16))
        ps_wtg = p2e(nc.psum_tensor("ps_wtg", [128, 1024], BF16))

        PH1_SS, PH1_SV, PH1_ST, PH1_SG = ss.n, sv.n, st.n, sg.n

        ev_x2ld = {}; ev_aff = {}; ev_xq = {}; ev_xqtr = {}; ev_xqevac = {}
        for tc in range(NTC):
            parq = tc % 2
            @block.sync
            def _(sync, tc=tc, parq=parq):
                if tc == 0:
                    sync.wait_ge(sv.h, ev_qsc)
                    sync.wait_ge(ss.h, PH1_SS)
                    sync.wait_ge(sg.h, PH1_SG)
                    sync.wait_ge(st.h, PH1_ST)
                    sync.wait_ge(qs.h, ev_x2st[nt_run - 1])
                if tc >= 2:
                    sync.wait_ge(sv.h, ev_xq[tc - 2])
                qs.dma(sync.dma_start(out=q_s[:, parq * D:(parq + 1) * D],
                                      in_=x2_tc[tc]))
            ev_x2ld[tc] = qs.n

            @block.scalar
            def _(scalar, tc=tc, parq=parq):
                scalar.wait_ge(qs.h, ev_x2ld[tc])
                ss.inc(scalar.activation(
                    q_s[:, parq * D:(parq + 1) * D],
                    q_s[:, parq * D:(parq + 1) * D],
                    AF.Identity, scale=qsc[:, tc:tc + 1], bias=qbi[:, tc:tc + 1]))
            ev_aff[tc] = ss.n

            @block.vector
            def _(vector, tc=tc, parq=parq):
                vector.wait_ge(ss.h, ev_aff[tc])
                if tc >= 2:
                    vector.wait_ge(st.h, ev_xqtr[tc - 2])
                vector.tensor_scalar(q_s[:, parq * D:(parq + 1) * D],
                                     q_s[:, parq * D:(parq + 1) * D],
                                     TWO23, -TWO23, ALU.add, ALU.add)
                sv.inc(vector.tensor_scalar(
                    xqn[:, parq * D:(parq + 1) * D],
                    q_s[:, parq * D:(parq + 1) * D],
                    QMAX, -QMAX - 1.0, ALU.min, ALU.max))
            ev_xq[tc] = sv.n

            @block.tensor
            def _(tensor, tc=tc, parq=parq):
                tensor.wait_ge(sv.h, ev_xq[tc])
                if tc >= 1:
                    tensor.wait_ge(ss.h, ev_xqevac[tc - 1])
                for kc in range(NKC):
                    st.inc(tensor.transpose(
                        ps_wt[:, kc * 128:(kc + 1) * 128],
                        xqn[:, parq * D + kc * 128:parq * D + (kc + 1) * 128],
                        eyebt[:, :]))
            ev_xqtr[tc] = st.n

            @block.scalar
            def _(scalar, tc=tc):
                scalar.wait_ge(st.h, ev_xqtr[tc])
                ss.inc(scalar.copy(
                    xqT[:, :].rearrange("p (k t) -> p k t", k=NKC, t=T)[:, :, tc * 128:(tc + 1) * 128],
                    ps_wt[:, :].rearrange("p (k b) -> p k b", k=NKC, b=128)))
            ev_xqevac[tc] = ss.n
        ev_xqT_done = ev_xqevac[NTC - 1]

        if stage <= 3:
            _finish()
            return nc

        ev_wdld = {}; ev_wdam = {}; ev_wdr = {}; ev_wdq = {}
        ev_wdtr = {}; ev_wdevac = {}; ev_wdafs = {}; ev_wdq_part = [0]
        for hc in range(NKC):
            @block.sync
            def _(sync, hc=hc):
                if hc >= 1:
                    sync.wait_ge(ss.h, ev_wdafs[hc - 1])
                qs.dma(sync.dma_start(out=wdst[:, :],
                                      in_=wd_d[hc * 128:(hc + 1) * 128, :]))
            ev_wdld[hc] = qs.n

            @block.vector
            def _(vector, hc=hc):
                vector.wait_ge(qs.h, ev_wdld[hc])
                if hc >= 1:
                    vector.wait_ge(ss.h, ev_wdafs[hc - 1])
                chd = Chain(vector, sv)
                chd(vector.tensor_reduce(S2c(WD_AM, 0), wdst[:, :], AX, ALU.max,
                                         apply_absolute_value=True))
                vector.tensor_scalar(S2(WD_SW), S2(WD_AM),
                                     1.0 / QMAX, None, ALU.mult)
                vector.tensor_copy(swdt[:, hc:hc + 1], S2c(WD_SW, 0))
                sv.inc(vector.reciprocal(S2(WD_R), S2(WD_SW)))
            ev_wdr[hc] = sv.n

            for cofs, clen in ((0, 2048), (2048, NI - 2048)):
                @block.scalar
                def _(scalar, hc=hc, cofs=cofs, clen=clen):
                    if cofs == 0:
                        scalar.wait_ge(sv.h, ev_wdr[hc])
                    if hc >= 1 or cofs > 0:
                        scalar.wait_ge(sv.h, ev_wdq_part[-1])
                    ss.inc(scalar.activation(
                        wrnd[:, 0:clen], wdst[:, cofs:cofs + clen], AF.Identity,
                        scale=S2c(WD_R, 0), bias=c23t[:, 0:1]))
                ev_wdaf = ss.n

                @block.vector
                def _(vector, hc=hc, cofs=cofs, clen=clen, ev=ev_wdaf):
                    vector.wait_ge(ss.h, ev)
                    if hc >= 1 and cofs == 0:
                        vector.wait_ge(st.h, ev_wdtr[hc - 1])
                    vector.tensor_scalar(wrnd[:, 0:clen], wrnd[:, 0:clen],
                                         -TWO23, QMAX, ALU.add, ALU.min)
                    sv.inc(vector.tensor_scalar(
                        wdqn[:, cofs:cofs + clen], wrnd[:, 0:clen],
                        -QMAX - 1.0, None, ALU.max))
                ev_wdq_part.append(sv.n)
            ev_wdq[hc] = sv.n
            ev_wdafs[hc] = ss.n

            for grp in range(3):
                lo = grp * 8
                hi_ = min(lo + 8, NIC)
                @block.tensor
                def _(tensor, hc=hc, lo=lo, hi_=hi_, grp=grp):
                    tensor.wait_ge(sv.h, ev_wdq[hc])
                    if hc == 0 and grp == 0:
                        tensor.wait_ge(ss.h, ev_xqT_done)
                    if grp >= 1:
                        tensor.wait_ge(ss.h, ev_wdevac[(hc, grp - 1)])
                    elif hc >= 1:
                        tensor.wait_ge(ss.h, ev_wdevac[(hc - 1, 2)])
                    for icc in range(lo, hi_):
                        st.inc(tensor.transpose(
                            ps_wt[:, (icc - lo) * 128:(icc - lo + 1) * 128],
                            wdqn[:, icc * 128:(icc + 1) * 128],
                            eyebt[:, :]))
                ev_wdtr_g = st.n

                @block.scalar
                def _(scalar, hc=hc, lo=lo, hi_=hi_, ev=ev_wdtr_g):
                    scalar.wait_ge(st.h, ev)
                    for icc in range(lo, hi_):
                        last = scalar.copy(
                            wdT[:, icc * 1024 + hc * 128:icc * 1024 + (hc + 1) * 128],
                            ps_wt[:, (icc - lo) * 128:(icc - lo + 1) * 128])
                    ss.inc(last)
                ev_wdevac[(hc, grp)] = ss.n
            ev_wdtr[hc] = st.n
        ev_wd_done = ev_wdevac[(NKC - 1, 2)]

        if stage <= 4:
            _finish()
            return nc

        ev_wgld = {}; ev_am2 = {}; ev_r2 = {}; ev_wgq = {}; ev_wgafs = {}
        ev_wgtr = {}; ev_wgte1 = {}; ev_wutr = {}; ev_wgte = {}
        ev_gmm = {}; ev_umm = {}; ev_sil = {}; ev_hmul = {}
        ev_hmax = {}; ev_hst = {}
        for ic in range(NIC):
            parw = ic % 2
            @block.sync
            def _(sync, ic=ic, parw=parw):
                if ic >= 2:
                    sync.wait_ge(ss.h, ev_wgafs[ic - 2])
                qs.dma(sync.dma_start(out=wgst[:, parw * D:(parw + 1) * D],
                                      in_=wg_d[ic * 128:(ic + 1) * 128, :]))
                qs.dma(sync.dma_start(out=wust[:, parw * D:(parw + 1) * D],
                                      in_=wu_d[ic * 128:(ic + 1) * 128, :]))
            ev_wgld[ic] = qs.n

            @block.vector
            def _(vector, ic=ic, parw=parw):
                vector.wait_ge(qs.h, ev_wgld[ic])
                if ic >= 1:
                    vector.wait_ge(ss.h, ev_wgafs[ic - 1])
                chw = Chain(vector, sv)
                vector.tensor_reduce(S2c(WG_AM, 0),
                                     wgst[:, parw * D:(parw + 1) * D],
                                     AX, ALU.max, apply_absolute_value=True)
                chw(vector.tensor_reduce(S2c(WU_AM, 0),
                                         wust[:, parw * D:(parw + 1) * D],
                                         AX, ALU.max, apply_absolute_value=True))
                vector.tensor_scalar(S2(WG_SW), S2(WG_AM),
                                     1.0 / QMAX, None, ALU.mult)
                vector.tensor_scalar(S2(WU_SW), S2(WU_AM),
                                     1.0 / QMAX, None, ALU.mult)
                vector.tensor_copy(swgt[:, ic:ic + 1], S2c(WG_SW, 0))
                vector.tensor_copy(swut[:, ic:ic + 1], S2c(WU_SW, 0))
                vector.reciprocal(S2(WG_R), S2(WG_SW))
                vector.reciprocal(S2(WU_R), S2(WU_SW))
                vector.tensor_scalar(S2(WG_MU), S2(WG_SW),
                                     sxv[:, 1:2], None, ALU.mult)
                vector.tensor_copy(mgt[:, ic:ic + 1], S2c(WG_MU, 0))
                vector.tensor_scalar(S2(WG_MU), S2(WU_SW),
                                     sxv[:, 1:2], None, ALU.mult)
                sv.inc(vector.tensor_copy(mut[:, ic:ic + 1], S2c(WG_MU, 0)))
            ev_r2[ic] = sv.n

            @block.scalar
            def _(scalar, ic=ic, parw=parw):
                scalar.wait_ge(sv.h, ev_r2[ic])
                if ic == 0:
                    scalar.wait_ge(sv.h, ev_wdq[NKC - 1])
                elif ic >= 1:
                    scalar.wait_ge(sv.h, ev_wgq[ic - 1])
                scalar.activation(wrnd[:, 0:D], wgst[:, parw * D:(parw + 1) * D],
                                  AF.Identity, scale=S2c(WG_R, 0),
                                  bias=c23t[:, 0:1])
                ss.inc(scalar.activation(wrnd[:, D:2 * D],
                                         wust[:, parw * D:(parw + 1) * D],
                                         AF.Identity, scale=S2c(WU_R, 0),
                                         bias=c23t[:, 0:1]))
            ev_wgaf = ss.n
            ev_wgafs[ic] = ss.n

            @block.vector
            def _(vector, ic=ic, parw=parw, ev=ev_wgaf):
                vector.wait_ge(ss.h, ev)
                if ic >= 2:
                    vector.wait_ge(st.h, ev_wutr[ic - 2])
                for half, wqn in ((0, wgqn), (1, wuqn)):
                    vector.tensor_scalar(wrnd[:, half * D:(half + 1) * D],
                                         wrnd[:, half * D:(half + 1) * D],
                                         -TWO23, QMAX, ALU.add, ALU.min)
                    sv.inc(vector.tensor_scalar(
                        wqn[:, parw * D:(parw + 1) * D],
                        wrnd[:, half * D:(half + 1) * D],
                        -QMAX - 1.0, None, ALU.max))
            ev_wgq[ic] = sv.n

            @block.tensor
            def _(tensor, ic=ic, parw=parw):
                tensor.wait_ge(sv.h, ev_wgq[ic])
                if ic >= 2:
                    tensor.wait_ge(ss.h, ev_wgte[ic - 2])
                for kc in range(NKC):
                    st.inc(tensor.transpose(
                        ps_wtg[:, kc * 128:(kc + 1) * 128],
                        wgqn[:, parw * D + kc * 128:parw * D + (kc + 1) * 128],
                        eyebt[:, :]))
            ev_wgtr[ic] = st.n

            @block.scalar
            def _(scalar, ic=ic, parw=parw):
                scalar.wait_ge(st.h, ev_wgtr[ic])
                ss.inc(scalar.copy(wgqT[:, parw * D:(parw + 1) * D],
                                          ps_wtg[:, :]))
            ev_wgte1[ic] = ss.n

            @block.tensor
            def _(tensor, ic=ic, parw=parw):
                tensor.wait_ge(ss.h, ev_wgte1[ic])
                for kc in range(NKC):
                    st.inc(tensor.transpose(
                        ps_wtg[:, kc * 128:(kc + 1) * 128],
                        wuqn[:, parw * D + kc * 128:parw * D + (kc + 1) * 128],
                        eyebt[:, :]))
            ev_wutr[ic] = st.n

            @block.scalar
            def _(scalar, ic=ic, parw=parw):
                scalar.wait_ge(st.h, ev_wutr[ic])
                ss.inc(scalar.copy(wuqT[:, parw * D:(parw + 1) * D],
                                          ps_wtg[:, :]))
            ev_wgte[ic] = ss.n

            for ts_ in range(NTG):
                parh = ts_ % 2

                @block.tensor
                def _(tensor, ic=ic, ts_=ts_, parw=parw):
                    tensor.wait_ge(ss.h, ev_wgte[ic])
                    tensor.wait_ge(ss.h, ev_xqevac[(ts_ + 1) * NKC - 1])
                    pk = (ic, ts_ - 1) if ts_ > 0 else (ic - 1, NTG - 1)
                    if pk in ev_sil:
                        tensor.wait_ge(ss.h, ev_sil[pk])
                    for kc in range(NKC):
                        for half in range(2):
                            st.inc(tensor.matmul(
                                ps_gate[:, half * 512:(half + 1) * 512],
                                wgqT[:, parw * D + kc * 128:parw * D + (kc + 1) * 128],
                                xqT[:, kc * T + ts_ * TG + half * 512:kc * T + ts_ * TG + (half + 1) * 512],
                                start=(kc == 0), stop=(kc == NKC - 1)))
                    ev_gmm[(ic, ts_)] = st.n
                    if pk in ev_hmul:
                        tensor.wait_ge(sv.h, ev_hmul[pk])
                    for kc in range(NKC):
                        for half in range(2):
                            st.inc(tensor.matmul(
                                ps_up[:, half * 512:(half + 1) * 512],
                                wuqT[:, parw * D + kc * 128:parw * D + (kc + 1) * 128],
                                xqT[:, kc * T + ts_ * TG + half * 512:kc * T + ts_ * TG + (half + 1) * 512],
                                start=(kc == 0), stop=(kc == NKC - 1)))
                ev_umm[(ic, ts_)] = st.n

                @block.scalar
                def _(scalar, ic=ic, ts_=ts_, parh=parh):
                    scalar.wait_ge(st.h, ev_gmm[(ic, ts_)])
                    pk = (ic, ts_ - 2) if ts_ >= 2 else \
                        ((ic - 1, NTG - 2 + ts_) if ic > 0 else None)
                    if pk is not None and pk in ev_hmul:
                        scalar.wait_ge(sv.h, ev_hmul[pk])
                    ss.inc(scalar.activation(
                        gsil[:, parh * TG:(parh + 1) * TG], ps_gate[:, :],
                        AF.Silu, scale=mgt[:, ic:ic + 1]))
                ev_sil[(ic, ts_)] = ss.n

                @block.vector
                def _(vector, ic=ic, ts_=ts_, parh=parh):
                    vector.wait_ge(st.h, ev_umm[(ic, ts_)])
                    vector.wait_ge(ss.h, ev_sil[(ic, ts_)])
                    pk = (ic, ts_ - 2) if ts_ >= 2 else \
                        ((ic - 1, NTG - 2 + ts_) if ic > 0 else None)
                    if pk is not None and pk in ev_hst:
                        vector.wait_ge(qs.h, ev_hst[pk])
                    sv.inc(vector.scalar_tensor_tensor(
                        h_t[:, parh * TG:(parh + 1) * TG], ps_up[:, :],
                        mut[:, ic:ic + 1], gsil[:, parh * TG:(parh + 1) * TG],
                        ALU.mult, ALU.mult))
                ev_hmul[(ic, ts_)] = sv.n

                @block.vector
                def _(vector, ic=ic, ts_=ts_, parh=parh):
                    sv.inc(vector.tensor_reduce(
                        hmaxc[:, ic * NTG + ts_:ic * NTG + ts_ + 1],
                        h_t[:, parh * TG:(parh + 1) * TG],
                        AX, ALU.max, apply_absolute_value=True))
                ev_hmax[(ic, ts_)] = sv.n

                @block.sync
                def _(sync, ic=ic, ts_=ts_, parh=parh):
                    sync.wait_ge(sv.h, ev_hmul[(ic, ts_)])
                    qs.dma(sync.dma_start(
                        out=h_ic[ic][:, ts_ * TG:(ts_ + 1) * TG],
                        in_=h_t[:, parh * TG:(parh + 1) * TG]))
                ev_hst[(ic, ts_)] = qs.n

        if stage <= 5:
            _finish()
            return nc

        @block.vector
        def _(vector):
            ch = Chain(vector, sv)
            vector.wait_ge(sv.h, ev_hmax[(NIC - 1, NTG - 1)])
            ch(vector.tensor_reduce(hmax[:, 0:1], hmaxc[:, :], AX, ALU.max))
        ev_hmf = sv.n

        @block.tensor
        def _(tensor):
            tensor.wait_ge(sv.h, ev_hmf)
            st.inc(tensor.transpose(ps_up[0:1, 0:128], hmax[:, 0:1], eyet[:, :]))
        ev_ht = st.n

        @block.vector
        def _(vector):
            vector.wait_ge(st.h, ev_ht)
            sv.inc(vector.tensor_copy(ccrow[0:1, :], ps_up[0:1, 0:128]))
        ev_hrow = sv.n

        @block.gpsimd
        def _(g):
            g.wait_ge(sv.h, ev_hrow)
            qg.dma(g.dma_start(out=cchi_d.ap(), in_=ccrow[0:1, :]))
            g.wait_ge(qg.h, qg.n)
            scc.inc(g.collective_compute(
                "AllReduce", ALU.max, replica_groups=[list(range(8))],
                ins=[cchi_d.ap()], outs=[ccho_d.ap()]))
            g.wait_ge(scc.h, scc.n)
            qg.dma(g.dma_start(out=ccrow[0:1, :], in_=ccho_d.ap()))
        ev_cch = qg.n

        @block.vector
        def _(vector):
            ch = Chain(vector, sv)
            vector.wait_ge(qg.h, ev_cch)
            ch(vector.tensor_reduce(sgc[0:1, 5:6], ccrow[0:1, :], AX, ALU.max))
            ch(vector.tensor_tensor(sgc[0:1, 5:6], sgc[0:1, 5:6],
                                    sgc[0:1, 3:4], ALU.mult))
        ev_gh = sv.n

        @block.tensor
        def _(tensor):
            tensor.wait_ge(sv.h, ev_gh)
            st.inc(tensor.matmul(ps_up[:, 0:1], ones1t[:, :], sgc[0:1, 5:6],
                                 start=True, stop=True))
        ev_ghb = st.n

        @block.vector
        def _(vector):
            ch = Chain(vector, sv)
            vector.wait_ge(st.h, ev_ghb)
            ch(vector.tensor_scalar(shv[:, 1:2], ps_up[:, 0:1], 1.0 / QMAX,
                                    None, ALU.mult))
            ch(vector.reciprocal(shv[:, 0:1], shv[:, 1:2]))
            ch(vector.tensor_scalar(swdt[:, :], swdt[:, :], shv[:, 1:2],
                                    None, ALU.mult))
        ev_md = sv.n

        if dbg == 2:
            dmp2 = {}
            for nm, tile_ in (("e_sxv", sxv), ("e_shv", shv), ("e_qsc", qsc),
                              ("e_qbi", qbi), ("e_swg", swgt), ("e_swu", swut),
                              ("e_swd", swdt), ("e_mgt", mgt), ("e_mut", mut),
                              ("e_xqT", xqT), ("e_lnm", lnmax),
                              ("e_ccr", ccrow), ("e_sgc", sgc),
                              ("e_hmx", hmax), ("e_m2", m2t), ("e_rs2", rs2t)):
                shp = [tile_.shape[0], tile_.shape[1]]
                dmp2[nm] = (nc.dram_tensor(nm, shp, tile_.dtype,
                                           kind="ExternalOutput").ap(), tile_)
            hdbg = nc.dram_tensor("e_h", [NI, T], F32, kind="ExternalOutput").ap()

            @block.sync
            def _(sync):
                sync.wait_ge(sv.h, sv.n)
                for nm, (ap_, tile_) in dmp2.items():
                    qs.dma(sync.dma_start(out=ap_, in_=tile_[:, :]))
                qs.dma(sync.dma_start(out=hdbg, in_=h_d))

        p2.close(); open_stacks.remove(p2)

        if stage <= 6:
            _finish()
            return nc

        p3 = ExitStack()
        open_stacks.append(p3)
        p3e = p3.enter_context
        hq = p3e(nc.sbuf_tensor("hq", [128, 2 * NIC * TG], BF16))
        hst_b = p3e(nc.sbuf_tensor("hst_b", [128, 2 * TG], F32))
        dsc = p3e(nc.sbuf_tensor("dsc", [128, NKC * TG], F32))
        x2b = p3e(nc.sbuf_tensor("x2b", [128, 2 * D], F32))
        outb = p3e(nc.sbuf_tensor("outb", [128, 2 * D], F32))
        ps_d = [p3e(nc.psum_tensor("ps_dA", [128, TG], F32)),
                p3e(nc.psum_tensor("ps_dB", [128, TG], F32))]
        ps_t2_single = p3e(nc.psum_tensor("ps_t2A", [128, 1024], F32))
        ps_t2 = [ps_t2_single, ps_t2_single]

        PH2_SS, PH2_SV, PH2_ST, PH2_SG = ss.n, sv.n, st.n, sg.n

        ev_hld = {}; ev_hq = {}; ev_hqs = {}; ev_dmm = {}; ev_dsc = {}
        ev_x2bld = {}; ev_dtr = {}; ev_oadd = {}; ev_ost = {}

        def emit_quant(tg):
            ptg = tg % 2
            for icc in range(NIC):
                pl = (tg * NIC + icc) % 2
                @block.sync
                def _(sync, tg=tg, icc=icc, pl=pl):
                    if tg == 0 and icc == 0:
                        sync.wait_ge(sv.h, ev_md)
                        sync.wait_ge(ss.h, PH2_SS)
                        sync.wait_ge(sg.h, PH2_SG)
                        sync.wait_ge(st.h, PH2_ST)
                        sync.wait_ge(qs.h, ev_hst[(NIC - 1, NTG - 1)])
                    k = tg * NIC + icc
                    if k >= 2:
                        pk = divmod(k - 2, NIC)
                        sync.wait_ge(sv.h, ev_hqs[(pk[0], pk[1])])
                    qs.dma(sync.dma_start(
                        out=hst_b[:, pl * TG:(pl + 1) * TG],
                        in_=h_ic[icc][:, tg * TG:(tg + 1) * TG]))
                ev_hld[(tg, icc)] = qs.n

                @block.scalar
                def _(scalar, tg=tg, icc=icc, pl=pl, ptg=ptg):
                    scalar.wait_ge(qs.h, ev_hld[(tg, icc)])
                    if tg >= 2 and icc == 0:
                        scalar.wait_ge(st.h, ev_dmm[(tg - 2, NKC - 1)])
                    ss.inc(scalar.activation(
                        hst_b[:, pl * TG:(pl + 1) * TG],
                        hst_b[:, pl * TG:(pl + 1) * TG],
                        AF.Identity, scale=shv[:, 0:1], bias=c23t[:, 0:1]))
                ev_hq1 = ss.n

                @block.vector
                def _(vector, tg=tg, icc=icc, pl=pl, ptg=ptg, ev=ev_hq1):
                    vector.wait_ge(ss.h, ev)
                    vector.tensor_scalar(
                        hst_b[:, pl * TG:(pl + 1) * TG],
                        hst_b[:, pl * TG:(pl + 1) * TG],
                        -TWO23, QMAX, ALU.add, ALU.min)
                    sv.inc(vector.tensor_scalar(
                        hq[:, (ptg * NIC + icc) * TG:(ptg * NIC + icc + 1) * TG],
                        hst_b[:, pl * TG:(pl + 1) * TG],
                        -QMAX - 1.0, None, ALU.max))
                ev_hq[(tg, icc)] = sv.n
                ev_hqs[(tg, icc)] = ev_hq[(tg, icc)]

        emit_quant(0)
        for tg in range(NTG):
            ptg = tg % 2
            if tg + 1 < NTG:
                emit_quant(tg + 1)

            for hc in range(NKC):
                pd = hc % 2
                @block.tensor
                def _(tensor, tg=tg, hc=hc, pd=pd, ptg=ptg):
                    if tg == 0 and hc == 0:
                        tensor.wait_ge(sv.h, ev_md)
                    tensor.wait_ge(sv.h, ev_hq[(tg, NIC - 1)])
                    pk = (tg, hc - 2) if hc >= 2 else \
                        ((tg - 1, NKC - 2 + hc) if tg > 0 else None)
                    if pk is not None and pk in ev_dsc:
                        tensor.wait_ge(sv.h, ev_dsc[pk])
                    for icc in range(NIC):
                        for half in range(2):
                            st.inc(tensor.matmul(
                                ps_d[pd][:, half * 512:(half + 1) * 512],
                                wdT[:, icc * 1024 + hc * 128:icc * 1024 + (hc + 1) * 128],
                                hq[:, (ptg * NIC + icc) * TG + half * 512:(ptg * NIC + icc) * TG + (half + 1) * 512],
                                start=(icc == 0), stop=(icc == NIC - 1)))
                ev_dmm[(tg, hc)] = st.n

                @block.vector
                def _(vector, tg=tg, hc=hc, pd=pd):
                    vector.wait_ge(st.h, ev_dmm[(tg, hc)])
                    if tg >= 1:
                        vector.wait_ge(st.h, ev_dtr[(tg - 1, 7)])
                    sv.inc(vector.tensor_scalar(
                        dsc[:, hc * TG:(hc + 1) * TG], ps_d[pd][:, :],
                        swdt[:, hc:hc + 1], None, ALU.mult))
                ev_dsc[(tg, hc)] = sv.n

            for tcg in range(NKC):
                pt2 = tcg % 2
                tc_abs = tg * NKC + tcg
                @block.sync
                def _(sync, tg=tg, tcg=tcg, tc_abs=tc_abs):
                    k = tc_abs
                    if k == 0:
                        sync.wait_ge(qs.h, ev_x2st[nt_run - 1])
                    if k >= 2:
                        sync.wait_ge(sv.h, ev_oadd[k - 2])
                    qs.dma(sync.dma_start(out=x2b[:, (k % 2) * D:(k % 2) * D + D],
                                          in_=x2_tc[tc_abs]))
                ev_x2bld[tc_abs] = qs.n

                @block.tensor
                def _(tensor, tg=tg, tcg=tcg, pt2=pt2):
                    tensor.wait_ge(sv.h, ev_dsc[(tg, NKC - 1)])
                    k = tg * NKC + tcg
                    if k >= 1:
                        tensor.wait_ge(sv.h, ev_oadd[k - 1])
                    for hc in range(NKC):
                        st.inc(tensor.transpose(
                            ps_t2[pt2][:, hc * 128:(hc + 1) * 128],
                            dsc[:, hc * TG + tcg * 128:hc * TG + (tcg + 1) * 128],
                            eyet[:, :]))
                ev_dtr[(tg, tcg)] = st.n

                @block.vector
                def _(vector, tg=tg, tcg=tcg, pt2=pt2, tc_abs=tc_abs):
                    vector.wait_ge(st.h, ev_dtr[(tg, tcg)])
                    vector.wait_ge(qs.h, ev_x2bld[tc_abs])
                    if tc_abs >= 2:
                        vector.wait_ge(qs.h, ev_ost[tc_abs - 2])
                    sv.inc(vector.tensor_tensor(
                        outb[:, (tc_abs % 2) * D:(tc_abs % 2) * D + D],
                        x2b[:, (tc_abs % 2) * D:(tc_abs % 2) * D + D],
                        ps_t2[pt2][:, :], ALU.add))
                ev_oadd[tc_abs] = sv.n

                @block.sync
                def _(sync, tc_abs=tc_abs):
                    sync.wait_ge(sv.h, ev_oadd[tc_abs])
                    qs.dma(sync.dma_start(out=y_tc[tc_abs],
                                          in_=outb[:, (tc_abs % 2) * D:(tc_abs % 2) * D + D]))
                ev_ost[tc_abs] = qs.n

        if dbg == 2:
            wdt_dbg = nc.dram_tensor("f_wdT", [128, NIC * 1024], BF16,
                                     kind="ExternalOutput").ap()
            hq_dbg = nc.dram_tensor("f_hq", [128, 2 * NIC * TG], BF16,
                                    kind="ExternalOutput").ap()
            dsc_dbg = nc.dram_tensor("f_dsc", [128, NKC * TG], F32,
                                     kind="ExternalOutput").ap()

            @block.sync
            def _(sync):
                sync.wait_ge(sv.h, sv.n)
                sync.wait_ge(st.h, st.n)
                qs.dma(sync.dma_start(out=wdt_dbg, in_=wdT[:, :]))
                qs.dma(sync.dma_start(out=hq_dbg, in_=hq[:, :]))
                qs.dma(sync.dma_start(out=dsc_dbg, in_=dsc[:, :]))

        @block.sync
        def _(sync):
            sync.wait_ge(qs.h, qs.n)
        @block.gpsimd
        def _(g):
            g.wait_ge(qg.h, qg.n)

        p3.close(); open_stacks.remove(p3)
        p2c.close(); open_stacks.remove(p2c)

    return nc


def kernel(L0_attn_x, L0_attn_norm_w, L0_attn_w_forget, L0_attn_w_input,
           L0_attn_w_output, L0_ffn_clip_x, L0_ffn_clip_down, L0_ffn_w_gate,
           L0_ffn_w_up, L0_ffn_w_down):
    nc = build_program(dbg=False)
    x = np.ascontiguousarray(np.asarray(L0_attn_x, dtype=np.float32))
    shared = {
        "wf": np.ascontiguousarray(np.asarray(L0_attn_w_forget, np.float32)),
        "wi": np.ascontiguousarray(np.asarray(L0_attn_w_input, np.float32)),
        "wo": np.ascontiguousarray(np.asarray(L0_attn_w_output, np.float32)),
        "nw": np.ascontiguousarray(np.asarray(L0_attn_norm_w, np.float32)),
        "cx": np.ascontiguousarray(np.asarray(L0_ffn_clip_x, np.float32)),
        "cd": np.ascontiguousarray(np.asarray(L0_ffn_clip_down, np.float32)),
        "wg": np.ascontiguousarray(np.asarray(L0_ffn_w_gate, np.float32)),
        "wu": np.ascontiguousarray(np.asarray(L0_ffn_w_up, np.float32)),
        "wd": np.ascontiguousarray(np.asarray(L0_ffn_w_down, np.float32)),
    }
    in_maps = [dict(shared, x=np.ascontiguousarray(x[c])) for c in range(8)]
    res = run_bass_kernel_spmd(nc, in_maps, core_ids=list(range(8)))
    return np.stack([np.asarray(res.results[c]["y"]) for c in range(8)], axis=0)


# version 05
# speedup vs baseline: 1.5004x; 1.0783x over previous
import numpy as np
import ml_dtypes
from contextlib import ExitStack
import concourse.bass as bass
import concourse.mybir as mybir
from concourse.bass_utils import run_bass_kernel_spmd

F32 = mybir.dt.float32
BF16 = mybir.dt.bfloat16
AF = mybir.ActivationFunctionType
ALU = mybir.AluOpType
AX = mybir.AxisListType.X

T = 4096
D = 1024
NH = 8
NI = 2816
NIC = NI // 128
NKC = D // 128
TILE = 512
NT = T // TILE
NTC = T // 128
TG = 1024
NTG = T // TG
TWO23 = 12582912.0
QMAX = 127.0
LN_EPS = 1e-5
RMS_EPS = 1e-6
FORGET_LB = 0.9


class Sem:
    def __init__(self, handle):
        self.h = handle
        self.n = 0

    def inc(self, instr, k=1):
        self.n += k
        instr.then_inc(self.h, k)
        return self.n

    def dma(self, instr):
        return self.inc(instr, 16)


class SlotDma:

    def __init__(self, nc, e, name, nslots):
        self.s = [Sem(e(nc.semaphore(f"{name}{i}"))) for i in range(nslots)]

    def dma(self, instr, slot):
        return self.s[slot].dma(instr)

    def wait(self, eng, slot, val):
        eng.wait_ge(self.s[slot].h, val)

    def wait_all(self, eng):
        for s_ in self.s:
            eng.wait_ge(s_.h, s_.n)


class Chain:

    def __init__(self, eng, sem):
        self.eng = eng
        self.sem = sem

    def __call__(self, instr):
        self.sem.inc(instr)
        self.eng.wait_ge(self.sem.h, self.sem.n)
        return instr


def build_program(dbg=False, stage=7, nt_run=NT):
    nc = bass.Bass("TRN2", target_bir_lowering=False, debug=False)

    x_d = nc.dram_tensor("x", [T, D], F32, kind="ExternalInput").ap()
    wf_d = nc.dram_tensor("wf", [D, D], F32, kind="ExternalInput").ap()
    wi_d = nc.dram_tensor("wi", [D, D], F32, kind="ExternalInput").ap()
    wo_d = nc.dram_tensor("wo", [D, D], F32, kind="ExternalInput").ap()
    nw_d = nc.dram_tensor("nw", [D], F32, kind="ExternalInput").ap()
    cx_d = nc.dram_tensor("cx", [1], F32, kind="ExternalInput").ap()
    cd_d = nc.dram_tensor("cd", [1], F32, kind="ExternalInput").ap()
    wg_d = nc.dram_tensor("wg", [NI, D], F32, kind="ExternalInput").ap()
    wu_d = nc.dram_tensor("wu", [NI, D], F32, kind="ExternalInput").ap()
    wd_d = nc.dram_tensor("wd", [D, NI], F32, kind="ExternalInput").ap()
    y_d = nc.dram_tensor("y", [T, D], F32, kind="ExternalOutput").ap()
    if dbg:
        y2_d = nc.dram_tensor("y2", [T, D], F32, kind="ExternalOutput").ap()
        y2_t4 = y2_d.rearrange("(n c p) d -> n p c d", n=NT, c=4, p=128)

    x2_d = nc.dram_tensor("x2buf", [T, D], F32).ap()
    h_d = nc.dram_tensor("hbuf", [NI, T], F32).ap()
    ccxi_d = nc.dram_tensor("ccxi", [1, 128], F32)
    ccxo_d = nc.dram_tensor("ccxo", [1, 128], F32)
    cchi_d = nc.dram_tensor("cchi", [1, 128], F32)
    ccho_d = nc.dram_tensor("ccho", [1, 128], F32)

    x_t4 = x_d.rearrange("(n c p) d -> n p c d", n=NT, c=4, p=128)
    x2_t4 = x2_d.rearrange("(n c p) d -> n p c d", n=NT, c=4, p=128)
    x2_tc = x2_d.rearrange("(n p) d -> n p d", n=NTC, p=128)
    y_tc = y_d.rearrange("(n p) d -> n p d", n=NTC, p=128)
    h_ic = h_d.rearrange("(a p) t -> a p t", a=NIC, p=128)

    eye32 = nc.inline_tensor(np.eye(128, dtype=np.float32), "ceye32").ap()
    eyeb = nc.inline_tensor(np.eye(128, dtype=ml_dtypes.bfloat16), "ceyeb").ap()
    hones_np = np.zeros((128, NH * NH), dtype=np.float32)
    for hh in range(NH):
        hones_np[:, hh * NH + hh] = 1.0
    hones = nc.inline_tensor(hones_np, "chones").ap()
    selb_np = np.zeros((NH, NH * 128), dtype=np.float32)
    for hh in range(NH):
        selb_np[hh, hh * 128:(hh + 1) * 128] = 1.0
    selb = nc.inline_tensor(selb_np, "cselb").ap()
    ones1 = nc.inline_tensor(np.ones((1, 128), dtype=np.float32), "cones1").ap()

    es = ExitStack()
    with es:
        e = es.enter_context

        qs = Sem(e(nc.semaphore("qs")))
        qg = Sem(e(nc.semaphore("qg")))
        sv = Sem(e(nc.semaphore("sv")))
        ss = Sem(e(nc.semaphore("ss")))
        st = Sem(e(nc.semaphore("st")))
        sg = Sem(e(nc.semaphore("sg")))
        scc = Sem(e(nc.semaphore("scc")))

        dwt = SlotDma(nc, e, "dwt", 2)
        dxl = SlotDma(nc, e, "dxl", 2)
        dx2s = SlotDma(nc, e, "dx2s", 1)
        dqx = SlotDma(nc, e, "dqx", 2)
        dwd = SlotDma(nc, e, "dwd", 1)
        dwg = SlotDma(nc, e, "dwg", 2)
        dhs = SlotDma(nc, e, "dhs", 2)
        dhl = SlotDma(nc, e, "dhl", 3)
        dxb = SlotDma(nc, e, "dxb", 2)
        dys = SlotDma(nc, e, "dys", 2)
        all_slotdma = [dwt, dxl, dx2s, dqx, dwd, dwg, dhs, dhl, dxb, dys]

        eyet = e(nc.sbuf_tensor("eyet", [128, 128], F32))
        eyebt = e(nc.sbuf_tensor("eyebt", [128, 128], BF16))
        honet = e(nc.sbuf_tensor("honet", [128, NH * NH], F32))
        selbt = e(nc.sbuf_tensor("selbt", [NH, NH * 128], F32))
        ones1t = e(nc.sbuf_tensor("ones1t", [1, 128], F32))
        nwt = e(nc.sbuf_tensor("nwt", [128, NKC], F32))
        m2t = e(nc.sbuf_tensor("m2t", [128, NTC], F32))
        rs2t = e(nc.sbuf_tensor("rs2t", [128, NTC], F32))
        qsc = e(nc.sbuf_tensor("qsc", [128, NTC], F32))
        qbi = e(nc.sbuf_tensor("qbi", [128, NTC], F32))
        carry = e(nc.sbuf_tensor("carry", [128, NH], F32))
        lnmax = e(nc.sbuf_tensor("lnmax", [128, 1], F32))
        lnmc = e(nc.sbuf_tensor("lnmc", [128, NT], F32))
        hmax = e(nc.sbuf_tensor("hmax", [128, 1], F32))
        hmaxc = e(nc.sbuf_tensor("hmaxc", [128, NIC * NTG], F32))
        sxv = e(nc.sbuf_tensor("sxv", [128, 4], F32))
        shv = e(nc.sbuf_tensor("shv", [128, 4], F32))
        sgc = e(nc.sbuf_tensor("sgc", [1, 8], F32))
        ccrow = e(nc.sbuf_tensor("ccrow", [1, 128], F32))
        swgt = e(nc.sbuf_tensor("swgt", [128, NIC], F32))
        swut = e(nc.sbuf_tensor("swut", [128, NIC], F32))
        swdt = e(nc.sbuf_tensor("swdt", [128, NKC], F32))
        stat2 = e(nc.sbuf_tensor("stat2", [128, 10 * 128], F32))
        statS = e(nc.sbuf_tensor("statS", [128, 256], F32))

        def S2(r):
            return stat2[:, r * 128:(r + 1) * 128]

        def S2c(r, c0, c1=None):
            c1 = c0 + 1 if c1 is None else c1
            return stat2[:, r * 128 + c0:r * 128 + c1]

        R_MEAN, R_VARP, R_A2, R_R0, R_WRK, R_M2, R_W2, R_SQ2, R_AMAX, R_TMP = range(10)
        WG_AM, WU_AM, WG_R, WU_R, WG_SW, WU_SW, WD_AM, WD_R, WD_SW, WG_MU = range(10)
        c23t = e(nc.sbuf_tensor("c23t", [128, 1], F32))
        open_stacks = []
        p1w = ExitStack()
        open_stacks.append(p1w)
        p1we = p1w.enter_context
        x_t = p1we(nc.sbuf_tensor("x_t", [128, 2 * 4096], F32))
        wfThi = p1we(nc.sbuf_tensor("wfThi", [128, NKC * D], BF16))
        wfTlo = p1we(nc.sbuf_tensor("wfTlo", [128, NKC * D], BF16))
        woThi = p1we(nc.sbuf_tensor("woThi", [128, NKC * D], BF16))
        woTlo = p1we(nc.sbuf_tensor("woTlo", [128, NKC * D], BF16))
        uThi = p1we(nc.sbuf_tensor("uThi", [128, NKC * NH], BF16))
        uTlo = p1we(nc.sbuf_tensor("uTlo", [128, NKC * NH], BF16))

        def x_sl(p):
            return x_t[:, p * 4096:(p + 1) * 4096]

        block = e(nc.Block())

        pro = ExitStack()
        pe_ = pro.enter_context
        wtile = pe_(nc.sbuf_tensor("wtile", [128, 2 * D], F32))
        u_sb = pe_(nc.sbuf_tensor("u_sb", [NH, D], F32))
        ps_w = pe_(nc.psum_tensor("ps_w", [128, 1024], F32))
        ps_u = pe_(nc.psum_tensor("ps_u", [NH, 1024], F32))

        @block.sync
        def _(sync):
            for dst, src in ((eyet, eye32), (eyebt, eyeb), (honet, hones),
                             (selbt, selb), (ones1t, ones1)):
                qs.dma(sync.dma_start(out=dst[:, :], in_=src))
            qs.dma(sync.dma_start(out=nwt[:, :],
                                  in_=nw_d.rearrange("(c p) -> p c", p=128),
                                  allow_slow_non_contiguous=True))
            qs.dma(sync.dma_start(out=sgc[0:1, 0:1], in_=cx_d.rearrange("(a o) -> a o", a=1)))
            qs.dma(sync.dma_start(out=sgc[0:1, 1:2], in_=cd_d.rearrange("(a o) -> a o", a=1)))
        ev_const = qs.n

        ev_xload = []
        @block.sync
        def _(sync):
            dxl.dma(sync.dma_start(out=x_sl(0), in_=x_t4[0]), 0)
        ev_xload.append(dxl.s[0].n)
        @block.sync
        def _(sync):
            dxl.dma(sync.dma_start(out=x_sl(1), in_=x_t4[1]), 1)
        ev_xload.append(dxl.s[1].n)

        ev_wi = []
        ev_umm_h = []
        for hh in range(NH):
            @block.sync
            def _(sync, hh=hh):
                if hh >= 2:
                    sync.wait_ge(st.h, ev_umm_h[hh - 2])
                dwt.dma(sync.dma_start(
                    out=wtile[:, (hh % 2) * D:(hh % 2) * D + D],
                    in_=wi_d[hh * 128:(hh + 1) * 128, :]), hh % 2)
            ev_wi.append(dwt.s[hh % 2].n)

            @block.tensor
            def _(tensor, hh=hh):
                if hh == 0:
                    tensor.wait_ge(qs.h, ev_const)
                dwt.wait(tensor, hh % 2, ev_wi[hh])
                for half in range(2):
                    st.inc(tensor.matmul(
                        ps_u[:, half * 512:(half + 1) * 512],
                        honet[:, hh * NH:(hh + 1) * NH],
                        wtile[:, (hh % 2) * D + half * 512:(hh % 2) * D + (half + 1) * 512],
                        start=(hh == 0), stop=(hh == NH - 1)))
            ev_umm_h.append(st.n)
        ev_umm = st.n

        @block.vector
        def _(vector):
            vector.wait_ge(st.h, ev_umm)
            sv.inc(vector.tensor_copy(u_sb[:, :], ps_u[:, :]))
        ev_usb = sv.n

        @block.tensor
        def _(tensor):
            tensor.wait_ge(sv.h, ev_usb)
            for kc in range(NKC):
                st.inc(tensor.transpose(ps_w[:, kc * NH:(kc + 1) * NH],
                                        u_sb[:, kc * 128:(kc + 1) * 128],
                                        eyet[0:NH, 0:NH]))
        ev_ut = st.n

        @block.vector
        def _(vector):
            ch = Chain(vector, sv)
            vector.wait_ge(st.h, ev_ut)
            for kc in range(NKC):
                ch(vector.tensor_scalar(
                    uThi[:, kc * NH:(kc + 1) * NH], ps_w[:, kc * NH:(kc + 1) * NH],
                    nwt[:, kc:kc + 1], None, ALU.mult))
                ch(vector.scalar_tensor_tensor(
                    uTlo[:, kc * NH:(kc + 1) * NH], ps_w[:, kc * NH:(kc + 1) * NH],
                    nwt[:, kc:kc + 1], uThi[:, kc * NH:(kc + 1) * NH],
                    ALU.mult, ALU.subtract))
        ev_utsplit = sv.n

        def prep_weight(w_dram, whi, wlo, init_v, init_s, init_tr):
            ev_ld = []
            evac_v = [init_v]
            evac_s = [init_s]
            ev_tr_all = {-2: init_tr[0], -1: init_tr[1]}
            for oc in range(NKC):
                @block.sync
                def _(sync, oc=oc):
                    sync.wait_ge(st.h, ev_tr_all[oc - 2])
                    dwt.dma(sync.dma_start(
                        out=wtile[:, (oc % 2) * D:(oc % 2) * D + D],
                        in_=w_dram[oc * 128:(oc + 1) * 128, :]), oc % 2)
                ev_ld.append(dwt.s[oc % 2].n)
                ev_tr = []
                @block.tensor
                def _(tensor, oc=oc):
                    dwt.wait(tensor, oc % 2, ev_ld[oc])
                    tensor.wait_ge(sv.h, evac_v[-1])
                    tensor.wait_ge(ss.h, evac_s[-1])
                    for kc in range(NKC):
                        ev_tr.append(st.inc(tensor.transpose(
                            ps_w[:, kc * 128:(kc + 1) * 128],
                            wtile[:, (oc % 2) * D + kc * 128:(oc % 2) * D + (kc + 1) * 128],
                            eyet[:, :])))
                ev_tr_all[oc] = st.n
                ev_hi = []
                @block.scalar
                def _(scalar, oc=oc):
                    for kc in range(NKC):
                        scalar.wait_ge(st.h, ev_tr[3 if kc < 4 else 7])
                        ev_hi.append(ss.inc(scalar.activation(
                            whi[:, kc * D + oc * 128:kc * D + (oc + 1) * 128],
                            ps_w[:, kc * 128:(kc + 1) * 128],
                            AF.Identity, scale=nwt[:, kc:kc + 1])))
                @block.vector
                def _(vector, oc=oc):
                    for kc in range(NKC):
                        vector.wait_ge(ss.h, ev_hi[3 if kc < 4 else 7])
                        sv.inc(vector.scalar_tensor_tensor(
                            wlo[:, kc * D + oc * 128:kc * D + (oc + 1) * 128],
                            ps_w[:, kc * 128:(kc + 1) * 128],
                            nwt[:, kc:kc + 1],
                            whi[:, kc * D + oc * 128:kc * D + (oc + 1) * 128],
                            ALU.mult, ALU.subtract))
                evac_v.append(sv.n)
                evac_s.append(ss.n)
            return evac_v[-1], evac_s[-1], (ev_tr_all[NKC - 2], ev_tr_all[NKC - 1])

        wf_v, wf_s, wf_tr = prep_weight(wf_d, wfThi, wfTlo, ev_utsplit, ss.n,
                                        (ev_umm, ev_umm))
        ev_pro_v, ev_pro_s, _ = prep_weight(wo_d, woThi, woTlo, wf_v, wf_s, wf_tr)
        pro.close()

        def _finish():
            for stk in reversed(open_stacks):
                stk.close()
            @block.sync
            def _(sync):
                sync.wait_ge(qs.h, qs.n)
                for d_ in all_slotdma:
                    d_.wait_all(sync)
                sync.wait_ge(sv.h, sv.n)
                sync.wait_ge(ss.h, ss.n)
                sync.wait_ge(st.h, st.n)
            @block.gpsimd
            def _(g):
                g.wait_ge(qg.h, qg.n)
                g.wait_ge(sg.h, sg.n)

        if stage <= 0:
            _finish()
            return nc

        p1 = ExitStack()
        open_stacks.append(p1)
        p1e = p1.enter_context
        z_t = p1e(nc.sbuf_tensor("z_t", [128, 2 * 1024], F32))
        r_t = p1e(nc.sbuf_tensor("r_t", [128, 4096], F32))
        x2_t = p1e(nc.sbuf_tensor("x2_t", [128, 4096], F32))
        zThi = p1e(nc.sbuf_tensor("zThi", [128, NKC * TILE], BF16))
        zTlo = p1e(nc.sbuf_tensor("zTlo", [128, NKC * TILE], BF16))
        y_tile = p1e(nc.sbuf_tensor("y_tile", [128, 4096], F32))
        f_b = p1e(nc.sbuf_tensor("f_b", [128, 2 * TILE], F32))
        i_b = p1e(nc.sbuf_tensor("i_b", [128, 2 * TILE], F32))
        o_b = p1e(nc.sbuf_tensor("o_b", [128, 2 * TILE], F32))
        g_b = p1e(nc.sbuf_tensor("g_b", [128, 2 * TILE], F32))
        v_sb = p1e(nc.sbuf_tensor("v_sb", [NH, TILE], F32))
        tmp_t = p1e(nc.sbuf_tensor("tmp_t", [128, 1024], F32))
        trash = p1e(nc.sbuf_tensor("trash", [128, 1024], F32))
        ps_rot = [p1e(nc.psum_tensor("ps_rotA", [128, 1024], F32)),
                  p1e(nc.psum_tensor("ps_rotB", [128, 1024], F32))]
        ps_f = p1e(nc.psum_tensor("ps_f", [128, TILE], F32))
        ps_o = p1e(nc.psum_tensor("ps_o", [128, TILE], F32))
        ps_v = p1e(nc.psum_tensor("ps_v", [NH, TILE], F32))
        ps_vb = p1e(nc.psum_tensor("ps_vb", [128, TILE], F32))

        @block.vector
        def _(vector):
            vector.wait_ge(qs.h, ev_const)
            vector.memset(carry[:, :], 0.0)
            vector.memset(lnmax[:, :], 0.0)
            vector.memset(lnmc[:, :], 0.0)
            vector.memset(hmaxc[:, :], 0.0)
            vector.memset(c23t[:, :], TWO23)
            sv.inc(vector.memset(hmax[:, :], 0.0))

        rot_guard = [None, None]
        rot_idx = [0]

        def rot_next():
            s_ = rot_idx[0] % 2
            rot_idx[0] += 1
            return s_


        ev_z = {}; ev_r = {}; ev_zt = {}; ev_zhi = {}; ev_zlo = {}
        ev_vsb = {}; ev_fsig = {}; ev_osig = {}; ev_g = {}; ev_ymul = {}
        ev_ytile = {}; ev_yt = {}; ev_x2add = {}; ev_x2st = {}
        ev_sq2 = {}; ev_amax4 = {}; ev_cxn = {}; ev_xdone = {}; ev_rs2 = {}
        ev_fproj = {}; ev_oproj = {}; ev_vb = {}; ev_vmm = {}

        for i in range(nt_run + 1):
            par = i % 2
            la = i - 1

            if i + 1 < nt_run:
                @block.sync
                def _(sync, i=i):
                    if i - 1 >= 0:
                        sync.wait_ge(ss.h, ev_xdone[i - 1])
                    dxl.dma(sync.dma_start(out=x_sl((i + 1) % 2),
                                           in_=x_t4[i + 1]), (i + 1) % 2)
                ev_xload.append(dxl.s[(i + 1) % 2].n)

            if la >= 0:
                for tc in range(4):
                    slot = rot_next()
                    @block.tensor
                    def _(tensor, la=la, tc=tc, slot=slot):
                        tensor.wait_ge(sv.h, ev_ytile[la])
                        if rot_guard[slot] is not None:
                            kind, val = rot_guard[slot]
                            tensor.wait_ge(kind.h, val)
                        for oc in range(NH):
                            st.inc(tensor.transpose(
                                ps_rot[slot][:, oc * 128:(oc + 1) * 128],
                                y_tile[:, oc * TILE + tc * 128:oc * TILE + (tc + 1) * 128],
                                eyet[:, :]))
                    ev_yt[(la, tc)] = st.n

                    @block.vector
                    def _(vector, la=la, tc=tc, slot=slot):
                        vector.wait_ge(st.h, ev_yt[(la, tc)])
                        vector.wait_ge(ss.h, ev_r[la])
                        if la >= 1 and tc == 0:
                            vector.wait_ge(ss.h, ev_sq2[la - 1])
                            dx2s.wait(vector, 0, ev_x2st[la - 1])
                        ev_x2add[(la, tc)] = sv.inc(vector.tensor_tensor(
                            x2_t[:, tc * 1024:(tc + 1) * 1024],
                            r_t[:, tc * 1024:(tc + 1) * 1024],
                            ps_rot[slot][:, :], ALU.add))
                    rot_guard[slot] = (sv, ev_x2add[(la, tc)])

                @block.vector
                def _(vector, la=la):
                    ch = Chain(vector, sv)
                    ch(vector.tensor_reduce(
                        S2c(R_M2, 0, 4),
                        x2_t[:, :].rearrange("p (c d) -> p c d", c=4, d=1024),
                        AX, ALU.add))
                    vector.tensor_scalar(S2(R_M2), S2(R_M2), 1.0 / D, None,
                                         ALU.mult)
                    vector.tensor_copy(m2t[:, la * 4:(la + 1) * 4],
                                       S2c(R_M2, 0, 4))
                    for tc in range(4):
                        vector.tensor_scalar(
                            tmp_t[:, :], x2_t[:, tc * 1024:(tc + 1) * 1024],
                            S2c(R_M2, tc), None, ALU.subtract)
                        last = vector.tensor_reduce(
                            S2c(R_AMAX, tc), tmp_t[:, :], AX, ALU.max,
                            apply_absolute_value=True)
                    sv.inc(last)
                ev_amax4[la] = sv.n

                @block.scalar
                def _(scalar, la=la):
                    scalar.wait_ge(sv.h, ev_x2add[(la, 3)])
                    if la >= 1:
                        scalar.wait_ge(sv.h, ev_rs2[la - 1])
                    scalar.memzero(statS[:, 128:132])
                    for tc in range(4):
                        last = scalar.activation(
                            trash[:, :], x2_t[:, tc * 1024:(tc + 1) * 1024],
                            AF.Square, accum_out=statS[:, 128 + tc:129 + tc])
                    ss.inc(last)
                ev_sq2[la] = ss.n

            if i < nt_run:
                @block.scalar
                def _(scalar, i=i, par=par):
                    dxl.wait(scalar, i % 2, ev_xload[i])
                    if i >= 1:
                        scalar.wait_ge(sv.h, ev_cxn[i - 1])
                    scalar.memzero(statS[:, 0:4])
                    for tc in range(4):
                        last = scalar.activation(
                            trash[:, :], x_sl(par)[:, tc * 1024:(tc + 1) * 1024],
                            AF.Square, accum_out=statS[:, tc:tc + 1])
                    ss.inc(last)
                ev_sqx = ss.n

                @block.vector
                def _(vector, i=i, par=par, la=la):
                    ch = Chain(vector, sv)
                    dxl.wait(vector, i % 2, ev_xload[i])
                    ch(vector.tensor_reduce(
                        S2c(R_MEAN, 0, 4),
                        x_sl(par)[:, :].rearrange("p (c d) -> p c d", c=4, d=1024),
                        AX, ALU.add))
                    vector.tensor_scalar(S2(R_MEAN), S2(R_MEAN),
                                         1.0 / D, None, ALU.mult)
                    vector.wait_ge(ss.h, ev_sqx)
                    vector.tensor_scalar(S2(R_VARP), statS[:, 0:128],
                                         1.0 / D, None, ALU.mult)
                    vector.tensor_tensor(S2(R_W2), S2(R_MEAN),
                                         S2(R_MEAN), ALU.mult)
                    vector.tensor_tensor(S2(R_VARP), S2(R_VARP),
                                         S2(R_W2), ALU.subtract)
                    vector.tensor_scalar(S2(R_VARP), S2(R_VARP),
                                         LN_EPS, None, ALU.add)
                    vector.tensor_scalar(S2(R_A2), S2(R_VARP),
                                         1.0 + RMS_EPS, -LN_EPS, ALU.mult, ALU.add)
                    vector.reciprocal(S2(R_R0), S2(R_VARP))
                    last = vector.reciprocal(S2(R_WRK), S2(R_A2))
                    if la >= 0:
                        vector.wait_ge(ss.h, ev_sq2[la])
                        vector.tensor_scalar(S2(R_SQ2), statS[:, 128:256],
                                             1.0 / D, None, ALU.mult)
                        vector.tensor_tensor(S2(R_W2), S2(R_M2),
                                             S2(R_M2), ALU.mult)
                        vector.tensor_tensor(S2(R_SQ2), S2(R_SQ2),
                                             S2(R_W2), ALU.subtract)
                        vector.tensor_scalar(S2(R_SQ2), S2(R_SQ2),
                                             LN_EPS, None, ALU.add)
                        last = vector.reciprocal(S2(R_TMP), S2(R_SQ2))
                    sv.inc(last)
                ev_recip = sv.n

                @block.scalar
                def _(scalar, la=la):
                    scalar.wait_ge(sv.h, ev_recip)
                    scalar.activation(S2(R_R0), S2(R_R0), AF.Sqrt)
                    last = scalar.activation(S2(R_WRK), S2(R_WRK), AF.Sqrt)
                    if la >= 0:
                        last = scalar.activation(S2(R_TMP), S2(R_TMP), AF.Sqrt)
                    ss.inc(last)
                ev_sqrt = ss.n

                @block.vector
                def _(vector, i=i, la=la):
                    ch = Chain(vector, sv)
                    vector.wait_ge(ss.h, ev_sqrt)
                    if la >= 0:
                        vector.tensor_tensor(S2(R_W2), S2(R_TMP),
                                             S2(R_TMP), ALU.mult)
                        vector.tensor_tensor(S2(R_W2), S2(R_W2),
                                             S2(R_SQ2), ALU.mult)
                        vector.tensor_scalar(S2(R_W2), S2(R_W2),
                                             -0.5, 1.5, ALU.mult, ALU.add)
                        vector.tensor_tensor(S2(R_TMP), S2(R_TMP),
                                             S2(R_W2), ALU.mult)
                        vector.tensor_copy(rs2t[:, la * 4:(la + 1) * 4],
                                           S2c(R_TMP, 0, 4))
                        vector.wait_ge(sv.h, ev_amax4[la])
                        vector.tensor_tensor(S2(R_AMAX), S2(R_AMAX),
                                             S2(R_TMP), ALU.mult)
                        sv.inc(vector.tensor_reduce(lnmc[:, la:la + 1],
                                                    S2c(R_AMAX, 0, 4),
                                                    AX, ALU.max))
                        ev_rs2[la] = sv.n
                    vector.tensor_tensor(S2(R_W2), S2(R_R0),
                                         S2(R_R0), ALU.mult)
                    vector.tensor_tensor(S2(R_W2), S2(R_W2),
                                         S2(R_VARP), ALU.mult)
                    vector.tensor_scalar(S2(R_W2), S2(R_W2),
                                         -0.5, 1.5, ALU.mult, ALU.add)
                    vector.tensor_tensor(S2(R_R0), S2(R_R0),
                                         S2(R_W2), ALU.mult)
                    vector.tensor_tensor(S2(R_W2), S2(R_WRK),
                                         S2(R_WRK), ALU.mult)
                    vector.tensor_tensor(S2(R_W2), S2(R_W2),
                                         S2(R_A2), ALU.mult)
                    vector.tensor_scalar(S2(R_W2), S2(R_W2),
                                         -0.5, 1.5, ALU.mult, ALU.add)
                    vector.tensor_tensor(S2(R_WRK), S2(R_WRK),
                                         S2(R_W2), ALU.mult)
                    vector.tensor_tensor(S2(R_A2), S2(R_MEAN),
                                         S2(R_WRK), ALU.mult)
                    vector.tensor_scalar(S2(R_A2), S2(R_A2),
                                         -1.0, None, ALU.mult)
                    vector.tensor_scalar(S2(R_VARP), S2(R_R0),
                                         1.0, None, ALU.add)
                    vector.tensor_tensor(S2(R_W2), S2(R_MEAN),
                                         S2(R_R0), ALU.mult)
                    sv.inc(vector.tensor_scalar(S2(R_W2), S2(R_W2),
                                                -1.0, None, ALU.mult))
                ev_cxn[i] = sv.n

                for tc in range(4):
                    slot = rot_next()
                    @block.scalar
                    def _(scalar, i=i, tc=tc, par=par):
                        scalar.wait_ge(sv.h, ev_cxn[i])
                        zt_prev = ev_zt.get((i, tc - 2)) if tc >= 2 else \
                            (ev_zt.get((i - 1, tc + 2)) if i >= 1 else None)
                        if zt_prev is not None:
                            scalar.wait_ge(st.h, zt_prev)
                        if ev_x2add.get((i - 1, tc)) is not None:
                            scalar.wait_ge(sv.h, ev_x2add[(i - 1, tc)])
                        ss.inc(scalar.activation(
                            z_t[:, (tc % 2) * 1024:(tc % 2) * 1024 + 1024],
                            x_sl(par)[:, tc * 1024:(tc + 1) * 1024],
                            AF.Identity, scale=S2c(R_WRK, tc),
                            bias=S2c(R_A2, tc)))
                        ev_z[(i, tc)] = ss.n
                        ss.inc(scalar.activation(
                            r_t[:, tc * 1024:(tc + 1) * 1024],
                            x_sl(par)[:, tc * 1024:(tc + 1) * 1024],
                            AF.Identity, scale=S2c(R_VARP, tc),
                            bias=S2c(R_W2, tc)))
                        if tc == 3:
                            ev_r[i] = ss.n
                            ev_xdone[i] = ss.n

                    @block.tensor
                    def _(tensor, i=i, tc=tc, slot=slot):
                        if i == 0 and tc == 0:
                            tensor.wait_ge(sv.h, ev_pro_v)
                            tensor.wait_ge(ss.h, ev_pro_s)
                            tensor.wait_ge(sv.h, ev_utsplit)
                        tensor.wait_ge(ss.h, ev_z[(i, tc)])
                        if rot_guard[slot] is not None:
                            kind, val = rot_guard[slot]
                            tensor.wait_ge(kind.h, val)
                        for kc in range(NKC):
                            st.inc(tensor.transpose(
                                ps_rot[slot][:, kc * 128:(kc + 1) * 128],
                                z_t[:, (tc % 2) * 1024 + kc * 128:(tc % 2) * 1024 + (kc + 1) * 128],
                                eyet[:, :]))
                    ev_zt[(i, tc)] = st.n

                    @block.scalar
                    def _(scalar, i=i, tc=tc, slot=slot):
                        scalar.wait_ge(st.h, ev_zt[(i, tc)])
                        if i >= 1 and tc == 0:
                            scalar.wait_ge(st.h, ev_oproj[(i - 1, NH - 1)])
                        ss.inc(scalar.copy(
                            zThi[:, :].rearrange("p (k t) -> p k t", k=NKC, t=TILE)[:, :, tc * 128:(tc + 1) * 128],
                            ps_rot[slot][:, :].rearrange("p (k b) -> p k b", k=NKC, b=128)))
                    ev_zhi[(i, tc)] = ss.n

                    @block.vector
                    def _(vector, i=i, tc=tc, slot=slot):
                        vector.wait_ge(st.h, ev_zt[(i, tc)])
                        vector.wait_ge(ss.h, ev_zhi[(i, tc)])
                        if i >= 1 and tc == 0:
                            vector.wait_ge(st.h, ev_oproj[(i - 1, NH - 1)])
                        sv.inc(vector.tensor_tensor(
                            zTlo[:, :].rearrange("p (k t) -> p k t", k=NKC, t=TILE)[:, :, tc * 128:(tc + 1) * 128],
                            ps_rot[slot][:, :].rearrange("p (k b) -> p k b", k=NKC, b=128),
                            zThi[:, :].rearrange("p (k t) -> p k t", k=NKC, t=TILE)[:, :, tc * 128:(tc + 1) * 128],
                            ALU.subtract))
                    ev_zlo[(i, tc)] = sv.n
                    rot_guard[slot] = (sv, ev_zlo[(i, tc)])

                    @block.tensor
                    def _(tensor, i=i, tc=tc):
                        tensor.wait_ge(ss.h, ev_zhi[(i, tc)])
                        tensor.wait_ge(sv.h, ev_zlo[(i, tc)])
                        if tc == 0 and i > 0:
                            tensor.wait_ge(sv.h, ev_vsb[i - 1])
                        n = 0
                        for kc in range(NKC):
                            zh = zThi[:, kc * TILE + tc * 128:kc * TILE + (tc + 1) * 128]
                            zl = zTlo[:, kc * TILE + tc * 128:kc * TILE + (tc + 1) * 128]
                            uh = uThi[:, kc * NH:(kc + 1) * NH]
                            ul = uTlo[:, kc * NH:(kc + 1) * NH]
                            for (wop, zop) in ((uh, zh), (ul, zh), (uh, zl)):
                                st.inc(tensor.matmul(
                                    ps_v[:, tc * 128:(tc + 1) * 128], wop, zop,
                                    start=(n == 0), stop=(n == 3 * NKC - 1)))
                                n += 1
                    ev_vmm[(i, tc)] = st.n

                @block.vector
                def _(vector, i=i):
                    vector.wait_ge(st.h, ev_vmm[(i, 3)])
                    sv.inc(vector.tensor_copy(v_sb[:, :], ps_v[:, :]))
                ev_vsb[i] = sv.n

                for oc in range(NH):
                    op = oc % 2

                    @block.tensor
                    def _(tensor, i=i, oc=oc):
                        prev = ev_fsig.get((i, oc - 1)) if oc > 0 else \
                            ev_fsig.get((i - 1, NH - 1))
                        if prev is not None:
                            tensor.wait_ge(ss.h, prev)
                        n = 0
                        for kc in range(NKC):
                            zh = zThi[:, kc * TILE:(kc + 1) * TILE]
                            zl = zTlo[:, kc * TILE:(kc + 1) * TILE]
                            wh = wfThi[:, kc * D + oc * 128:kc * D + (oc + 1) * 128]
                            wl = wfTlo[:, kc * D + oc * 128:kc * D + (oc + 1) * 128]
                            for (wop, zop) in ((wh, zh), (wl, zh), (wh, zl)):
                                st.inc(tensor.matmul(
                                    ps_f[:, :], wop, zop,
                                    start=(n == 0), stop=(n == 3 * NKC - 1)))
                                n += 1
                    ev_fproj[(i, oc)] = st.n

                    @block.scalar
                    def _(scalar, i=i, oc=oc, op=op):
                        scalar.wait_ge(st.h, ev_fproj[(i, oc)])
                        prevy = ev_ymul.get((i, oc - 2)) if oc >= 2 else \
                            (ev_ymul.get((i - 1, NH - 2 + oc)) if i >= 1 else None)
                        if prevy is not None:
                            scalar.wait_ge(sv.h, prevy)
                        ss.inc(scalar.activation(
                            f_b[:, op * TILE:(op + 1) * TILE], ps_f[:, :], AF.Sigmoid))
                    ev_fsig[(i, oc)] = ss.n

                    @block.tensor
                    def _(tensor, i=i, oc=oc):
                        prev = ev_osig.get((i, oc - 1)) if oc > 0 else \
                            ev_osig.get((i - 1, NH - 1))
                        if prev is not None:
                            tensor.wait_ge(ss.h, prev)
                        n = 0
                        for kc in range(NKC):
                            zh = zThi[:, kc * TILE:(kc + 1) * TILE]
                            zl = zTlo[:, kc * TILE:(kc + 1) * TILE]
                            wh = woThi[:, kc * D + oc * 128:kc * D + (oc + 1) * 128]
                            wl = woTlo[:, kc * D + oc * 128:kc * D + (oc + 1) * 128]
                            for (wop, zop) in ((wh, zh), (wl, zh), (wh, zl)):
                                st.inc(tensor.matmul(
                                    ps_o[:, :], wop, zop,
                                    start=(n == 0), stop=(n == 3 * NKC - 1)))
                                n += 1
                        ev_oproj[(i, oc)] = st.n
                        tensor.wait_ge(sv.h, ev_vsb[i])
                        prevg = ev_g.get((i, oc - 1)) if oc > 0 else \
                            ev_g.get((i - 1, NH - 1))
                        if prevg is not None:
                            tensor.wait_ge(sv.h, prevg)
                        st.inc(tensor.matmul(
                            ps_vb[:, :], selbt[:, oc * 128:(oc + 1) * 128],
                            v_sb[:, :], start=True, stop=True))
                    ev_vb[(i, oc)] = st.n

                    @block.scalar
                    def _(scalar, i=i, oc=oc, op=op):
                        scalar.wait_ge(st.h, ev_oproj[(i, oc)])
                        prevy = ev_ymul.get((i, oc - 2)) if oc >= 2 else \
                            (ev_ymul.get((i - 1, NH - 2 + oc)) if i >= 1 else None)
                        if prevy is not None:
                            scalar.wait_ge(sv.h, prevy)
                        ss.inc(scalar.activation(
                            o_b[:, op * TILE:(op + 1) * TILE], ps_o[:, :], AF.Sigmoid))
                    ev_osig[(i, oc)] = ss.n

                    @block.vector
                    def _(vector, i=i, oc=oc, op=op):
                        fsl = f_b[:, op * TILE:(op + 1) * TILE]
                        isl = i_b[:, op * TILE:(op + 1) * TILE]
                        gsl = g_b[:, op * TILE:(op + 1) * TILE]
                        osl = o_b[:, op * TILE:(op + 1) * TILE]
                        if oc == 0 and i >= 1 and (i - 1, 3) in ev_yt:
                            vector.wait_ge(st.h, ev_yt[(i - 1, 3)])
                        vector.wait_ge(ss.h, ev_fsig[(i, oc)])
                        vector.tensor_scalar(fsl, fsl, FORGET_LB, None, ALU.max)
                        vector.tensor_scalar(isl, fsl, -1.0, 1.0, ALU.mult, ALU.add)
                        vector.wait_ge(st.h, ev_vb[(i, oc)])
                        ev_g[(i, oc)] = sv.inc(vector.tensor_tensor(
                            gsl, isl, ps_vb[:, :], ALU.mult))
                        sv.inc(vector.tensor_tensor_scan(
                            fsl, fsl, gsl, carry[:, oc:oc + 1], ALU.mult, ALU.add))
                        vector.wait_ge(sv.h, sv.n)
                        vector.tensor_copy(carry[:, oc:oc + 1],
                                           fsl[:, TILE - 1:TILE])
                        vector.wait_ge(ss.h, ev_osig[(i, oc)])
                        sv.inc(vector.tensor_tensor(
                            y_tile[:, oc * TILE:(oc + 1) * TILE], osl, fsl,
                            ALU.mult))
                    ev_ymul[(i, oc)] = sv.n
                ev_ytile[i] = ev_ymul[(i, NH - 1)]

            if i == nt_run and la >= 0:
                @block.vector
                def _(vector, la=la):
                    vector.wait_ge(ss.h, ev_sq2[la])
                    vector.tensor_scalar(S2(R_SQ2), statS[:, 128:256],
                                         1.0 / D, None, ALU.mult)
                    vector.tensor_tensor(S2(R_W2), S2(R_M2),
                                         S2(R_M2), ALU.mult)
                    vector.tensor_tensor(S2(R_SQ2), S2(R_SQ2),
                                         S2(R_W2), ALU.subtract)
                    vector.tensor_scalar(S2(R_SQ2), S2(R_SQ2),
                                         LN_EPS, None, ALU.add)
                    sv.inc(vector.reciprocal(S2(R_TMP), S2(R_SQ2)))
                ev_rc2 = sv.n

                @block.scalar
                def _(scalar):
                    scalar.wait_ge(sv.h, ev_rc2)
                    ss.inc(scalar.activation(S2(R_TMP), S2(R_TMP), AF.Sqrt))
                ev_sq2s = ss.n

                @block.vector
                def _(vector, la=la):
                    ch = Chain(vector, sv)
                    vector.wait_ge(ss.h, ev_sq2s)
                    vector.tensor_tensor(S2(R_W2), S2(R_TMP),
                                         S2(R_TMP), ALU.mult)
                    vector.tensor_tensor(S2(R_W2), S2(R_W2),
                                         S2(R_SQ2), ALU.mult)
                    vector.tensor_scalar(S2(R_W2), S2(R_W2),
                                         -0.5, 1.5, ALU.mult, ALU.add)
                    vector.tensor_tensor(S2(R_TMP), S2(R_TMP),
                                         S2(R_W2), ALU.mult)
                    vector.tensor_copy(rs2t[:, la * 4:(la + 1) * 4],
                                       S2c(R_TMP, 0, 4))
                    vector.wait_ge(sv.h, ev_amax4[la])
                    vector.tensor_tensor(S2(R_AMAX), S2(R_AMAX),
                                         S2(R_TMP), ALU.mult)
                    sv.inc(vector.tensor_reduce(lnmc[:, la:la + 1],
                                                S2c(R_AMAX, 0, 4),
                                                AX, ALU.max))
                ev_rs2[la] = sv.n

            if la >= 0:
                @block.sync
                def _(sync, la=la):
                    sync.wait_ge(sv.h, ev_rs2[la])
                    sync.wait_ge(ss.h, ev_sq2[la])
                    dx2s.dma(sync.dma_start(out=x2_t4[la], in_=x2_t[:, :]), 0)
                    if dbg:
                        qs.dma(sync.dma_start(out=y2_t4[la], in_=x2_t[:, :]))
                ev_x2st[la] = dx2s.s[0].n

        if stage <= 1:
            if dbg == 2:
                dmp = {}
                for nm, tile_ in (("d_r", r_t), ("d_y", y_tile), ("d_x2", x2_t),
                                  ("d_f", f_b), ("d_o", o_b), ("d_g", g_b),
                                  ("d_stat", stat), ("d_m2", m2t), ("d_rs2", rs2t),
                                  ("d_carry", carry), ("d_zhi", zThi),
                                  ("d_uhi", uThi), ("d_ulo", uTlo)):
                    shp = [tile_.shape[0], tile_.shape[1]]
                    dt_ = tile_.dtype
                    dmp[nm] = (nc.dram_tensor(nm, shp, dt_, kind="ExternalOutput").ap(), tile_)
                vdmp = nc.dram_tensor("d_v", [NH, TILE], F32, kind="ExternalOutput").ap()

                @block.sync
                def _(sync):
                    sync.wait_ge(sv.h, sv.n)
                    sync.wait_ge(ss.h, ss.n)
                    sync.wait_ge(st.h, st.n)
                    for nm, (ap_, tile_) in dmp.items():
                        qs.dma(sync.dma_start(out=ap_, in_=tile_[:, :]))
                    qs.dma(sync.dma_start(out=vdmp, in_=v_sb[:, :]))
            _finish()
            return nc

        @block.vector
        def _(vector):
            ch = Chain(vector, sv)
            vector.wait_ge(sv.h, ev_rs2[nt_run - 1])
            ch(vector.tensor_reduce(lnmax[:, 0:1], lnmc[:, 0:nt_run],
                                    AX, ALU.max))
        ev_lnf = sv.n

        @block.tensor
        def _(tensor):
            tensor.wait_ge(sv.h, ev_lnf)
            st.inc(tensor.transpose(ps_v[0:1, 0:128], lnmax[:, 0:1], eyet[:, :]))
        ev_lt = st.n

        @block.vector
        def _(vector):
            vector.wait_ge(st.h, ev_lt)
            sv.inc(vector.tensor_copy(ccrow[0:1, :], ps_v[0:1, 0:128]))
        ev_lrow = sv.n

        @block.scalar
        def _(scalar):
            scalar.wait_ge(qs.h, ev_const)
            scalar.activation(sgc[0:1, 2:3], sgc[0:1, 0:1], AF.Exp, scale=-1.0)
            ss.inc(scalar.activation(sgc[0:1, 3:4], sgc[0:1, 1:2], AF.Exp, scale=-1.0))
        ev_exp = ss.n

        @block.gpsimd
        def _(g):
            g.wait_ge(sv.h, ev_lrow)
            qg.dma(g.dma_start(out=ccxi_d.ap(), in_=ccrow[0:1, :]))
            g.wait_ge(qg.h, qg.n)
            scc.inc(g.collective_compute(
                "AllReduce", ALU.max, replica_groups=[list(range(8))],
                ins=[ccxi_d.ap()], outs=[ccxo_d.ap()]))
            g.wait_ge(scc.h, scc.n)
            qg.dma(g.dma_start(out=ccrow[0:1, :], in_=ccxo_d.ap()))
        ev_ccx = qg.n

        @block.vector
        def _(vector):
            ch = Chain(vector, sv)
            vector.wait_ge(ss.h, ev_exp)
            ch(vector.tensor_scalar(sgc[0:1, 2:4], sgc[0:1, 2:4], 1.0, None, ALU.add))
            ch(vector.reciprocal(sgc[0:1, 2:4], sgc[0:1, 2:4]))
            vector.wait_ge(qg.h, ev_ccx)
            ch(vector.tensor_reduce(sgc[0:1, 4:5], ccrow[0:1, :], AX, ALU.max))
            ch(vector.tensor_tensor(sgc[0:1, 4:5], sgc[0:1, 4:5],
                                    sgc[0:1, 2:3], ALU.mult))
        ev_gm = sv.n

        @block.tensor
        def _(tensor):
            tensor.wait_ge(sv.h, ev_gm)
            st.inc(tensor.matmul(ps_vb[:, 0:1], ones1t[:, :], sgc[0:1, 4:5],
                                 start=True, stop=True))
        ev_gb = st.n

        @block.vector
        def _(vector):
            ch = Chain(vector, sv)
            vector.wait_ge(st.h, ev_gb)
            ch(vector.tensor_scalar(sxv[:, 1:2], ps_vb[:, 0:1], 1.0 / QMAX,
                                    None, ALU.mult))
            ch(vector.reciprocal(sxv[:, 0:1], sxv[:, 1:2]))
            ch(vector.tensor_scalar(qsc[:, :], rs2t[:, :], sxv[:, 0:1], None,
                                    ALU.mult))
            ch(vector.tensor_tensor(qbi[:, :], m2t[:, :], qsc[:, :], ALU.mult))
            ch(vector.tensor_scalar(qbi[:, :], qbi[:, :], -1.0, None, ALU.mult))
        ev_qsc = sv.n

        p1.close(); open_stacks.remove(p1)
        p1w.close(); open_stacks.remove(p1w)

        if stage <= 2:
            _finish()
            return nc

        p2c = ExitStack()
        open_stacks.append(p2c)
        wdT = p2c.enter_context(nc.sbuf_tensor("wdT", [128, NIC * 1024], BF16))

        p2 = ExitStack()
        open_stacks.append(p2)
        p2e = p2.enter_context
        xqT = p2e(nc.sbuf_tensor("xqT", [128, NKC * T], BF16))
        wdst = p2e(nc.sbuf_tensor("wdst", [128, NI], F32))
        wdqn = p2e(nc.sbuf_tensor("wdqn", [128, NI], BF16))
        wgst = p2e(nc.sbuf_tensor("wgst", [128, 2 * D], F32))
        wust = p2e(nc.sbuf_tensor("wust", [128, 2 * D], F32))
        wgqn = p2e(nc.sbuf_tensor("wgqn", [128, 2 * D], BF16))
        wuqn = p2e(nc.sbuf_tensor("wuqn", [128, 2 * D], BF16))
        wgqT = p2e(nc.sbuf_tensor("wgqT", [128, 2 * D], BF16))
        wuqT = p2e(nc.sbuf_tensor("wuqT", [128, 2 * D], BF16))
        q_s = p2e(nc.sbuf_tensor("q_s", [128, 2 * D], F32))
        xqn = p2e(nc.sbuf_tensor("xqn", [128, 2 * D], BF16))
        wrnd = p2e(nc.sbuf_tensor("wrnd", [128, 2 * D], F32))
        gsil = p2e(nc.sbuf_tensor("gsil", [128, 2 * TG], F32))
        h_t = p2e(nc.sbuf_tensor("h_t", [128, 2 * TG], F32))
        mgt = p2e(nc.sbuf_tensor("mgt", [128, NIC], F32))
        mut = p2e(nc.sbuf_tensor("mut", [128, NIC], F32))
        ps_gate = p2e(nc.psum_tensor("ps_gate", [128, TG], F32))
        ps_up = p2e(nc.psum_tensor("ps_up", [128, TG], F32))
        ps_wt = p2e(nc.psum_tensor("ps_wt", [128, 1024], BF16))
        ps_wtg = p2e(nc.psum_tensor("ps_wtg", [128, 1024], BF16))

        PH1_SS, PH1_SV, PH1_ST, PH1_SG = ss.n, sv.n, st.n, sg.n

        ev_x2ld = {}; ev_aff = {}; ev_xq = {}; ev_xqtr = {}; ev_xqevac = {}
        for tc in range(NTC):
            parq = tc % 2
            @block.sync
            def _(sync, tc=tc, parq=parq):
                if tc == 0:
                    sync.wait_ge(sv.h, ev_qsc)
                    sync.wait_ge(ss.h, PH1_SS)
                    sync.wait_ge(sg.h, PH1_SG)
                    sync.wait_ge(st.h, PH1_ST)
                    dx2s.wait(sync, 0, ev_x2st[nt_run - 1])
                if tc >= 2:
                    sync.wait_ge(sv.h, ev_xq[tc - 2])
                dqx.dma(sync.dma_start(out=q_s[:, parq * D:(parq + 1) * D],
                                       in_=x2_tc[tc]), parq)
            ev_x2ld[tc] = dqx.s[parq].n

            @block.scalar
            def _(scalar, tc=tc, parq=parq):
                dqx.wait(scalar, tc % 2, ev_x2ld[tc])
                ss.inc(scalar.activation(
                    q_s[:, parq * D:(parq + 1) * D],
                    q_s[:, parq * D:(parq + 1) * D],
                    AF.Identity, scale=qsc[:, tc:tc + 1], bias=qbi[:, tc:tc + 1]))
            ev_aff[tc] = ss.n

            @block.vector
            def _(vector, tc=tc, parq=parq):
                vector.wait_ge(ss.h, ev_aff[tc])
                if tc >= 2:
                    vector.wait_ge(st.h, ev_xqtr[tc - 2])
                vector.tensor_scalar(q_s[:, parq * D:(parq + 1) * D],
                                     q_s[:, parq * D:(parq + 1) * D],
                                     TWO23, -TWO23, ALU.add, ALU.add)
                sv.inc(vector.tensor_scalar(
                    xqn[:, parq * D:(parq + 1) * D],
                    q_s[:, parq * D:(parq + 1) * D],
                    QMAX, -QMAX - 1.0, ALU.min, ALU.max))
            ev_xq[tc] = sv.n

            @block.tensor
            def _(tensor, tc=tc, parq=parq):
                tensor.wait_ge(sv.h, ev_xq[tc])
                if tc >= 1:
                    tensor.wait_ge(ss.h, ev_xqevac[tc - 1])
                for kc in range(NKC):
                    st.inc(tensor.transpose(
                        ps_wt[:, kc * 128:(kc + 1) * 128],
                        xqn[:, parq * D + kc * 128:parq * D + (kc + 1) * 128],
                        eyebt[:, :]))
            ev_xqtr[tc] = st.n

            @block.scalar
            def _(scalar, tc=tc):
                scalar.wait_ge(st.h, ev_xqtr[tc])
                ss.inc(scalar.copy(
                    xqT[:, :].rearrange("p (k t) -> p k t", k=NKC, t=T)[:, :, tc * 128:(tc + 1) * 128],
                    ps_wt[:, :].rearrange("p (k b) -> p k b", k=NKC, b=128)))
            ev_xqevac[tc] = ss.n
        ev_xqT_done = ev_xqevac[NTC - 1]

        if stage <= 3:
            _finish()
            return nc

        ev_wdld = {}; ev_wdam = {}; ev_wdr = {}; ev_wdq = {}
        ev_wdtr = {}; ev_wdevac = {}; ev_wdafs = {}; ev_wdq_part = [0]
        for hc in range(NKC):
            @block.sync
            def _(sync, hc=hc):
                if hc >= 1:
                    sync.wait_ge(ss.h, ev_wdafs[hc - 1])
                dwd.dma(sync.dma_start(out=wdst[:, :],
                                       in_=wd_d[hc * 128:(hc + 1) * 128, :]), 0)
            ev_wdld[hc] = dwd.s[0].n

            @block.vector
            def _(vector, hc=hc):
                dwd.wait(vector, 0, ev_wdld[hc])
                if hc >= 1:
                    vector.wait_ge(ss.h, ev_wdafs[hc - 1])
                chd = Chain(vector, sv)
                chd(vector.tensor_reduce(S2c(WD_AM, 0), wdst[:, :], AX, ALU.max,
                                         apply_absolute_value=True))
                vector.tensor_scalar(S2(WD_SW), S2(WD_AM),
                                     1.0 / QMAX, None, ALU.mult)
                vector.tensor_copy(swdt[:, hc:hc + 1], S2c(WD_SW, 0))
                sv.inc(vector.reciprocal(S2(WD_R), S2(WD_SW)))
            ev_wdr[hc] = sv.n

            for cofs, clen in ((0, 2048), (2048, NI - 2048)):
                @block.scalar
                def _(scalar, hc=hc, cofs=cofs, clen=clen):
                    if cofs == 0:
                        scalar.wait_ge(sv.h, ev_wdr[hc])
                    if hc >= 1 or cofs > 0:
                        scalar.wait_ge(sv.h, ev_wdq_part[-1])
                    ss.inc(scalar.activation(
                        wrnd[:, 0:clen], wdst[:, cofs:cofs + clen], AF.Identity,
                        scale=S2c(WD_R, 0), bias=c23t[:, 0:1]))
                ev_wdaf = ss.n

                @block.vector
                def _(vector, hc=hc, cofs=cofs, clen=clen, ev=ev_wdaf):
                    vector.wait_ge(ss.h, ev)
                    if hc >= 1 and cofs == 0:
                        vector.wait_ge(st.h, ev_wdtr[hc - 1])
                    vector.tensor_scalar(wrnd[:, 0:clen], wrnd[:, 0:clen],
                                         -TWO23, QMAX, ALU.add, ALU.min)
                    sv.inc(vector.tensor_scalar(
                        wdqn[:, cofs:cofs + clen], wrnd[:, 0:clen],
                        -QMAX - 1.0, None, ALU.max))
                ev_wdq_part.append(sv.n)
            ev_wdq[hc] = sv.n
            ev_wdafs[hc] = ss.n

            for grp in range(3):
                lo = grp * 8
                hi_ = min(lo + 8, NIC)
                @block.tensor
                def _(tensor, hc=hc, lo=lo, hi_=hi_, grp=grp):
                    tensor.wait_ge(sv.h, ev_wdq[hc])
                    if hc == 0 and grp == 0:
                        tensor.wait_ge(ss.h, ev_xqT_done)
                    if grp >= 1:
                        tensor.wait_ge(ss.h, ev_wdevac[(hc, grp - 1)])
                    elif hc >= 1:
                        tensor.wait_ge(ss.h, ev_wdevac[(hc - 1, 2)])
                    for icc in range(lo, hi_):
                        st.inc(tensor.transpose(
                            ps_wt[:, (icc - lo) * 128:(icc - lo + 1) * 128],
                            wdqn[:, icc * 128:(icc + 1) * 128],
                            eyebt[:, :]))
                ev_wdtr_g = st.n

                @block.scalar
                def _(scalar, hc=hc, lo=lo, hi_=hi_, ev=ev_wdtr_g):
                    scalar.wait_ge(st.h, ev)
                    for icc in range(lo, hi_):
                        last = scalar.copy(
                            wdT[:, icc * 1024 + hc * 128:icc * 1024 + (hc + 1) * 128],
                            ps_wt[:, (icc - lo) * 128:(icc - lo + 1) * 128])
                    ss.inc(last)
                ev_wdevac[(hc, grp)] = ss.n
            ev_wdtr[hc] = st.n
        ev_wd_done = ev_wdevac[(NKC - 1, 2)]

        if stage <= 4:
            _finish()
            return nc

        ev_wgld = {}; ev_am2 = {}; ev_r2 = {}; ev_wgq = {}; ev_wgafs = {}
        ev_wgtr = {}; ev_wgte1 = {}; ev_wutr = {}; ev_wgte = {}
        ev_gmm = {}; ev_umm = {}; ev_sil = {}; ev_hmul = {}
        ev_hmax = {}; ev_hst = {}
        for ic in range(NIC):
            parw = ic % 2
            @block.sync
            def _(sync, ic=ic, parw=parw):
                if ic >= 2:
                    sync.wait_ge(ss.h, ev_wgafs[ic - 2])
                dwg.dma(sync.dma_start(out=wgst[:, parw * D:(parw + 1) * D],
                                       in_=wg_d[ic * 128:(ic + 1) * 128, :]), parw)
                dwg.dma(sync.dma_start(out=wust[:, parw * D:(parw + 1) * D],
                                       in_=wu_d[ic * 128:(ic + 1) * 128, :]), parw)
            ev_wgld[ic] = dwg.s[parw].n

            @block.vector
            def _(vector, ic=ic, parw=parw):
                dwg.wait(vector, ic % 2, ev_wgld[ic])
                if ic >= 1:
                    vector.wait_ge(ss.h, ev_wgafs[ic - 1])
                chw = Chain(vector, sv)
                vector.tensor_reduce(S2c(WG_AM, 0),
                                     wgst[:, parw * D:(parw + 1) * D],
                                     AX, ALU.max, apply_absolute_value=True)
                chw(vector.tensor_reduce(S2c(WU_AM, 0),
                                         wust[:, parw * D:(parw + 1) * D],
                                         AX, ALU.max, apply_absolute_value=True))
                vector.tensor_scalar(S2(WG_SW), S2(WG_AM),
                                     1.0 / QMAX, None, ALU.mult)
                vector.tensor_scalar(S2(WU_SW), S2(WU_AM),
                                     1.0 / QMAX, None, ALU.mult)
                vector.tensor_copy(swgt[:, ic:ic + 1], S2c(WG_SW, 0))
                vector.tensor_copy(swut[:, ic:ic + 1], S2c(WU_SW, 0))
                vector.reciprocal(S2(WG_R), S2(WG_SW))
                vector.reciprocal(S2(WU_R), S2(WU_SW))
                vector.tensor_scalar(S2(WG_MU), S2(WG_SW),
                                     sxv[:, 1:2], None, ALU.mult)
                vector.tensor_copy(mgt[:, ic:ic + 1], S2c(WG_MU, 0))
                vector.tensor_scalar(S2(WG_MU), S2(WU_SW),
                                     sxv[:, 1:2], None, ALU.mult)
                sv.inc(vector.tensor_copy(mut[:, ic:ic + 1], S2c(WG_MU, 0)))
            ev_r2[ic] = sv.n

            @block.scalar
            def _(scalar, ic=ic, parw=parw):
                scalar.wait_ge(sv.h, ev_r2[ic])
                if ic == 0:
                    scalar.wait_ge(sv.h, ev_wdq[NKC - 1])
                elif ic >= 1:
                    scalar.wait_ge(sv.h, ev_wgq[ic - 1])
                scalar.activation(wrnd[:, 0:D], wgst[:, parw * D:(parw + 1) * D],
                                  AF.Identity, scale=S2c(WG_R, 0),
                                  bias=c23t[:, 0:1])
                ss.inc(scalar.activation(wrnd[:, D:2 * D],
                                         wust[:, parw * D:(parw + 1) * D],
                                         AF.Identity, scale=S2c(WU_R, 0),
                                         bias=c23t[:, 0:1]))
            ev_wgaf = ss.n
            ev_wgafs[ic] = ss.n

            @block.vector
            def _(vector, ic=ic, parw=parw, ev=ev_wgaf):
                vector.wait_ge(ss.h, ev)
                if ic >= 2:
                    vector.wait_ge(st.h, ev_wutr[ic - 2])
                for half, wqn in ((0, wgqn), (1, wuqn)):
                    vector.tensor_scalar(wrnd[:, half * D:(half + 1) * D],
                                         wrnd[:, half * D:(half + 1) * D],
                                         -TWO23, QMAX, ALU.add, ALU.min)
                    sv.inc(vector.tensor_scalar(
                        wqn[:, parw * D:(parw + 1) * D],
                        wrnd[:, half * D:(half + 1) * D],
                        -QMAX - 1.0, None, ALU.max))
            ev_wgq[ic] = sv.n

            @block.tensor
            def _(tensor, ic=ic, parw=parw):
                tensor.wait_ge(sv.h, ev_wgq[ic])
                if ic >= 2:
                    tensor.wait_ge(ss.h, ev_wgte[ic - 2])
                for kc in range(NKC):
                    st.inc(tensor.transpose(
                        ps_wtg[:, kc * 128:(kc + 1) * 128],
                        wgqn[:, parw * D + kc * 128:parw * D + (kc + 1) * 128],
                        eyebt[:, :]))
            ev_wgtr[ic] = st.n

            @block.scalar
            def _(scalar, ic=ic, parw=parw):
                scalar.wait_ge(st.h, ev_wgtr[ic])
                ss.inc(scalar.copy(wgqT[:, parw * D:(parw + 1) * D],
                                          ps_wtg[:, :]))
            ev_wgte1[ic] = ss.n

            @block.tensor
            def _(tensor, ic=ic, parw=parw):
                tensor.wait_ge(ss.h, ev_wgte1[ic])
                for kc in range(NKC):
                    st.inc(tensor.transpose(
                        ps_wtg[:, kc * 128:(kc + 1) * 128],
                        wuqn[:, parw * D + kc * 128:parw * D + (kc + 1) * 128],
                        eyebt[:, :]))
            ev_wutr[ic] = st.n

            @block.scalar
            def _(scalar, ic=ic, parw=parw):
                scalar.wait_ge(st.h, ev_wutr[ic])
                ss.inc(scalar.copy(wuqT[:, parw * D:(parw + 1) * D],
                                          ps_wtg[:, :]))
            ev_wgte[ic] = ss.n

            for ts_ in range(NTG):
                parh = ts_ % 2

                @block.tensor
                def _(tensor, ic=ic, ts_=ts_, parw=parw):
                    tensor.wait_ge(ss.h, ev_wgte[ic])
                    tensor.wait_ge(ss.h, ev_xqevac[(ts_ + 1) * NKC - 1])
                    pk = (ic, ts_ - 1) if ts_ > 0 else (ic - 1, NTG - 1)
                    if pk in ev_sil:
                        tensor.wait_ge(ss.h, ev_sil[pk])
                    for kc in range(NKC):
                        for half in range(2):
                            st.inc(tensor.matmul(
                                ps_gate[:, half * 512:(half + 1) * 512],
                                wgqT[:, parw * D + kc * 128:parw * D + (kc + 1) * 128],
                                xqT[:, kc * T + ts_ * TG + half * 512:kc * T + ts_ * TG + (half + 1) * 512],
                                start=(kc == 0), stop=(kc == NKC - 1)))
                    ev_gmm[(ic, ts_)] = st.n
                    if pk in ev_hmul:
                        tensor.wait_ge(sv.h, ev_hmul[pk])
                    for kc in range(NKC):
                        for half in range(2):
                            st.inc(tensor.matmul(
                                ps_up[:, half * 512:(half + 1) * 512],
                                wuqT[:, parw * D + kc * 128:parw * D + (kc + 1) * 128],
                                xqT[:, kc * T + ts_ * TG + half * 512:kc * T + ts_ * TG + (half + 1) * 512],
                                start=(kc == 0), stop=(kc == NKC - 1)))
                ev_umm[(ic, ts_)] = st.n

                @block.scalar
                def _(scalar, ic=ic, ts_=ts_, parh=parh):
                    scalar.wait_ge(st.h, ev_gmm[(ic, ts_)])
                    pk = (ic, ts_ - 2) if ts_ >= 2 else \
                        ((ic - 1, NTG - 2 + ts_) if ic > 0 else None)
                    if pk is not None and pk in ev_hmul:
                        scalar.wait_ge(sv.h, ev_hmul[pk])
                    ss.inc(scalar.activation(
                        gsil[:, parh * TG:(parh + 1) * TG], ps_gate[:, :],
                        AF.Silu, scale=mgt[:, ic:ic + 1]))
                ev_sil[(ic, ts_)] = ss.n

                @block.vector
                def _(vector, ic=ic, ts_=ts_, parh=parh):
                    vector.wait_ge(st.h, ev_umm[(ic, ts_)])
                    vector.wait_ge(ss.h, ev_sil[(ic, ts_)])
                    pk = (ic, ts_ - 2) if ts_ >= 2 else \
                        ((ic - 1, NTG - 2 + ts_) if ic > 0 else None)
                    if pk is not None and pk in ev_hst:
                        dhs.wait(vector, ts_ % 2, ev_hst[pk])
                    sv.inc(vector.scalar_tensor_tensor(
                        h_t[:, parh * TG:(parh + 1) * TG], ps_up[:, :],
                        mut[:, ic:ic + 1], gsil[:, parh * TG:(parh + 1) * TG],
                        ALU.mult, ALU.mult))
                ev_hmul[(ic, ts_)] = sv.n

                @block.vector
                def _(vector, ic=ic, ts_=ts_, parh=parh):
                    sv.inc(vector.tensor_reduce(
                        hmaxc[:, ic * NTG + ts_:ic * NTG + ts_ + 1],
                        h_t[:, parh * TG:(parh + 1) * TG],
                        AX, ALU.max, apply_absolute_value=True))
                ev_hmax[(ic, ts_)] = sv.n

                @block.gpsimd
                def _(g, ic=ic, ts_=ts_, parh=parh):
                    g.wait_ge(sv.h, ev_hmul[(ic, ts_)])
                    dhs.dma(g.dma_start(
                        out=h_ic[ic][:, ts_ * TG:(ts_ + 1) * TG],
                        in_=h_t[:, parh * TG:(parh + 1) * TG]), parh)
                ev_hst[(ic, ts_)] = dhs.s[parh].n

        if stage <= 5:
            _finish()
            return nc

        @block.vector
        def _(vector):
            ch = Chain(vector, sv)
            vector.wait_ge(sv.h, ev_hmax[(NIC - 1, NTG - 1)])
            ch(vector.tensor_reduce(hmax[:, 0:1], hmaxc[:, :], AX, ALU.max))
        ev_hmf = sv.n

        @block.tensor
        def _(tensor):
            tensor.wait_ge(sv.h, ev_hmf)
            st.inc(tensor.transpose(ps_up[0:1, 0:128], hmax[:, 0:1], eyet[:, :]))
        ev_ht = st.n

        @block.vector
        def _(vector):
            vector.wait_ge(st.h, ev_ht)
            sv.inc(vector.tensor_copy(ccrow[0:1, :], ps_up[0:1, 0:128]))
        ev_hrow = sv.n

        @block.gpsimd
        def _(g):
            g.wait_ge(sv.h, ev_hrow)
            qg.dma(g.dma_start(out=cchi_d.ap(), in_=ccrow[0:1, :]))
            g.wait_ge(qg.h, qg.n)
            scc.inc(g.collective_compute(
                "AllReduce", ALU.max, replica_groups=[list(range(8))],
                ins=[cchi_d.ap()], outs=[ccho_d.ap()]))
            g.wait_ge(scc.h, scc.n)
            qg.dma(g.dma_start(out=ccrow[0:1, :], in_=ccho_d.ap()))
        ev_cch = qg.n

        @block.vector
        def _(vector):
            ch = Chain(vector, sv)
            vector.wait_ge(qg.h, ev_cch)
            ch(vector.tensor_reduce(sgc[0:1, 5:6], ccrow[0:1, :], AX, ALU.max))
            ch(vector.tensor_tensor(sgc[0:1, 5:6], sgc[0:1, 5:6],
                                    sgc[0:1, 3:4], ALU.mult))
        ev_gh = sv.n

        @block.tensor
        def _(tensor):
            tensor.wait_ge(sv.h, ev_gh)
            st.inc(tensor.matmul(ps_up[:, 0:1], ones1t[:, :], sgc[0:1, 5:6],
                                 start=True, stop=True))
        ev_ghb = st.n

        @block.vector
        def _(vector):
            ch = Chain(vector, sv)
            vector.wait_ge(st.h, ev_ghb)
            ch(vector.tensor_scalar(shv[:, 1:2], ps_up[:, 0:1], 1.0 / QMAX,
                                    None, ALU.mult))
            ch(vector.reciprocal(shv[:, 0:1], shv[:, 1:2]))
            ch(vector.tensor_scalar(swdt[:, :], swdt[:, :], shv[:, 1:2],
                                    None, ALU.mult))
        ev_md = sv.n

        if dbg == 2:
            dmp2 = {}
            for nm, tile_ in (("e_sxv", sxv), ("e_shv", shv), ("e_qsc", qsc),
                              ("e_qbi", qbi), ("e_swg", swgt), ("e_swu", swut),
                              ("e_swd", swdt), ("e_mgt", mgt), ("e_mut", mut),
                              ("e_xqT", xqT), ("e_lnm", lnmax),
                              ("e_ccr", ccrow), ("e_sgc", sgc),
                              ("e_hmx", hmax), ("e_m2", m2t), ("e_rs2", rs2t)):
                shp = [tile_.shape[0], tile_.shape[1]]
                dmp2[nm] = (nc.dram_tensor(nm, shp, tile_.dtype,
                                           kind="ExternalOutput").ap(), tile_)
            hdbg = nc.dram_tensor("e_h", [NI, T], F32, kind="ExternalOutput").ap()

            @block.sync
            def _(sync):
                sync.wait_ge(sv.h, sv.n)
                for nm, (ap_, tile_) in dmp2.items():
                    qs.dma(sync.dma_start(out=ap_, in_=tile_[:, :]))
                qs.dma(sync.dma_start(out=hdbg, in_=h_d))

        p2.close(); open_stacks.remove(p2)

        if stage <= 6:
            _finish()
            return nc

        p3 = ExitStack()
        open_stacks.append(p3)
        p3e = p3.enter_context
        hq = p3e(nc.sbuf_tensor("hq", [128, 2 * NIC * TG], BF16))
        hst_b = p3e(nc.sbuf_tensor("hst_b", [128, 3 * TG], F32))
        dsc = p3e(nc.sbuf_tensor("dsc", [128, NKC * TG], F32))
        x2b = p3e(nc.sbuf_tensor("x2b", [128, 2 * D], F32))
        outb = p3e(nc.sbuf_tensor("outb", [128, 2 * D], F32))
        ps_d = [p3e(nc.psum_tensor("ps_dA", [128, TG], F32)),
                p3e(nc.psum_tensor("ps_dB", [128, TG], F32))]
        ps_t2_single = p3e(nc.psum_tensor("ps_t2A", [128, 1024], F32))
        ps_t2 = [ps_t2_single, ps_t2_single]

        PH2_SS, PH2_SV, PH2_ST, PH2_SG = ss.n, sv.n, st.n, sg.n

        ev_hld = {}; ev_hq = {}; ev_hqs = {}; ev_dmm = {}; ev_dsc = {}
        ev_x2bld = {}; ev_dtr = {}; ev_oadd = {}; ev_ost = {}

        def emit_quant(tg):
            ptg = tg % 2
            for icc in range(NIC):
                pl = (tg * NIC + icc) % 3
                k = tg * NIC + icc
                if icc % 2 == 0:
                    @block.sync
                    def _(sync, tg=tg, icc=icc, pl=pl, k=k):
                        if tg == 0 and icc == 0:
                            sync.wait_ge(sv.h, ev_md)
                            sync.wait_ge(ss.h, PH2_SS)
                            sync.wait_ge(sg.h, PH2_SG)
                            sync.wait_ge(st.h, PH2_ST)
                            dhs.wait_all(sync)
                        if k >= 3:
                            pk = divmod(k - 3, NIC)
                            sync.wait_ge(sv.h, ev_hqs[(pk[0], pk[1])])
                        dhl.dma(sync.dma_start(
                            out=hst_b[:, pl * TG:(pl + 1) * TG],
                            in_=h_ic[icc][:, tg * TG:(tg + 1) * TG]), pl)
                    ev_hld[(tg, icc)] = (pl, dhl.s[pl].n)
                else:
                    @block.gpsimd
                    def _(g, tg=tg, icc=icc, pl=pl, k=k):
                        if tg == 0 and icc == 1:
                            g.wait_ge(sv.h, ev_md)
                            g.wait_ge(ss.h, PH2_SS)
                            g.wait_ge(sg.h, PH2_SG)
                            g.wait_ge(st.h, PH2_ST)
                            dhs.wait_all(g)
                        if k >= 3:
                            pk = divmod(k - 3, NIC)
                            g.wait_ge(sv.h, ev_hqs[(pk[0], pk[1])])
                        dhl.dma(g.dma_start(
                            out=hst_b[:, pl * TG:(pl + 1) * TG],
                            in_=h_ic[icc][:, tg * TG:(tg + 1) * TG]), pl)
                    ev_hld[(tg, icc)] = (pl, dhl.s[pl].n)

                @block.vector
                def _(vector, tg=tg, icc=icc, pl=pl, ptg=ptg):
                    pl_, v_ = ev_hld[(tg, icc)]
                    dhl.wait(vector, pl_, v_)
                    if tg >= 2 and icc == 0:
                        vector.wait_ge(st.h, ev_dmm[(tg - 2, NKC - 1)])
                    vector.tensor_scalar(
                        hst_b[:, pl * TG:(pl + 1) * TG],
                        hst_b[:, pl * TG:(pl + 1) * TG],
                        shv[:, 0:1], TWO23, ALU.mult, ALU.add)
                    vector.tensor_scalar(
                        hst_b[:, pl * TG:(pl + 1) * TG],
                        hst_b[:, pl * TG:(pl + 1) * TG],
                        -TWO23, QMAX, ALU.add, ALU.min)
                    sv.inc(vector.tensor_scalar(
                        hq[:, (ptg * NIC + icc) * TG:(ptg * NIC + icc + 1) * TG],
                        hst_b[:, pl * TG:(pl + 1) * TG],
                        -QMAX - 1.0, None, ALU.max))
                ev_hq[(tg, icc)] = sv.n
                ev_hqs[(tg, icc)] = ev_hq[(tg, icc)]

        emit_quant(0)
        for tg in range(NTG):
            ptg = tg % 2
            if tg + 1 < NTG:
                emit_quant(tg + 1)

            for hc in range(NKC):
                pd = hc % 2
                @block.tensor
                def _(tensor, tg=tg, hc=hc, pd=pd, ptg=ptg):
                    if tg == 0 and hc == 0:
                        tensor.wait_ge(sv.h, ev_md)
                    tensor.wait_ge(sv.h, ev_hq[(tg, NIC - 1)])
                    pk = (tg, hc - 2) if hc >= 2 else \
                        ((tg - 1, NKC - 2 + hc) if tg > 0 else None)
                    if pk is not None and pk in ev_dsc:
                        tensor.wait_ge(sv.h, ev_dsc[pk])
                    for icc in range(NIC):
                        for half in range(2):
                            st.inc(tensor.matmul(
                                ps_d[pd][:, half * 512:(half + 1) * 512],
                                wdT[:, icc * 1024 + hc * 128:icc * 1024 + (hc + 1) * 128],
                                hq[:, (ptg * NIC + icc) * TG + half * 512:(ptg * NIC + icc) * TG + (half + 1) * 512],
                                start=(icc == 0), stop=(icc == NIC - 1)))
                ev_dmm[(tg, hc)] = st.n

                @block.vector
                def _(vector, tg=tg, hc=hc, pd=pd):
                    vector.wait_ge(st.h, ev_dmm[(tg, hc)])
                    if tg >= 1:
                        vector.wait_ge(st.h, ev_dtr[(tg - 1, 7)])
                    sv.inc(vector.tensor_scalar(
                        dsc[:, hc * TG:(hc + 1) * TG], ps_d[pd][:, :],
                        swdt[:, hc:hc + 1], None, ALU.mult))
                ev_dsc[(tg, hc)] = sv.n

            for tcg in range(NKC):
                pt2 = tcg % 2
                tc_abs = tg * NKC + tcg
                @block.sync
                def _(sync, tg=tg, tcg=tcg, tc_abs=tc_abs):
                    k = tc_abs
                    if k == 0:
                        dx2s.wait(sync, 0, ev_x2st[nt_run - 1])
                    if k >= 2:
                        sync.wait_ge(sv.h, ev_oadd[k - 2])
                    dxb.dma(sync.dma_start(
                        out=x2b[:, (k % 2) * D:(k % 2) * D + D],
                        in_=x2_tc[tc_abs]), k % 2)
                ev_x2bld[tc_abs] = dxb.s[tc_abs % 2].n

                @block.tensor
                def _(tensor, tg=tg, tcg=tcg, pt2=pt2):
                    tensor.wait_ge(sv.h, ev_dsc[(tg, NKC - 1)])
                    k = tg * NKC + tcg
                    if k >= 1:
                        tensor.wait_ge(sv.h, ev_oadd[k - 1])
                    for hc in range(NKC):
                        st.inc(tensor.transpose(
                            ps_t2[pt2][:, hc * 128:(hc + 1) * 128],
                            dsc[:, hc * TG + tcg * 128:hc * TG + (tcg + 1) * 128],
                            eyet[:, :]))
                ev_dtr[(tg, tcg)] = st.n

                @block.vector
                def _(vector, tg=tg, tcg=tcg, pt2=pt2, tc_abs=tc_abs):
                    vector.wait_ge(st.h, ev_dtr[(tg, tcg)])
                    dxb.wait(vector, tc_abs % 2, ev_x2bld[tc_abs])
                    if tc_abs >= 2:
                        dys.wait(vector, tc_abs % 2, ev_ost[tc_abs - 2])
                    sv.inc(vector.tensor_tensor(
                        outb[:, (tc_abs % 2) * D:(tc_abs % 2) * D + D],
                        x2b[:, (tc_abs % 2) * D:(tc_abs % 2) * D + D],
                        ps_t2[pt2][:, :], ALU.add))
                ev_oadd[tc_abs] = sv.n

                @block.sync
                def _(sync, tc_abs=tc_abs):
                    sync.wait_ge(sv.h, ev_oadd[tc_abs])
                    dys.dma(sync.dma_start(
                        out=y_tc[tc_abs],
                        in_=outb[:, (tc_abs % 2) * D:(tc_abs % 2) * D + D]),
                        tc_abs % 2)
                ev_ost[tc_abs] = dys.s[tc_abs % 2].n

        if dbg == 2:
            wdt_dbg = nc.dram_tensor("f_wdT", [128, NIC * 1024], BF16,
                                     kind="ExternalOutput").ap()
            hq_dbg = nc.dram_tensor("f_hq", [128, 2 * NIC * TG], BF16,
                                    kind="ExternalOutput").ap()
            dsc_dbg = nc.dram_tensor("f_dsc", [128, NKC * TG], F32,
                                     kind="ExternalOutput").ap()

            @block.sync
            def _(sync):
                sync.wait_ge(sv.h, sv.n)
                sync.wait_ge(st.h, st.n)
                qs.dma(sync.dma_start(out=wdt_dbg, in_=wdT[:, :]))
                qs.dma(sync.dma_start(out=hq_dbg, in_=hq[:, :]))
                qs.dma(sync.dma_start(out=dsc_dbg, in_=dsc[:, :]))

        @block.sync
        def _(sync):
            sync.wait_ge(qs.h, qs.n)
            for d_ in all_slotdma:
                d_.wait_all(sync)
        @block.gpsimd
        def _(g):
            g.wait_ge(qg.h, qg.n)

        p3.close(); open_stacks.remove(p3)
        p2c.close(); open_stacks.remove(p2c)

    return nc


def kernel(L0_attn_x, L0_attn_norm_w, L0_attn_w_forget, L0_attn_w_input,
           L0_attn_w_output, L0_ffn_clip_x, L0_ffn_clip_down, L0_ffn_w_gate,
           L0_ffn_w_up, L0_ffn_w_down):
    nc = build_program(dbg=False)
    x = np.ascontiguousarray(np.asarray(L0_attn_x, dtype=np.float32))
    shared = {
        "wf": np.ascontiguousarray(np.asarray(L0_attn_w_forget, np.float32)),
        "wi": np.ascontiguousarray(np.asarray(L0_attn_w_input, np.float32)),
        "wo": np.ascontiguousarray(np.asarray(L0_attn_w_output, np.float32)),
        "nw": np.ascontiguousarray(np.asarray(L0_attn_norm_w, np.float32)),
        "cx": np.ascontiguousarray(np.asarray(L0_ffn_clip_x, np.float32)),
        "cd": np.ascontiguousarray(np.asarray(L0_ffn_clip_down, np.float32)),
        "wg": np.ascontiguousarray(np.asarray(L0_ffn_w_gate, np.float32)),
        "wu": np.ascontiguousarray(np.asarray(L0_ffn_w_up, np.float32)),
        "wd": np.ascontiguousarray(np.asarray(L0_ffn_w_down, np.float32)),
    }
    in_maps = [dict(shared, x=np.ascontiguousarray(x[c])) for c in range(8)]
    res = run_bass_kernel_spmd(nc, in_maps, core_ids=list(range(8)))
    return np.stack([np.asarray(res.results[c]["y"]) for c in range(8)], axis=0)


# version 06
# speedup vs baseline: 1.6301x; 1.0865x over previous
import numpy as np
import ml_dtypes
from contextlib import ExitStack
import concourse.bass as bass
import concourse.mybir as mybir
from concourse.bass_utils import run_bass_kernel_spmd

F32 = mybir.dt.float32
BF16 = mybir.dt.bfloat16
AF = mybir.ActivationFunctionType
ALU = mybir.AluOpType
AX = mybir.AxisListType.X

T = 4096
D = 1024
NH = 8
NI = 2816
NIC = NI // 128
NKC = D // 128
TILE = 512
NT = T // TILE
NTC = T // 128
TG = 1024
NTG = T // TG
TWO23 = 12582912.0
QMAX = 127.0
LN_EPS = 1e-5
RMS_EPS = 1e-6
FORGET_LB = 0.9


class Sem:
    def __init__(self, handle):
        self.h = handle
        self.n = 0

    def inc(self, instr, k=1):
        self.n += k
        instr.then_inc(self.h, k)
        return self.n

    def dma(self, instr):
        return self.inc(instr, 16)


class SlotDma:

    def __init__(self, nc, e, name, nslots):
        self.s = [Sem(e(nc.semaphore(f"{name}{i}"))) for i in range(nslots)]

    def dma(self, instr, slot):
        return self.s[slot].dma(instr)

    def wait(self, eng, slot, val):
        eng.wait_ge(self.s[slot].h, val)

    def wait_all(self, eng):
        for s_ in self.s:
            eng.wait_ge(s_.h, s_.n)


class Chain:

    def __init__(self, eng, sem):
        self.eng = eng
        self.sem = sem

    def __call__(self, instr):
        self.sem.inc(instr)
        self.eng.wait_ge(self.sem.h, self.sem.n)
        return instr


def build_program(dbg=False, stage=7, nt_run=NT):
    nc = bass.Bass("TRN2", target_bir_lowering=False, debug=False)

    x_d = nc.dram_tensor("x", [T, D], F32, kind="ExternalInput").ap()
    wf_d = nc.dram_tensor("wf", [D, D], F32, kind="ExternalInput").ap()
    wi_d = nc.dram_tensor("wi", [D, D], F32, kind="ExternalInput").ap()
    wo_d = nc.dram_tensor("wo", [D, D], F32, kind="ExternalInput").ap()
    nw_d = nc.dram_tensor("nw", [D], F32, kind="ExternalInput").ap()
    cx_d = nc.dram_tensor("cx", [1], F32, kind="ExternalInput").ap()
    cd_d = nc.dram_tensor("cd", [1], F32, kind="ExternalInput").ap()
    wg_d = nc.dram_tensor("wg", [NI, D], F32, kind="ExternalInput").ap()
    wu_d = nc.dram_tensor("wu", [NI, D], F32, kind="ExternalInput").ap()
    wd_d = nc.dram_tensor("wd", [D, NI], F32, kind="ExternalInput").ap()
    y_d = nc.dram_tensor("y", [T, D], F32, kind="ExternalOutput").ap()
    if dbg:
        y2_d = nc.dram_tensor("y2", [T, D], F32, kind="ExternalOutput").ap()
        y2_t4 = y2_d.rearrange("(n c p) d -> n p c d", n=NT, c=4, p=128)

    x2_d = nc.dram_tensor("x2buf", [T, D], F32).ap()
    h_d = nc.dram_tensor("hbuf", [NI, T], F32).ap()
    ccxi_d = nc.dram_tensor("ccxi", [1, 128], F32)
    ccxo_d = nc.dram_tensor("ccxo", [1, 128], F32)
    cchi_d = nc.dram_tensor("cchi", [1, 128], F32)
    ccho_d = nc.dram_tensor("ccho", [1, 128], F32)

    x_t4 = x_d.rearrange("(n c p) d -> n p c d", n=NT, c=4, p=128)
    x2_t4 = x2_d.rearrange("(n c p) d -> n p c d", n=NT, c=4, p=128)
    x2_tc = x2_d.rearrange("(n p) d -> n p d", n=NTC, p=128)
    y_tc = y_d.rearrange("(n p) d -> n p d", n=NTC, p=128)
    h_ic = h_d.rearrange("(a p) t -> a p t", a=NIC, p=128)

    eye32 = nc.inline_tensor(np.eye(128, dtype=np.float32), "ceye32").ap()
    eyeb = nc.inline_tensor(np.eye(128, dtype=ml_dtypes.bfloat16), "ceyeb").ap()
    hones_np = np.zeros((128, NH * NH), dtype=np.float32)
    for hh in range(NH):
        hones_np[:, hh * NH + hh] = 1.0
    hones = nc.inline_tensor(hones_np, "chones").ap()
    selb_np = np.zeros((NH, NH * 128), dtype=np.float32)
    for hh in range(NH):
        selb_np[hh, hh * 128:(hh + 1) * 128] = 1.0
    selb = nc.inline_tensor(selb_np, "cselb").ap()
    ones1 = nc.inline_tensor(np.ones((1, 128), dtype=np.float32), "cones1").ap()

    es = ExitStack()
    with es:
        e = es.enter_context

        qs = Sem(e(nc.semaphore("qs")))
        qg = Sem(e(nc.semaphore("qg")))
        sv = Sem(e(nc.semaphore("sv")))
        ss = Sem(e(nc.semaphore("ss")))
        st = Sem(e(nc.semaphore("st")))
        sg = Sem(e(nc.semaphore("sg")))
        scc = Sem(e(nc.semaphore("scc")))

        dwt = SlotDma(nc, e, "dwt", 2)
        dxl = SlotDma(nc, e, "dxl", 2)
        dx2s = SlotDma(nc, e, "dx2s", 1)
        dqx = SlotDma(nc, e, "dqx", 2)
        dwd = SlotDma(nc, e, "dwd", 1)
        dwg = SlotDma(nc, e, "dwg", 2)
        dhs = SlotDma(nc, e, "dhs", 2)
        dhl = SlotDma(nc, e, "dhl", 3)
        dxb = SlotDma(nc, e, "dxb", 2)
        dys = SlotDma(nc, e, "dys", 2)
        all_slotdma = [dwt, dxl, dx2s, dqx, dwd, dwg, dhs, dhl, dxb, dys]

        eyet = e(nc.sbuf_tensor("eyet", [128, 128], F32))
        eyebt = e(nc.sbuf_tensor("eyebt", [128, 128], BF16))
        honet = e(nc.sbuf_tensor("honet", [128, NH * NH], F32))
        selbt = e(nc.sbuf_tensor("selbt", [NH, NH * 128], F32))
        ones1t = e(nc.sbuf_tensor("ones1t", [1, 128], F32))
        nwt = e(nc.sbuf_tensor("nwt", [128, NKC], F32))
        m2t = e(nc.sbuf_tensor("m2t", [128, NTC], F32))
        rs2t = e(nc.sbuf_tensor("rs2t", [128, NTC], F32))
        qsc = e(nc.sbuf_tensor("qsc", [128, NTC], F32))
        qbi = e(nc.sbuf_tensor("qbi", [128, NTC], F32))
        carry = e(nc.sbuf_tensor("carry", [128, NH], F32))
        lnmax = e(nc.sbuf_tensor("lnmax", [128, 1], F32))
        lnmc = e(nc.sbuf_tensor("lnmc", [128, NT], F32))
        hmax = e(nc.sbuf_tensor("hmax", [128, 1], F32))
        hmaxc = e(nc.sbuf_tensor("hmaxc", [128, NIC * NTG], F32))
        sxv = e(nc.sbuf_tensor("sxv", [128, 4], F32))
        shv = e(nc.sbuf_tensor("shv", [128, 4], F32))
        sgc = e(nc.sbuf_tensor("sgc", [1, 8], F32))
        ccrow = e(nc.sbuf_tensor("ccrow", [1, 128], F32))
        swgt = e(nc.sbuf_tensor("swgt", [128, NIC], F32))
        swut = e(nc.sbuf_tensor("swut", [128, NIC], F32))
        swdt = e(nc.sbuf_tensor("swdt", [128, NKC], F32))
        stat2 = e(nc.sbuf_tensor("stat2", [128, 10 * 128], F32))
        statS = e(nc.sbuf_tensor("statS", [128, 256], F32))

        def S2(r):
            return stat2[:, r * 128:(r + 1) * 128]

        def S2c(r, c0, c1=None):
            c1 = c0 + 1 if c1 is None else c1
            return stat2[:, r * 128 + c0:r * 128 + c1]

        R_MEAN, R_VARP, R_A2, R_R0, R_WRK, R_M2, R_W2, R_SQ2, R_AMAX, R_TMP = range(10)
        WG_AM, WU_AM, WG_R, WU_R, WG_SW, WU_SW, WD_AM, WD_R, WD_SW, WG_MU = range(10)
        c23t = e(nc.sbuf_tensor("c23t", [128, 1], F32))
        open_stacks = []
        p1w = ExitStack()
        open_stacks.append(p1w)
        p1we = p1w.enter_context
        x_t = p1we(nc.sbuf_tensor("x_t", [128, 2 * 4096], F32))
        wfThi = p1we(nc.sbuf_tensor("wfThi", [128, NKC * D], BF16))
        wfTlo = p1we(nc.sbuf_tensor("wfTlo", [128, NKC * D], BF16))
        woThi = p1we(nc.sbuf_tensor("woThi", [128, NKC * D], BF16))
        woTlo = p1we(nc.sbuf_tensor("woTlo", [128, NKC * D], BF16))
        uThi = p1we(nc.sbuf_tensor("uThi", [128, NKC * NH], BF16))
        uTlo = p1we(nc.sbuf_tensor("uTlo", [128, NKC * NH], BF16))

        def x_sl(p):
            return x_t[:, p * 4096:(p + 1) * 4096]

        block = e(nc.Block())

        pro = ExitStack()
        pe_ = pro.enter_context
        wtile = pe_(nc.sbuf_tensor("wtile", [128, 2 * D], F32))
        u_sb = pe_(nc.sbuf_tensor("u_sb", [NH, D], F32))
        ps_w = pe_(nc.psum_tensor("ps_w", [128, 1024], F32))
        ps_u = pe_(nc.psum_tensor("ps_u", [NH, 1024], F32))

        @block.sync
        def _(sync):
            for dst, src in ((eyet, eye32), (eyebt, eyeb), (honet, hones),
                             (selbt, selb), (ones1t, ones1)):
                qs.dma(sync.dma_start(out=dst[:, :], in_=src))
            qs.dma(sync.dma_start(out=nwt[:, :],
                                  in_=nw_d.rearrange("(c p) -> p c", p=128),
                                  allow_slow_non_contiguous=True))
            qs.dma(sync.dma_start(out=sgc[0:1, 0:1], in_=cx_d.rearrange("(a o) -> a o", a=1)))
            qs.dma(sync.dma_start(out=sgc[0:1, 1:2], in_=cd_d.rearrange("(a o) -> a o", a=1)))
        ev_const = qs.n

        ev_xload = []
        @block.sync
        def _(sync):
            dxl.dma(sync.dma_start(out=x_sl(0), in_=x_t4[0]), 0)
        ev_xload.append(dxl.s[0].n)
        @block.sync
        def _(sync):
            dxl.dma(sync.dma_start(out=x_sl(1), in_=x_t4[1]), 1)
        ev_xload.append(dxl.s[1].n)

        ev_wi = []
        ev_umm_h = []
        for hh in range(NH):
            @block.sync
            def _(sync, hh=hh):
                if hh >= 2:
                    sync.wait_ge(st.h, ev_umm_h[hh - 2])
                dwt.dma(sync.dma_start(
                    out=wtile[:, (hh % 2) * D:(hh % 2) * D + D],
                    in_=wi_d[hh * 128:(hh + 1) * 128, :]), hh % 2)
            ev_wi.append(dwt.s[hh % 2].n)

            @block.tensor
            def _(tensor, hh=hh):
                if hh == 0:
                    tensor.wait_ge(qs.h, ev_const)
                dwt.wait(tensor, hh % 2, ev_wi[hh])
                for half in range(2):
                    st.inc(tensor.matmul(
                        ps_u[:, half * 512:(half + 1) * 512],
                        honet[:, hh * NH:(hh + 1) * NH],
                        wtile[:, (hh % 2) * D + half * 512:(hh % 2) * D + (half + 1) * 512],
                        start=(hh == 0), stop=(hh == NH - 1)))
            ev_umm_h.append(st.n)
        ev_umm = st.n

        @block.vector
        def _(vector):
            vector.wait_ge(st.h, ev_umm)
            sv.inc(vector.tensor_copy(u_sb[:, :], ps_u[:, :]))
        ev_usb = sv.n

        @block.tensor
        def _(tensor):
            tensor.wait_ge(sv.h, ev_usb)
            for kc in range(NKC):
                st.inc(tensor.transpose(ps_w[:, kc * NH:(kc + 1) * NH],
                                        u_sb[:, kc * 128:(kc + 1) * 128],
                                        eyet[0:NH, 0:NH]))
        ev_ut = st.n

        @block.vector
        def _(vector):
            ch = Chain(vector, sv)
            vector.wait_ge(st.h, ev_ut)
            for kc in range(NKC):
                ch(vector.tensor_scalar(
                    uThi[:, kc * NH:(kc + 1) * NH], ps_w[:, kc * NH:(kc + 1) * NH],
                    nwt[:, kc:kc + 1], None, ALU.mult))
                ch(vector.scalar_tensor_tensor(
                    uTlo[:, kc * NH:(kc + 1) * NH], ps_w[:, kc * NH:(kc + 1) * NH],
                    nwt[:, kc:kc + 1], uThi[:, kc * NH:(kc + 1) * NH],
                    ALU.mult, ALU.subtract))
        ev_utsplit = sv.n

        def prep_weight(w_dram, whi, wlo, init_v, init_s, init_tr):
            ev_ld = []
            evac_v = [init_v]
            evac_s = [init_s]
            ev_tr_all = {-2: init_tr[0], -1: init_tr[1]}
            for oc in range(NKC):
                @block.sync
                def _(sync, oc=oc):
                    sync.wait_ge(st.h, ev_tr_all[oc - 2])
                    dwt.dma(sync.dma_start(
                        out=wtile[:, (oc % 2) * D:(oc % 2) * D + D],
                        in_=w_dram[oc * 128:(oc + 1) * 128, :]), oc % 2)
                ev_ld.append(dwt.s[oc % 2].n)
                ev_tr = []
                @block.tensor
                def _(tensor, oc=oc):
                    dwt.wait(tensor, oc % 2, ev_ld[oc])
                    tensor.wait_ge(sv.h, evac_v[-1])
                    tensor.wait_ge(ss.h, evac_s[-1])
                    for kc in range(NKC):
                        ev_tr.append(st.inc(tensor.transpose(
                            ps_w[:, kc * 128:(kc + 1) * 128],
                            wtile[:, (oc % 2) * D + kc * 128:(oc % 2) * D + (kc + 1) * 128],
                            eyet[:, :])))
                ev_tr_all[oc] = st.n
                ev_hi = []
                @block.scalar
                def _(scalar, oc=oc):
                    for kc in range(NKC):
                        scalar.wait_ge(st.h, ev_tr[3 if kc < 4 else 7])
                        ev_hi.append(ss.inc(scalar.activation(
                            whi[:, kc * D + oc * 128:kc * D + (oc + 1) * 128],
                            ps_w[:, kc * 128:(kc + 1) * 128],
                            AF.Identity, scale=nwt[:, kc:kc + 1])))
                @block.vector
                def _(vector, oc=oc):
                    for kc in range(NKC):
                        vector.wait_ge(ss.h, ev_hi[3 if kc < 4 else 7])
                        sv.inc(vector.scalar_tensor_tensor(
                            wlo[:, kc * D + oc * 128:kc * D + (oc + 1) * 128],
                            ps_w[:, kc * 128:(kc + 1) * 128],
                            nwt[:, kc:kc + 1],
                            whi[:, kc * D + oc * 128:kc * D + (oc + 1) * 128],
                            ALU.mult, ALU.subtract))
                evac_v.append(sv.n)
                evac_s.append(ss.n)
            return evac_v[-1], evac_s[-1], (ev_tr_all[NKC - 2], ev_tr_all[NKC - 1])

        wf_v, wf_s, wf_tr = prep_weight(wf_d, wfThi, wfTlo, ev_utsplit, ss.n,
                                        (ev_umm, ev_umm))
        ev_pro_v, ev_pro_s, _ = prep_weight(wo_d, woThi, woTlo, wf_v, wf_s, wf_tr)
        pro.close()

        def _finish():
            for stk in reversed(open_stacks):
                stk.close()
            @block.sync
            def _(sync):
                sync.wait_ge(qs.h, qs.n)
                for d_ in all_slotdma:
                    d_.wait_all(sync)
                sync.wait_ge(sv.h, sv.n)
                sync.wait_ge(ss.h, ss.n)
                sync.wait_ge(st.h, st.n)
            @block.gpsimd
            def _(g):
                g.wait_ge(qg.h, qg.n)
                g.wait_ge(sg.h, sg.n)

        if stage <= 0:
            _finish()
            return nc

        p1 = ExitStack()
        open_stacks.append(p1)
        p1e = p1.enter_context
        z_t = p1e(nc.sbuf_tensor("z_t", [128, 2 * 1024], F32))
        r_t = p1e(nc.sbuf_tensor("r_t", [128, 4096], F32))
        x2_t = p1e(nc.sbuf_tensor("x2_t", [128, 4096], F32))
        zThi = p1e(nc.sbuf_tensor("zThi", [128, NKC * TILE], BF16))
        zTlo = p1e(nc.sbuf_tensor("zTlo", [128, NKC * TILE], BF16))
        y_tile = p1e(nc.sbuf_tensor("y_tile", [128, 4096], F32))
        f_b = p1e(nc.sbuf_tensor("f_b", [128, 2 * TILE], F32))
        i_b = p1e(nc.sbuf_tensor("i_b", [128, 2 * TILE], F32))
        o_b = p1e(nc.sbuf_tensor("o_b", [128, 2 * TILE], F32))
        g_b = p1e(nc.sbuf_tensor("g_b", [128, 2 * TILE], F32))
        v_sb = p1e(nc.sbuf_tensor("v_sb", [NH, TILE], F32))
        tmp_t = p1e(nc.sbuf_tensor("tmp_t", [128, 1024], F32))
        trash = p1e(nc.sbuf_tensor("trash", [128, 1024], F32))
        ps_rot = [p1e(nc.psum_tensor("ps_rotA", [128, 1024], F32)),
                  p1e(nc.psum_tensor("ps_rotB", [128, 1024], F32))]
        ps_f = p1e(nc.psum_tensor("ps_f", [128, TILE], F32))
        ps_o = p1e(nc.psum_tensor("ps_o", [128, TILE], F32))
        ps_v = p1e(nc.psum_tensor("ps_v", [NH, TILE], F32))
        ps_vb = p1e(nc.psum_tensor("ps_vb", [128, TILE], F32))

        @block.vector
        def _(vector):
            vector.wait_ge(qs.h, ev_const)
            vector.memset(carry[:, :], 0.0)
            vector.memset(lnmax[:, :], 0.0)
            vector.memset(lnmc[:, :], 0.0)
            vector.memset(hmaxc[:, :], 0.0)
            vector.memset(c23t[:, :], TWO23)
            sv.inc(vector.memset(hmax[:, :], 0.0))

        rot_guard = [None, None]
        rot_idx = [0]

        def rot_next():
            s_ = rot_idx[0] % 2
            rot_idx[0] += 1
            return s_


        ev_z = {}; ev_r = {}; ev_zt = {}; ev_zhi = {}; ev_zlo = {}
        ev_vsb = {}; ev_fsig = {}; ev_osig = {}; ev_g = {}; ev_ymul = {}
        ev_ytile = {}; ev_yt = {}; ev_x2add = {}; ev_x2st = {}
        ev_sq2 = {}; ev_amax4 = {}; ev_cxn = {}; ev_xdone = {}; ev_rs2 = {}
        ev_fproj = {}; ev_oproj = {}; ev_vb = {}; ev_vmm = {}

        for i in range(nt_run + 1):
            par = i % 2
            la = i - 1

            if i + 1 < nt_run:
                @block.sync
                def _(sync, i=i):
                    if i - 1 >= 0:
                        sync.wait_ge(ss.h, ev_xdone[i - 1])
                    dxl.dma(sync.dma_start(out=x_sl((i + 1) % 2),
                                           in_=x_t4[i + 1]), (i + 1) % 2)
                ev_xload.append(dxl.s[(i + 1) % 2].n)

            if la >= 0:
                for tc in range(4):
                    slot = rot_next()
                    @block.tensor
                    def _(tensor, la=la, tc=tc, slot=slot):
                        tensor.wait_ge(sv.h, ev_ytile[la])
                        if rot_guard[slot] is not None:
                            kind, val = rot_guard[slot]
                            tensor.wait_ge(kind.h, val)
                        for oc in range(NH):
                            st.inc(tensor.transpose(
                                ps_rot[slot][:, oc * 128:(oc + 1) * 128],
                                y_tile[:, oc * TILE + tc * 128:oc * TILE + (tc + 1) * 128],
                                eyet[:, :]))
                    ev_yt[(la, tc)] = st.n

                    @block.vector
                    def _(vector, la=la, tc=tc, slot=slot):
                        vector.wait_ge(st.h, ev_yt[(la, tc)])
                        vector.wait_ge(ss.h, ev_r[la])
                        if la >= 1 and tc == 0:
                            vector.wait_ge(ss.h, ev_sq2[la - 1])
                            dx2s.wait(vector, 0, ev_x2st[la - 1])
                        ev_x2add[(la, tc)] = sv.inc(vector.tensor_tensor(
                            x2_t[:, tc * 1024:(tc + 1) * 1024],
                            r_t[:, tc * 1024:(tc + 1) * 1024],
                            ps_rot[slot][:, :], ALU.add))
                    rot_guard[slot] = (sv, ev_x2add[(la, tc)])

                @block.vector
                def _(vector, la=la):
                    ch = Chain(vector, sv)
                    ch(vector.tensor_reduce(
                        S2c(R_M2, 0, 4),
                        x2_t[:, :].rearrange("p (c d) -> p c d", c=4, d=1024),
                        AX, ALU.add))
                    vector.tensor_scalar(S2(R_M2), S2(R_M2), 1.0 / D, None,
                                         ALU.mult)
                    vector.tensor_copy(m2t[:, la * 4:(la + 1) * 4],
                                       S2c(R_M2, 0, 4))
                    for tc in range(4):
                        vector.tensor_scalar(
                            tmp_t[:, :], x2_t[:, tc * 1024:(tc + 1) * 1024],
                            S2c(R_M2, tc), None, ALU.subtract)
                        last = vector.tensor_reduce(
                            S2c(R_AMAX, tc), tmp_t[:, :], AX, ALU.max,
                            apply_absolute_value=True)
                    sv.inc(last)
                ev_amax4[la] = sv.n

                @block.scalar
                def _(scalar, la=la):
                    scalar.wait_ge(sv.h, ev_x2add[(la, 3)])
                    if la >= 1:
                        scalar.wait_ge(sv.h, ev_rs2[la - 1])
                    scalar.memzero(statS[:, 128:132])
                    for tc in range(4):
                        last = scalar.activation(
                            trash[:, :], x2_t[:, tc * 1024:(tc + 1) * 1024],
                            AF.Square, accum_out=statS[:, 128 + tc:129 + tc])
                    ss.inc(last)
                ev_sq2[la] = ss.n

            if i < nt_run:
                @block.scalar
                def _(scalar, i=i, par=par):
                    dxl.wait(scalar, i % 2, ev_xload[i])
                    if i >= 1:
                        scalar.wait_ge(sv.h, ev_cxn[i - 1])
                    scalar.memzero(statS[:, 0:4])
                    for tc in range(4):
                        last = scalar.activation(
                            trash[:, :], x_sl(par)[:, tc * 1024:(tc + 1) * 1024],
                            AF.Square, accum_out=statS[:, tc:tc + 1])
                    ss.inc(last)
                ev_sqx = ss.n

                @block.vector
                def _(vector, i=i, par=par, la=la):
                    ch = Chain(vector, sv)
                    dxl.wait(vector, i % 2, ev_xload[i])
                    ch(vector.tensor_reduce(
                        S2c(R_MEAN, 0, 4),
                        x_sl(par)[:, :].rearrange("p (c d) -> p c d", c=4, d=1024),
                        AX, ALU.add))
                    vector.tensor_scalar(S2(R_MEAN), S2(R_MEAN),
                                         1.0 / D, None, ALU.mult)
                    vector.wait_ge(ss.h, ev_sqx)
                    vector.tensor_scalar(S2(R_VARP), statS[:, 0:128],
                                         1.0 / D, None, ALU.mult)
                    vector.tensor_tensor(S2(R_W2), S2(R_MEAN),
                                         S2(R_MEAN), ALU.mult)
                    vector.tensor_tensor(S2(R_VARP), S2(R_VARP),
                                         S2(R_W2), ALU.subtract)
                    vector.tensor_scalar(S2(R_VARP), S2(R_VARP),
                                         LN_EPS, None, ALU.add)
                    vector.tensor_scalar(S2(R_A2), S2(R_VARP),
                                         1.0 + RMS_EPS, -LN_EPS, ALU.mult, ALU.add)
                    vector.reciprocal(S2(R_R0), S2(R_VARP))
                    last = vector.reciprocal(S2(R_WRK), S2(R_A2))
                    if la >= 0:
                        vector.wait_ge(ss.h, ev_sq2[la])
                        vector.tensor_scalar(S2(R_SQ2), statS[:, 128:256],
                                             1.0 / D, None, ALU.mult)
                        vector.tensor_tensor(S2(R_W2), S2(R_M2),
                                             S2(R_M2), ALU.mult)
                        vector.tensor_tensor(S2(R_SQ2), S2(R_SQ2),
                                             S2(R_W2), ALU.subtract)
                        vector.tensor_scalar(S2(R_SQ2), S2(R_SQ2),
                                             LN_EPS, None, ALU.add)
                        last = vector.reciprocal(S2(R_TMP), S2(R_SQ2))
                    sv.inc(last)
                ev_recip = sv.n

                @block.scalar
                def _(scalar, la=la):
                    scalar.wait_ge(sv.h, ev_recip)
                    scalar.activation(S2(R_R0), S2(R_R0), AF.Sqrt)
                    last = scalar.activation(S2(R_WRK), S2(R_WRK), AF.Sqrt)
                    if la >= 0:
                        last = scalar.activation(S2(R_TMP), S2(R_TMP), AF.Sqrt)
                    ss.inc(last)
                ev_sqrt = ss.n

                @block.vector
                def _(vector, i=i, la=la):
                    ch = Chain(vector, sv)
                    vector.wait_ge(ss.h, ev_sqrt)
                    if la >= 0:
                        vector.tensor_tensor(S2(R_W2), S2(R_TMP),
                                             S2(R_TMP), ALU.mult)
                        vector.tensor_tensor(S2(R_W2), S2(R_W2),
                                             S2(R_SQ2), ALU.mult)
                        vector.tensor_scalar(S2(R_W2), S2(R_W2),
                                             -0.5, 1.5, ALU.mult, ALU.add)
                        vector.tensor_tensor(S2(R_TMP), S2(R_TMP),
                                             S2(R_W2), ALU.mult)
                        vector.tensor_copy(rs2t[:, la * 4:(la + 1) * 4],
                                           S2c(R_TMP, 0, 4))
                        vector.wait_ge(sv.h, ev_amax4[la])
                        vector.tensor_tensor(S2(R_AMAX), S2(R_AMAX),
                                             S2(R_TMP), ALU.mult)
                        sv.inc(vector.tensor_reduce(lnmc[:, la:la + 1],
                                                    S2c(R_AMAX, 0, 4),
                                                    AX, ALU.max))
                        ev_rs2[la] = sv.n
                    vector.tensor_tensor(S2(R_W2), S2(R_R0),
                                         S2(R_R0), ALU.mult)
                    vector.tensor_tensor(S2(R_W2), S2(R_W2),
                                         S2(R_VARP), ALU.mult)
                    vector.tensor_scalar(S2(R_W2), S2(R_W2),
                                         -0.5, 1.5, ALU.mult, ALU.add)
                    vector.tensor_tensor(S2(R_R0), S2(R_R0),
                                         S2(R_W2), ALU.mult)
                    vector.tensor_tensor(S2(R_W2), S2(R_WRK),
                                         S2(R_WRK), ALU.mult)
                    vector.tensor_tensor(S2(R_W2), S2(R_W2),
                                         S2(R_A2), ALU.mult)
                    vector.tensor_scalar(S2(R_W2), S2(R_W2),
                                         -0.5, 1.5, ALU.mult, ALU.add)
                    vector.tensor_tensor(S2(R_WRK), S2(R_WRK),
                                         S2(R_W2), ALU.mult)
                    vector.tensor_tensor(S2(R_A2), S2(R_MEAN),
                                         S2(R_WRK), ALU.mult)
                    vector.tensor_scalar(S2(R_A2), S2(R_A2),
                                         -1.0, None, ALU.mult)
                    vector.tensor_scalar(S2(R_VARP), S2(R_R0),
                                         1.0, None, ALU.add)
                    vector.tensor_tensor(S2(R_W2), S2(R_MEAN),
                                         S2(R_R0), ALU.mult)
                    sv.inc(vector.tensor_scalar(S2(R_W2), S2(R_W2),
                                                -1.0, None, ALU.mult))
                ev_cxn[i] = sv.n

                for tc in range(4):
                    slot = rot_next()
                    @block.scalar
                    def _(scalar, i=i, tc=tc, par=par):
                        scalar.wait_ge(sv.h, ev_cxn[i])
                        zt_prev = ev_zt.get((i, tc - 2)) if tc >= 2 else \
                            (ev_zt.get((i - 1, tc + 2)) if i >= 1 else None)
                        if zt_prev is not None:
                            scalar.wait_ge(st.h, zt_prev)
                        if ev_x2add.get((i - 1, tc)) is not None:
                            scalar.wait_ge(sv.h, ev_x2add[(i - 1, tc)])
                        ss.inc(scalar.activation(
                            z_t[:, (tc % 2) * 1024:(tc % 2) * 1024 + 1024],
                            x_sl(par)[:, tc * 1024:(tc + 1) * 1024],
                            AF.Identity, scale=S2c(R_WRK, tc),
                            bias=S2c(R_A2, tc)))
                        ev_z[(i, tc)] = ss.n
                        ss.inc(scalar.activation(
                            r_t[:, tc * 1024:(tc + 1) * 1024],
                            x_sl(par)[:, tc * 1024:(tc + 1) * 1024],
                            AF.Identity, scale=S2c(R_VARP, tc),
                            bias=S2c(R_W2, tc)))
                        if tc == 3:
                            ev_r[i] = ss.n
                            ev_xdone[i] = ss.n

                    @block.tensor
                    def _(tensor, i=i, tc=tc, slot=slot):
                        if i == 0 and tc == 0:
                            tensor.wait_ge(sv.h, ev_pro_v)
                            tensor.wait_ge(ss.h, ev_pro_s)
                            tensor.wait_ge(sv.h, ev_utsplit)
                        tensor.wait_ge(ss.h, ev_z[(i, tc)])
                        if rot_guard[slot] is not None:
                            kind, val = rot_guard[slot]
                            tensor.wait_ge(kind.h, val)
                        for kc in range(NKC):
                            st.inc(tensor.transpose(
                                ps_rot[slot][:, kc * 128:(kc + 1) * 128],
                                z_t[:, (tc % 2) * 1024 + kc * 128:(tc % 2) * 1024 + (kc + 1) * 128],
                                eyet[:, :]))
                    ev_zt[(i, tc)] = st.n

                    @block.scalar
                    def _(scalar, i=i, tc=tc, slot=slot):
                        scalar.wait_ge(st.h, ev_zt[(i, tc)])
                        if i >= 1 and tc == 0:
                            scalar.wait_ge(st.h, ev_oproj[(i - 1, NH - 1)])
                        ss.inc(scalar.copy(
                            zThi[:, :].rearrange("p (k t) -> p k t", k=NKC, t=TILE)[:, :, tc * 128:(tc + 1) * 128],
                            ps_rot[slot][:, :].rearrange("p (k b) -> p k b", k=NKC, b=128)))
                    ev_zhi[(i, tc)] = ss.n

                    @block.vector
                    def _(vector, i=i, tc=tc, slot=slot):
                        vector.wait_ge(st.h, ev_zt[(i, tc)])
                        vector.wait_ge(ss.h, ev_zhi[(i, tc)])
                        if i >= 1 and tc == 0:
                            vector.wait_ge(st.h, ev_oproj[(i - 1, NH - 1)])
                        sv.inc(vector.tensor_tensor(
                            zTlo[:, :].rearrange("p (k t) -> p k t", k=NKC, t=TILE)[:, :, tc * 128:(tc + 1) * 128],
                            ps_rot[slot][:, :].rearrange("p (k b) -> p k b", k=NKC, b=128),
                            zThi[:, :].rearrange("p (k t) -> p k t", k=NKC, t=TILE)[:, :, tc * 128:(tc + 1) * 128],
                            ALU.subtract))
                    ev_zlo[(i, tc)] = sv.n
                    rot_guard[slot] = (sv, ev_zlo[(i, tc)])

                    @block.tensor
                    def _(tensor, i=i, tc=tc):
                        tensor.wait_ge(ss.h, ev_zhi[(i, tc)])
                        tensor.wait_ge(sv.h, ev_zlo[(i, tc)])
                        if tc == 0 and i > 0:
                            tensor.wait_ge(sv.h, ev_vsb[i - 1])
                        n = 0
                        for kc in range(NKC):
                            zh = zThi[:, kc * TILE + tc * 128:kc * TILE + (tc + 1) * 128]
                            zl = zTlo[:, kc * TILE + tc * 128:kc * TILE + (tc + 1) * 128]
                            uh = uThi[:, kc * NH:(kc + 1) * NH]
                            ul = uTlo[:, kc * NH:(kc + 1) * NH]
                            for (wop, zop) in ((uh, zh), (ul, zh), (uh, zl)):
                                st.inc(tensor.matmul(
                                    ps_v[:, tc * 128:(tc + 1) * 128], wop, zop,
                                    start=(n == 0), stop=(n == 3 * NKC - 1)))
                                n += 1
                    ev_vmm[(i, tc)] = st.n

                @block.vector
                def _(vector, i=i):
                    vector.wait_ge(st.h, ev_vmm[(i, 3)])
                    sv.inc(vector.tensor_copy(v_sb[:, :], ps_v[:, :]))
                ev_vsb[i] = sv.n

                for oc in range(NH):
                    op = oc % 2

                    @block.tensor
                    def _(tensor, i=i, oc=oc):
                        prev = ev_fsig.get((i, oc - 1)) if oc > 0 else \
                            ev_fsig.get((i - 1, NH - 1))
                        if prev is not None:
                            tensor.wait_ge(ss.h, prev)
                        n = 0
                        for kc in range(NKC):
                            zh = zThi[:, kc * TILE:(kc + 1) * TILE]
                            zl = zTlo[:, kc * TILE:(kc + 1) * TILE]
                            wh = wfThi[:, kc * D + oc * 128:kc * D + (oc + 1) * 128]
                            wl = wfTlo[:, kc * D + oc * 128:kc * D + (oc + 1) * 128]
                            for (wop, zop) in ((wh, zh), (wl, zh), (wh, zl)):
                                st.inc(tensor.matmul(
                                    ps_f[:, :], wop, zop,
                                    start=(n == 0), stop=(n == 3 * NKC - 1)))
                                n += 1
                    ev_fproj[(i, oc)] = st.n

                    @block.scalar
                    def _(scalar, i=i, oc=oc, op=op):
                        scalar.wait_ge(st.h, ev_fproj[(i, oc)])
                        prevy = ev_ymul.get((i, oc - 2)) if oc >= 2 else \
                            (ev_ymul.get((i - 1, NH - 2 + oc)) if i >= 1 else None)
                        if prevy is not None:
                            scalar.wait_ge(sv.h, prevy)
                        ss.inc(scalar.activation(
                            f_b[:, op * TILE:(op + 1) * TILE], ps_f[:, :], AF.Sigmoid))
                    ev_fsig[(i, oc)] = ss.n

                    @block.tensor
                    def _(tensor, i=i, oc=oc):
                        prev = ev_osig.get((i, oc - 1)) if oc > 0 else \
                            ev_osig.get((i - 1, NH - 1))
                        if prev is not None:
                            tensor.wait_ge(ss.h, prev)
                        n = 0
                        for kc in range(NKC):
                            zh = zThi[:, kc * TILE:(kc + 1) * TILE]
                            zl = zTlo[:, kc * TILE:(kc + 1) * TILE]
                            wh = woThi[:, kc * D + oc * 128:kc * D + (oc + 1) * 128]
                            wl = woTlo[:, kc * D + oc * 128:kc * D + (oc + 1) * 128]
                            for (wop, zop) in ((wh, zh), (wl, zh), (wh, zl)):
                                st.inc(tensor.matmul(
                                    ps_o[:, :], wop, zop,
                                    start=(n == 0), stop=(n == 3 * NKC - 1)))
                                n += 1
                        ev_oproj[(i, oc)] = st.n
                        tensor.wait_ge(sv.h, ev_vsb[i])
                        prevg = ev_g.get((i, oc - 1)) if oc > 0 else \
                            ev_g.get((i - 1, NH - 1))
                        if prevg is not None:
                            tensor.wait_ge(sv.h, prevg)
                        st.inc(tensor.matmul(
                            ps_vb[:, :], selbt[:, oc * 128:(oc + 1) * 128],
                            v_sb[:, :], start=True, stop=True))
                    ev_vb[(i, oc)] = st.n

                    @block.scalar
                    def _(scalar, i=i, oc=oc, op=op):
                        scalar.wait_ge(st.h, ev_oproj[(i, oc)])
                        prevy = ev_ymul.get((i, oc - 2)) if oc >= 2 else \
                            (ev_ymul.get((i - 1, NH - 2 + oc)) if i >= 1 else None)
                        if prevy is not None:
                            scalar.wait_ge(sv.h, prevy)
                        ss.inc(scalar.activation(
                            o_b[:, op * TILE:(op + 1) * TILE], ps_o[:, :], AF.Sigmoid))
                    ev_osig[(i, oc)] = ss.n

                    @block.vector
                    def _(vector, i=i, oc=oc, op=op):
                        fsl = f_b[:, op * TILE:(op + 1) * TILE]
                        isl = i_b[:, op * TILE:(op + 1) * TILE]
                        gsl = g_b[:, op * TILE:(op + 1) * TILE]
                        osl = o_b[:, op * TILE:(op + 1) * TILE]
                        if oc == 0 and i >= 1 and (i - 1, 3) in ev_yt:
                            vector.wait_ge(st.h, ev_yt[(i - 1, 3)])
                        vector.wait_ge(ss.h, ev_fsig[(i, oc)])
                        vector.tensor_scalar(fsl, fsl, FORGET_LB, None, ALU.max)
                        vector.tensor_scalar(isl, fsl, -1.0, 1.0, ALU.mult, ALU.add)
                        vector.wait_ge(st.h, ev_vb[(i, oc)])
                        ev_g[(i, oc)] = sv.inc(vector.tensor_tensor(
                            gsl, isl, ps_vb[:, :], ALU.mult))
                        sv.inc(vector.tensor_tensor_scan(
                            fsl, fsl, gsl, carry[:, oc:oc + 1], ALU.mult, ALU.add))
                        vector.wait_ge(sv.h, sv.n)
                        vector.tensor_copy(carry[:, oc:oc + 1],
                                           fsl[:, TILE - 1:TILE])
                        vector.wait_ge(ss.h, ev_osig[(i, oc)])
                        sv.inc(vector.tensor_tensor(
                            y_tile[:, oc * TILE:(oc + 1) * TILE], osl, fsl,
                            ALU.mult))
                    ev_ymul[(i, oc)] = sv.n
                ev_ytile[i] = ev_ymul[(i, NH - 1)]

            if i == nt_run and la >= 0:
                @block.vector
                def _(vector, la=la):
                    vector.wait_ge(ss.h, ev_sq2[la])
                    vector.tensor_scalar(S2(R_SQ2), statS[:, 128:256],
                                         1.0 / D, None, ALU.mult)
                    vector.tensor_tensor(S2(R_W2), S2(R_M2),
                                         S2(R_M2), ALU.mult)
                    vector.tensor_tensor(S2(R_SQ2), S2(R_SQ2),
                                         S2(R_W2), ALU.subtract)
                    vector.tensor_scalar(S2(R_SQ2), S2(R_SQ2),
                                         LN_EPS, None, ALU.add)
                    sv.inc(vector.reciprocal(S2(R_TMP), S2(R_SQ2)))
                ev_rc2 = sv.n

                @block.scalar
                def _(scalar):
                    scalar.wait_ge(sv.h, ev_rc2)
                    ss.inc(scalar.activation(S2(R_TMP), S2(R_TMP), AF.Sqrt))
                ev_sq2s = ss.n

                @block.vector
                def _(vector, la=la):
                    ch = Chain(vector, sv)
                    vector.wait_ge(ss.h, ev_sq2s)
                    vector.tensor_tensor(S2(R_W2), S2(R_TMP),
                                         S2(R_TMP), ALU.mult)
                    vector.tensor_tensor(S2(R_W2), S2(R_W2),
                                         S2(R_SQ2), ALU.mult)
                    vector.tensor_scalar(S2(R_W2), S2(R_W2),
                                         -0.5, 1.5, ALU.mult, ALU.add)
                    vector.tensor_tensor(S2(R_TMP), S2(R_TMP),
                                         S2(R_W2), ALU.mult)
                    vector.tensor_copy(rs2t[:, la * 4:(la + 1) * 4],
                                       S2c(R_TMP, 0, 4))
                    vector.wait_ge(sv.h, ev_amax4[la])
                    vector.tensor_tensor(S2(R_AMAX), S2(R_AMAX),
                                         S2(R_TMP), ALU.mult)
                    sv.inc(vector.tensor_reduce(lnmc[:, la:la + 1],
                                                S2c(R_AMAX, 0, 4),
                                                AX, ALU.max))
                ev_rs2[la] = sv.n

            if la >= 0:
                @block.sync
                def _(sync, la=la):
                    sync.wait_ge(sv.h, ev_rs2[la])
                    sync.wait_ge(ss.h, ev_sq2[la])
                    dx2s.dma(sync.dma_start(out=x2_t4[la], in_=x2_t[:, :]), 0)
                    if dbg:
                        qs.dma(sync.dma_start(out=y2_t4[la], in_=x2_t[:, :]))
                ev_x2st[la] = dx2s.s[0].n

        if stage <= 1:
            if dbg == 2:
                dmp = {}
                for nm, tile_ in (("d_r", r_t), ("d_y", y_tile), ("d_x2", x2_t),
                                  ("d_f", f_b), ("d_o", o_b), ("d_g", g_b),
                                  ("d_stat", stat), ("d_m2", m2t), ("d_rs2", rs2t),
                                  ("d_carry", carry), ("d_zhi", zThi),
                                  ("d_uhi", uThi), ("d_ulo", uTlo)):
                    shp = [tile_.shape[0], tile_.shape[1]]
                    dt_ = tile_.dtype
                    dmp[nm] = (nc.dram_tensor(nm, shp, dt_, kind="ExternalOutput").ap(), tile_)
                vdmp = nc.dram_tensor("d_v", [NH, TILE], F32, kind="ExternalOutput").ap()

                @block.sync
                def _(sync):
                    sync.wait_ge(sv.h, sv.n)
                    sync.wait_ge(ss.h, ss.n)
                    sync.wait_ge(st.h, st.n)
                    for nm, (ap_, tile_) in dmp.items():
                        qs.dma(sync.dma_start(out=ap_, in_=tile_[:, :]))
                    qs.dma(sync.dma_start(out=vdmp, in_=v_sb[:, :]))
            _finish()
            return nc

        @block.vector
        def _(vector):
            ch = Chain(vector, sv)
            vector.wait_ge(sv.h, ev_rs2[nt_run - 1])
            ch(vector.tensor_reduce(lnmax[:, 0:1], lnmc[:, 0:nt_run],
                                    AX, ALU.max))
        ev_lnf = sv.n

        @block.tensor
        def _(tensor):
            tensor.wait_ge(sv.h, ev_lnf)
            st.inc(tensor.transpose(ps_v[0:1, 0:128], lnmax[:, 0:1], eyet[:, :]))
        ev_lt = st.n

        @block.vector
        def _(vector):
            vector.wait_ge(st.h, ev_lt)
            sv.inc(vector.tensor_copy(ccrow[0:1, :], ps_v[0:1, 0:128]))
        ev_lrow = sv.n

        @block.scalar
        def _(scalar):
            scalar.wait_ge(qs.h, ev_const)
            scalar.activation(sgc[0:1, 2:3], sgc[0:1, 0:1], AF.Exp, scale=-1.0)
            ss.inc(scalar.activation(sgc[0:1, 3:4], sgc[0:1, 1:2], AF.Exp, scale=-1.0))
        ev_exp = ss.n

        @block.gpsimd
        def _(g):
            g.wait_ge(sv.h, ev_lrow)
            qg.dma(g.dma_start(out=ccxi_d.ap(), in_=ccrow[0:1, :]))
            g.wait_ge(qg.h, qg.n)
            scc.inc(g.collective_compute(
                "AllReduce", ALU.max, replica_groups=[list(range(8))],
                ins=[ccxi_d.ap()], outs=[ccxo_d.ap()]))
            g.wait_ge(scc.h, scc.n)
            qg.dma(g.dma_start(out=ccrow[0:1, :], in_=ccxo_d.ap()))
        ev_ccx = qg.n

        @block.vector
        def _(vector):
            ch = Chain(vector, sv)
            vector.wait_ge(ss.h, ev_exp)
            ch(vector.tensor_scalar(sgc[0:1, 2:4], sgc[0:1, 2:4], 1.0, None, ALU.add))
            ch(vector.reciprocal(sgc[0:1, 2:4], sgc[0:1, 2:4]))
            vector.wait_ge(qg.h, ev_ccx)
            ch(vector.tensor_reduce(sgc[0:1, 4:5], ccrow[0:1, :], AX, ALU.max))
            ch(vector.tensor_tensor(sgc[0:1, 4:5], sgc[0:1, 4:5],
                                    sgc[0:1, 2:3], ALU.mult))
        ev_gm = sv.n

        @block.tensor
        def _(tensor):
            tensor.wait_ge(sv.h, ev_gm)
            st.inc(tensor.matmul(ps_vb[:, 0:1], ones1t[:, :], sgc[0:1, 4:5],
                                 start=True, stop=True))
        ev_gb = st.n

        @block.vector
        def _(vector):
            ch = Chain(vector, sv)
            vector.wait_ge(st.h, ev_gb)
            ch(vector.tensor_scalar(sxv[:, 1:2], ps_vb[:, 0:1], 1.0 / QMAX,
                                    None, ALU.mult))
            ch(vector.reciprocal(sxv[:, 0:1], sxv[:, 1:2]))
            ch(vector.tensor_scalar(qsc[:, :], rs2t[:, :], sxv[:, 0:1], None,
                                    ALU.mult))
            ch(vector.tensor_tensor(qbi[:, :], m2t[:, :], qsc[:, :], ALU.mult))
            ch(vector.tensor_scalar(qbi[:, :], qbi[:, :], -1.0, None, ALU.mult))
        ev_qsc = sv.n

        p1.close(); open_stacks.remove(p1)
        p1w.close(); open_stacks.remove(p1w)

        if stage <= 2:
            _finish()
            return nc

        p2c = ExitStack()
        open_stacks.append(p2c)
        wdT = p2c.enter_context(nc.sbuf_tensor("wdT", [128, NIC * 1024], BF16))

        p2 = ExitStack()
        open_stacks.append(p2)
        p2e = p2.enter_context
        xqT = p2e(nc.sbuf_tensor("xqT", [128, NKC * T], BF16))
        wdst = p2e(nc.sbuf_tensor("wdst", [128, NI], F32))
        wdqn = p2e(nc.sbuf_tensor("wdqn", [128, NI], BF16))
        wgst = p2e(nc.sbuf_tensor("wgst", [128, 2 * D], F32))
        wust = p2e(nc.sbuf_tensor("wust", [128, 2 * D], F32))
        wgqn = p2e(nc.sbuf_tensor("wgqn", [128, 2 * D], BF16))
        wuqn = p2e(nc.sbuf_tensor("wuqn", [128, 2 * D], BF16))
        wgqT = p2e(nc.sbuf_tensor("wgqT", [128, 2 * D], BF16))
        wuqT = p2e(nc.sbuf_tensor("wuqT", [128, 2 * D], BF16))
        q_s = p2e(nc.sbuf_tensor("q_s", [128, 2 * D], F32))
        xqn = p2e(nc.sbuf_tensor("xqn", [128, 2 * D], BF16))
        wrnd = p2e(nc.sbuf_tensor("wrnd", [128, 2 * D], F32))
        gsil = p2e(nc.sbuf_tensor("gsil", [128, 2 * TG], F32))
        h_t = p2e(nc.sbuf_tensor("h_t", [128, 2 * TG], F32))
        mgt = p2e(nc.sbuf_tensor("mgt", [128, NIC], F32))
        mut = p2e(nc.sbuf_tensor("mut", [128, NIC], F32))
        ps_gate = [p2e(nc.psum_tensor("ps_gateA", [128, TG], F32))]
        ps_up = p2e(nc.psum_tensor("ps_up", [128, TG], F32))
        ps_wt = p2e(nc.psum_tensor("ps_wt", [128, 1024], BF16))
        ps_wtg = p2e(nc.psum_tensor("ps_wtg", [128, 1024], BF16))
        ps_gate.append(p2e(nc.psum_tensor("ps_gateB", [128, TG], F32)))

        PH1_SS, PH1_SV, PH1_ST, PH1_SG = ss.n, sv.n, st.n, sg.n

        ev_x2ld = {}; ev_aff = {}; ev_xq = {}; ev_xqtr = {}; ev_xqevac = {}
        for tc in range(NTC):
            parq = tc % 2
            @block.sync
            def _(sync, tc=tc, parq=parq):
                if tc == 0:
                    sync.wait_ge(sv.h, ev_qsc)
                    sync.wait_ge(ss.h, PH1_SS)
                    sync.wait_ge(sg.h, PH1_SG)
                    sync.wait_ge(st.h, PH1_ST)
                    dx2s.wait(sync, 0, ev_x2st[nt_run - 1])
                if tc >= 2:
                    sync.wait_ge(sv.h, ev_xq[tc - 2])
                dqx.dma(sync.dma_start(out=q_s[:, parq * D:(parq + 1) * D],
                                       in_=x2_tc[tc]), parq)
            ev_x2ld[tc] = dqx.s[parq].n

            @block.vector
            def _(vector, tc=tc, parq=parq):
                dqx.wait(vector, tc % 2, ev_x2ld[tc])
                if tc >= 2:
                    vector.wait_ge(st.h, ev_xqtr[tc - 2])
                vector.tensor_scalar(q_s[:, parq * D:(parq + 1) * D],
                                     q_s[:, parq * D:(parq + 1) * D],
                                     qsc[:, tc:tc + 1], qbi[:, tc:tc + 1],
                                     ALU.mult, ALU.add)
                vector.tensor_scalar(q_s[:, parq * D:(parq + 1) * D],
                                     q_s[:, parq * D:(parq + 1) * D],
                                     TWO23, -TWO23, ALU.add, ALU.add)
                sv.inc(vector.tensor_scalar(
                    xqn[:, parq * D:(parq + 1) * D],
                    q_s[:, parq * D:(parq + 1) * D],
                    QMAX, -QMAX - 1.0, ALU.min, ALU.max))
            ev_xq[tc] = sv.n

            @block.tensor
            def _(tensor, tc=tc, parq=parq):
                tensor.wait_ge(sv.h, ev_xq[tc])
                if tc >= 1:
                    tensor.wait_ge(ss.h, ev_xqevac[tc - 1])
                for kc in range(NKC):
                    st.inc(tensor.transpose(
                        ps_wt[:, kc * 128:(kc + 1) * 128],
                        xqn[:, parq * D + kc * 128:parq * D + (kc + 1) * 128],
                        eyebt[:, :]))
            ev_xqtr[tc] = st.n

            @block.scalar
            def _(scalar, tc=tc):
                scalar.wait_ge(st.h, ev_xqtr[tc])
                ss.inc(scalar.copy(
                    xqT[:, :].rearrange("p (k t) -> p k t", k=NKC, t=T)[:, :, tc * 128:(tc + 1) * 128],
                    ps_wt[:, :].rearrange("p (k b) -> p k b", k=NKC, b=128)))
            ev_xqevac[tc] = ss.n
        ev_xqT_done = ev_xqevac[NTC - 1]

        if stage <= 3:
            _finish()
            return nc

        ev_wdld = {}; ev_wdam = {}; ev_wdr = {}; ev_wdq = {}
        ev_wdtr = {}; ev_wdevac = {}; ev_wdafs = {}; ev_wdq_part = [0]
        for hc in range(NKC):
            @block.sync
            def _(sync, hc=hc):
                if hc >= 1:
                    sync.wait_ge(sv.h, ev_wdafs[hc - 1])
                dwd.dma(sync.dma_start(out=wdst[:, :],
                                       in_=wd_d[hc * 128:(hc + 1) * 128, :]), 0)
            ev_wdld[hc] = dwd.s[0].n

            @block.vector
            def _(vector, hc=hc):
                dwd.wait(vector, 0, ev_wdld[hc])
                chd = Chain(vector, sv)
                chd(vector.tensor_reduce(S2c(WD_AM, 0), wdst[:, :], AX, ALU.max,
                                         apply_absolute_value=True))
                vector.tensor_scalar(S2(WD_SW), S2(WD_AM),
                                     1.0 / QMAX, None, ALU.mult)
                vector.tensor_copy(swdt[:, hc:hc + 1], S2c(WD_SW, 0))
                sv.inc(vector.reciprocal(S2(WD_R), S2(WD_SW)))
            ev_wdr[hc] = sv.n

            for cofs, clen in ((0, 2048), (2048, NI - 2048)):
                @block.vector
                def _(vector, hc=hc, cofs=cofs, clen=clen):
                    if hc >= 1 and cofs == 0:
                        vector.wait_ge(st.h, ev_wdtr[hc - 1])
                    vector.tensor_scalar(wrnd[:, 0:clen], wdst[:, cofs:cofs + clen],
                                         S2c(WD_R, 0), TWO23, ALU.mult, ALU.add)
                    vector.tensor_scalar(wrnd[:, 0:clen], wrnd[:, 0:clen],
                                         -TWO23, QMAX, ALU.add, ALU.min)
                    sv.inc(vector.tensor_scalar(
                        wdqn[:, cofs:cofs + clen], wrnd[:, 0:clen],
                        -QMAX - 1.0, None, ALU.max))
                ev_wdq_part.append(sv.n)
            ev_wdq[hc] = sv.n
            ev_wdafs[hc] = sv.n

            for grp in range(3):
                lo = grp * 8
                hi_ = min(lo + 8, NIC)
                @block.tensor
                def _(tensor, hc=hc, lo=lo, hi_=hi_, grp=grp):
                    tensor.wait_ge(sv.h, ev_wdq[hc])
                    if hc == 0 and grp == 0:
                        tensor.wait_ge(ss.h, ev_xqT_done)
                    if grp >= 1:
                        tensor.wait_ge(ss.h, ev_wdevac[(hc, grp - 1)])
                    elif hc >= 1:
                        tensor.wait_ge(ss.h, ev_wdevac[(hc - 1, 2)])
                    for icc in range(lo, hi_):
                        st.inc(tensor.transpose(
                            ps_wt[:, (icc - lo) * 128:(icc - lo + 1) * 128],
                            wdqn[:, icc * 128:(icc + 1) * 128],
                            eyebt[:, :]))
                ev_wdtr_g = st.n

                @block.scalar
                def _(scalar, hc=hc, lo=lo, hi_=hi_, ev=ev_wdtr_g):
                    scalar.wait_ge(st.h, ev)
                    for icc in range(lo, hi_):
                        last = scalar.copy(
                            wdT[:, icc * 1024 + hc * 128:icc * 1024 + (hc + 1) * 128],
                            ps_wt[:, (icc - lo) * 128:(icc - lo + 1) * 128])
                    ss.inc(last)
                ev_wdevac[(hc, grp)] = ss.n
            ev_wdtr[hc] = st.n
        ev_wd_done = ev_wdevac[(NKC - 1, 2)]

        if stage <= 4:
            _finish()
            return nc

        ev_wgld = {}; ev_am2 = {}; ev_r2 = {}; ev_wgq = {}; ev_wgafs = {}
        ev_wgtr = {}; ev_wgte1 = {}; ev_wutr = {}; ev_wgte = {}
        ev_gmm = {}; ev_umm = {}; ev_sil = {}; ev_hmul = {}
        ev_hmax = {}; ev_hst = {}
        for ic in range(NIC):
            parw = ic % 2
            @block.sync
            def _(sync, ic=ic, parw=parw):
                if ic >= 2:
                    sync.wait_ge(sv.h, ev_wgafs[ic - 2])
                dwg.dma(sync.dma_start(out=wgst[:, parw * D:(parw + 1) * D],
                                       in_=wg_d[ic * 128:(ic + 1) * 128, :]), parw)
                dwg.dma(sync.dma_start(out=wust[:, parw * D:(parw + 1) * D],
                                       in_=wu_d[ic * 128:(ic + 1) * 128, :]), parw)
            ev_wgld[ic] = dwg.s[parw].n

            @block.vector
            def _(vector, ic=ic, parw=parw):
                dwg.wait(vector, ic % 2, ev_wgld[ic])
                chw = Chain(vector, sv)
                vector.tensor_reduce(S2c(WG_AM, 0),
                                     wgst[:, parw * D:(parw + 1) * D],
                                     AX, ALU.max, apply_absolute_value=True)
                chw(vector.tensor_reduce(S2c(WU_AM, 0),
                                         wust[:, parw * D:(parw + 1) * D],
                                         AX, ALU.max, apply_absolute_value=True))
                vector.tensor_scalar(S2(WG_SW), S2(WG_AM),
                                     1.0 / QMAX, None, ALU.mult)
                vector.tensor_scalar(S2(WU_SW), S2(WU_AM),
                                     1.0 / QMAX, None, ALU.mult)
                vector.tensor_copy(swgt[:, ic:ic + 1], S2c(WG_SW, 0))
                vector.tensor_copy(swut[:, ic:ic + 1], S2c(WU_SW, 0))
                vector.reciprocal(S2(WG_R), S2(WG_SW))
                vector.reciprocal(S2(WU_R), S2(WU_SW))
                vector.tensor_scalar(S2(WG_MU), S2(WG_SW),
                                     sxv[:, 1:2], None, ALU.mult)
                vector.tensor_copy(mgt[:, ic:ic + 1], S2c(WG_MU, 0))
                vector.tensor_scalar(S2(WG_MU), S2(WU_SW),
                                     sxv[:, 1:2], None, ALU.mult)
                sv.inc(vector.tensor_copy(mut[:, ic:ic + 1], S2c(WG_MU, 0)))
            ev_r2[ic] = sv.n

            @block.vector
            def _(vector, ic=ic, parw=parw):
                if ic >= 2:
                    vector.wait_ge(st.h, ev_wutr[ic - 2])
                for half, wst_, rr, wqn in ((0, wgst, WG_R, wgqn),
                                            (1, wust, WU_R, wuqn)):
                    vector.tensor_scalar(wrnd[:, half * D:(half + 1) * D],
                                         wst_[:, parw * D:(parw + 1) * D],
                                         S2c(rr, 0), TWO23, ALU.mult, ALU.add)
                    vector.tensor_scalar(wrnd[:, half * D:(half + 1) * D],
                                         wrnd[:, half * D:(half + 1) * D],
                                         -TWO23, QMAX, ALU.add, ALU.min)
                    sv.inc(vector.tensor_scalar(
                        wqn[:, parw * D:(parw + 1) * D],
                        wrnd[:, half * D:(half + 1) * D],
                        -QMAX - 1.0, None, ALU.max))
            ev_wgq[ic] = sv.n
            ev_wgafs[ic] = sv.n

            @block.tensor
            def _(tensor, ic=ic, parw=parw):
                tensor.wait_ge(sv.h, ev_wgq[ic])
                if ic >= 2:
                    tensor.wait_ge(sv.h, ev_wgte[ic - 2])
                for kc in range(NKC):
                    st.inc(tensor.transpose(
                        ps_wtg[:, kc * 128:(kc + 1) * 128],
                        wgqn[:, parw * D + kc * 128:parw * D + (kc + 1) * 128],
                        eyebt[:, :]))
            ev_wgtr[ic] = st.n

            @block.vector
            def _(vector, ic=ic, parw=parw):
                vector.wait_ge(st.h, ev_wgtr[ic])
                sv.inc(vector.tensor_copy(wgqT[:, parw * D:(parw + 1) * D],
                                          ps_wtg[:, :]))
            ev_wgte1[ic] = sv.n

            @block.tensor
            def _(tensor, ic=ic, parw=parw):
                tensor.wait_ge(sv.h, ev_wgte1[ic])
                for kc in range(NKC):
                    st.inc(tensor.transpose(
                        ps_wtg[:, kc * 128:(kc + 1) * 128],
                        wuqn[:, parw * D + kc * 128:parw * D + (kc + 1) * 128],
                        eyebt[:, :]))
            ev_wutr[ic] = st.n

            @block.vector
            def _(vector, ic=ic, parw=parw):
                vector.wait_ge(st.h, ev_wutr[ic])
                sv.inc(vector.tensor_copy(wuqT[:, parw * D:(parw + 1) * D],
                                          ps_wtg[:, :]))
            ev_wgte[ic] = sv.n

            for ts_ in range(NTG):
                parh = ts_ % 2

                @block.tensor
                def _(tensor, ic=ic, ts_=ts_, parw=parw):
                    tensor.wait_ge(sv.h, ev_wgte[ic])
                    tensor.wait_ge(ss.h, ev_xqevac[(ts_ + 1) * NKC - 1])
                    pk = (ic, ts_ - 2) if ts_ >= 2 else \
                        ((ic - 1, NTG - 2 + ts_) if ic > 0 else None)
                    if pk is not None and pk in ev_sil:
                        tensor.wait_ge(ss.h, ev_sil[pk])
                    for kc in range(NKC):
                        for half in range(2):
                            st.inc(tensor.matmul(
                                ps_gate[ts_ % 2][:, half * 512:(half + 1) * 512],
                                wgqT[:, parw * D + kc * 128:parw * D + (kc + 1) * 128],
                                xqT[:, kc * T + ts_ * TG + half * 512:kc * T + ts_ * TG + (half + 1) * 512],
                                start=(kc == 0), stop=(kc == NKC - 1)))
                    ev_gmm[(ic, ts_)] = st.n
                    if pk in ev_hmul:
                        tensor.wait_ge(sv.h, ev_hmul[pk])
                    for kc in range(NKC):
                        for half in range(2):
                            st.inc(tensor.matmul(
                                ps_up[:, half * 512:(half + 1) * 512],
                                wuqT[:, parw * D + kc * 128:parw * D + (kc + 1) * 128],
                                xqT[:, kc * T + ts_ * TG + half * 512:kc * T + ts_ * TG + (half + 1) * 512],
                                start=(kc == 0), stop=(kc == NKC - 1)))
                ev_umm[(ic, ts_)] = st.n

                @block.scalar
                def _(scalar, ic=ic, ts_=ts_, parh=parh):
                    scalar.wait_ge(st.h, ev_gmm[(ic, ts_)])
                    pk = (ic, ts_ - 2) if ts_ >= 2 else \
                        ((ic - 1, NTG - 2 + ts_) if ic > 0 else None)
                    if pk is not None and pk in ev_hmul:
                        scalar.wait_ge(sv.h, ev_hmul[pk])
                    ss.inc(scalar.activation(
                        gsil[:, parh * TG:(parh + 1) * TG],
                        ps_gate[ts_ % 2][:, :],
                        AF.Silu, scale=mgt[:, ic:ic + 1]))
                ev_sil[(ic, ts_)] = ss.n

                @block.vector
                def _(vector, ic=ic, ts_=ts_, parh=parh):
                    vector.wait_ge(st.h, ev_umm[(ic, ts_)])
                    vector.wait_ge(ss.h, ev_sil[(ic, ts_)])
                    pk = (ic, ts_ - 2) if ts_ >= 2 else \
                        ((ic - 1, NTG - 2 + ts_) if ic > 0 else None)
                    if pk is not None and pk in ev_hst:
                        dhs.wait(vector, ts_ % 2, ev_hst[pk])
                    sv.inc(vector.scalar_tensor_tensor(
                        h_t[:, parh * TG:(parh + 1) * TG], ps_up[:, :],
                        mut[:, ic:ic + 1], gsil[:, parh * TG:(parh + 1) * TG],
                        ALU.mult, ALU.mult))
                ev_hmul[(ic, ts_)] = sv.n

                @block.vector
                def _(vector, ic=ic, ts_=ts_, parh=parh):
                    sv.inc(vector.tensor_reduce(
                        hmaxc[:, ic * NTG + ts_:ic * NTG + ts_ + 1],
                        h_t[:, parh * TG:(parh + 1) * TG],
                        AX, ALU.max, apply_absolute_value=True))
                ev_hmax[(ic, ts_)] = sv.n

                @block.gpsimd
                def _(g, ic=ic, ts_=ts_, parh=parh):
                    g.wait_ge(sv.h, ev_hmul[(ic, ts_)])
                    dhs.dma(g.dma_start(
                        out=h_ic[ic][:, ts_ * TG:(ts_ + 1) * TG],
                        in_=h_t[:, parh * TG:(parh + 1) * TG]), parh)
                ev_hst[(ic, ts_)] = dhs.s[parh].n

        if stage <= 5:
            _finish()
            return nc

        @block.vector
        def _(vector):
            ch = Chain(vector, sv)
            vector.wait_ge(sv.h, ev_hmax[(NIC - 1, NTG - 1)])
            ch(vector.tensor_reduce(hmax[:, 0:1], hmaxc[:, :], AX, ALU.max))
        ev_hmf = sv.n

        @block.tensor
        def _(tensor):
            tensor.wait_ge(sv.h, ev_hmf)
            st.inc(tensor.transpose(ps_up[0:1, 0:128], hmax[:, 0:1], eyet[:, :]))
        ev_ht = st.n

        @block.vector
        def _(vector):
            vector.wait_ge(st.h, ev_ht)
            sv.inc(vector.tensor_copy(ccrow[0:1, :], ps_up[0:1, 0:128]))
        ev_hrow = sv.n

        @block.gpsimd
        def _(g):
            g.wait_ge(sv.h, ev_hrow)
            qg.dma(g.dma_start(out=cchi_d.ap(), in_=ccrow[0:1, :]))
            g.wait_ge(qg.h, qg.n)
            scc.inc(g.collective_compute(
                "AllReduce", ALU.max, replica_groups=[list(range(8))],
                ins=[cchi_d.ap()], outs=[ccho_d.ap()]))
            g.wait_ge(scc.h, scc.n)
            qg.dma(g.dma_start(out=ccrow[0:1, :], in_=ccho_d.ap()))
        ev_cch = qg.n

        @block.vector
        def _(vector):
            ch = Chain(vector, sv)
            vector.wait_ge(qg.h, ev_cch)
            ch(vector.tensor_reduce(sgc[0:1, 5:6], ccrow[0:1, :], AX, ALU.max))
            ch(vector.tensor_tensor(sgc[0:1, 5:6], sgc[0:1, 5:6],
                                    sgc[0:1, 3:4], ALU.mult))
        ev_gh = sv.n

        @block.tensor
        def _(tensor):
            tensor.wait_ge(sv.h, ev_gh)
            st.inc(tensor.matmul(ps_up[:, 0:1], ones1t[:, :], sgc[0:1, 5:6],
                                 start=True, stop=True))
        ev_ghb = st.n

        @block.vector
        def _(vector):
            ch = Chain(vector, sv)
            vector.wait_ge(st.h, ev_ghb)
            ch(vector.tensor_scalar(shv[:, 1:2], ps_up[:, 0:1], 1.0 / QMAX,
                                    None, ALU.mult))
            ch(vector.reciprocal(shv[:, 0:1], shv[:, 1:2]))
            ch(vector.tensor_scalar(swdt[:, :], swdt[:, :], shv[:, 1:2],
                                    None, ALU.mult))
        ev_md = sv.n

        if dbg == 2:
            dmp2 = {}
            for nm, tile_ in (("e_sxv", sxv), ("e_shv", shv), ("e_qsc", qsc),
                              ("e_qbi", qbi), ("e_swg", swgt), ("e_swu", swut),
                              ("e_swd", swdt), ("e_mgt", mgt), ("e_mut", mut),
                              ("e_xqT", xqT), ("e_lnm", lnmax),
                              ("e_ccr", ccrow), ("e_sgc", sgc),
                              ("e_hmx", hmax), ("e_m2", m2t), ("e_rs2", rs2t)):
                shp = [tile_.shape[0], tile_.shape[1]]
                dmp2[nm] = (nc.dram_tensor(nm, shp, tile_.dtype,
                                           kind="ExternalOutput").ap(), tile_)
            hdbg = nc.dram_tensor("e_h", [NI, T], F32, kind="ExternalOutput").ap()

            @block.sync
            def _(sync):
                sync.wait_ge(sv.h, sv.n)
                for nm, (ap_, tile_) in dmp2.items():
                    qs.dma(sync.dma_start(out=ap_, in_=tile_[:, :]))
                qs.dma(sync.dma_start(out=hdbg, in_=h_d))

        p2.close(); open_stacks.remove(p2)

        if stage <= 6:
            _finish()
            return nc

        p3 = ExitStack()
        open_stacks.append(p3)
        p3e = p3.enter_context
        hq = p3e(nc.sbuf_tensor("hq", [128, 2 * NIC * TG], BF16))
        hst_b = p3e(nc.sbuf_tensor("hst_b", [128, 3 * TG], F32))
        dsc = p3e(nc.sbuf_tensor("dsc", [128, NKC * TG], F32))
        x2b = p3e(nc.sbuf_tensor("x2b", [128, 2 * D], F32))
        outb = p3e(nc.sbuf_tensor("outb", [128, 2 * D], F32))
        ps_d = [p3e(nc.psum_tensor("ps_dA", [128, TG], F32)),
                p3e(nc.psum_tensor("ps_dB", [128, TG], F32))]
        ps_t2_single = p3e(nc.psum_tensor("ps_t2A", [128, 1024], F32))
        ps_t2 = [ps_t2_single, ps_t2_single]

        PH2_SS, PH2_SV, PH2_ST, PH2_SG = ss.n, sv.n, st.n, sg.n

        ev_hld = {}; ev_hq = {}; ev_hqs = {}; ev_dmm = {}; ev_dsc = {}
        ev_x2bld = {}; ev_dtr = {}; ev_oadd = {}; ev_ost = {}

        def emit_quant(tg):
            ptg = tg % 2
            for icc in range(NIC):
                pl = (tg * NIC + icc) % 3
                k = tg * NIC + icc
                if icc % 2 == 0:
                    @block.sync
                    def _(sync, tg=tg, icc=icc, pl=pl, k=k):
                        if tg == 0 and icc == 0:
                            sync.wait_ge(sv.h, ev_md)
                            sync.wait_ge(ss.h, PH2_SS)
                            sync.wait_ge(sg.h, PH2_SG)
                            sync.wait_ge(st.h, PH2_ST)
                            dhs.wait_all(sync)
                        if k >= 3:
                            pk = divmod(k - 3, NIC)
                            sync.wait_ge(sv.h, ev_hqs[(pk[0], pk[1])])
                        dhl.dma(sync.dma_start(
                            out=hst_b[:, pl * TG:(pl + 1) * TG],
                            in_=h_ic[icc][:, tg * TG:(tg + 1) * TG]), pl)
                    ev_hld[(tg, icc)] = (pl, dhl.s[pl].n)
                else:
                    @block.gpsimd
                    def _(g, tg=tg, icc=icc, pl=pl, k=k):
                        if tg == 0 and icc == 1:
                            g.wait_ge(sv.h, ev_md)
                            g.wait_ge(ss.h, PH2_SS)
                            g.wait_ge(sg.h, PH2_SG)
                            g.wait_ge(st.h, PH2_ST)
                            dhs.wait_all(g)
                        if k >= 3:
                            pk = divmod(k - 3, NIC)
                            g.wait_ge(sv.h, ev_hqs[(pk[0], pk[1])])
                        dhl.dma(g.dma_start(
                            out=hst_b[:, pl * TG:(pl + 1) * TG],
                            in_=h_ic[icc][:, tg * TG:(tg + 1) * TG]), pl)
                    ev_hld[(tg, icc)] = (pl, dhl.s[pl].n)

                @block.vector
                def _(vector, tg=tg, icc=icc, pl=pl, ptg=ptg):
                    pl_, v_ = ev_hld[(tg, icc)]
                    dhl.wait(vector, pl_, v_)
                    if tg >= 2 and icc == 0:
                        vector.wait_ge(st.h, ev_dmm[(tg - 2, NKC - 1)])
                    vector.tensor_scalar(
                        hst_b[:, pl * TG:(pl + 1) * TG],
                        hst_b[:, pl * TG:(pl + 1) * TG],
                        shv[:, 0:1], TWO23, ALU.mult, ALU.add)
                    vector.tensor_scalar(
                        hst_b[:, pl * TG:(pl + 1) * TG],
                        hst_b[:, pl * TG:(pl + 1) * TG],
                        -TWO23, QMAX, ALU.add, ALU.min)
                    sv.inc(vector.tensor_scalar(
                        hq[:, (ptg * NIC + icc) * TG:(ptg * NIC + icc + 1) * TG],
                        hst_b[:, pl * TG:(pl + 1) * TG],
                        -QMAX - 1.0, None, ALU.max))
                ev_hq[(tg, icc)] = sv.n
                ev_hqs[(tg, icc)] = ev_hq[(tg, icc)]

        emit_quant(0)
        for tg in range(NTG):
            ptg = tg % 2
            if tg + 1 < NTG:
                emit_quant(tg + 1)

            for hc in range(NKC):
                pd = hc % 2
                @block.tensor
                def _(tensor, tg=tg, hc=hc, pd=pd, ptg=ptg):
                    if tg == 0 and hc == 0:
                        tensor.wait_ge(sv.h, ev_md)
                    tensor.wait_ge(sv.h, ev_hq[(tg, NIC - 1)])
                    pk = (tg, hc - 2) if hc >= 2 else \
                        ((tg - 1, NKC - 2 + hc) if tg > 0 else None)
                    if pk is not None and pk in ev_dsc:
                        tensor.wait_ge(sv.h, ev_dsc[pk])
                    for icc in range(NIC):
                        for half in range(2):
                            st.inc(tensor.matmul(
                                ps_d[pd][:, half * 512:(half + 1) * 512],
                                wdT[:, icc * 1024 + hc * 128:icc * 1024 + (hc + 1) * 128],
                                hq[:, (ptg * NIC + icc) * TG + half * 512:(ptg * NIC + icc) * TG + (half + 1) * 512],
                                start=(icc == 0), stop=(icc == NIC - 1)))
                ev_dmm[(tg, hc)] = st.n

                @block.vector
                def _(vector, tg=tg, hc=hc, pd=pd):
                    vector.wait_ge(st.h, ev_dmm[(tg, hc)])
                    if tg >= 1:
                        vector.wait_ge(st.h, ev_dtr[(tg - 1, 7)])
                    sv.inc(vector.tensor_scalar(
                        dsc[:, hc * TG:(hc + 1) * TG], ps_d[pd][:, :],
                        swdt[:, hc:hc + 1], None, ALU.mult))
                ev_dsc[(tg, hc)] = sv.n

            for tcg in range(NKC):
                pt2 = tcg % 2
                tc_abs = tg * NKC + tcg
                @block.sync
                def _(sync, tg=tg, tcg=tcg, tc_abs=tc_abs):
                    k = tc_abs
                    if k == 0:
                        dx2s.wait(sync, 0, ev_x2st[nt_run - 1])
                    if k >= 2:
                        sync.wait_ge(sv.h, ev_oadd[k - 2])
                    dxb.dma(sync.dma_start(
                        out=x2b[:, (k % 2) * D:(k % 2) * D + D],
                        in_=x2_tc[tc_abs]), k % 2)
                ev_x2bld[tc_abs] = dxb.s[tc_abs % 2].n

                @block.tensor
                def _(tensor, tg=tg, tcg=tcg, pt2=pt2):
                    tensor.wait_ge(sv.h, ev_dsc[(tg, NKC - 1)])
                    k = tg * NKC + tcg
                    if k >= 1:
                        tensor.wait_ge(sv.h, ev_oadd[k - 1])
                    for hc in range(NKC):
                        st.inc(tensor.transpose(
                            ps_t2[pt2][:, hc * 128:(hc + 1) * 128],
                            dsc[:, hc * TG + tcg * 128:hc * TG + (tcg + 1) * 128],
                            eyet[:, :]))
                ev_dtr[(tg, tcg)] = st.n

                @block.vector
                def _(vector, tg=tg, tcg=tcg, pt2=pt2, tc_abs=tc_abs):
                    vector.wait_ge(st.h, ev_dtr[(tg, tcg)])
                    dxb.wait(vector, tc_abs % 2, ev_x2bld[tc_abs])
                    if tc_abs >= 2:
                        dys.wait(vector, tc_abs % 2, ev_ost[tc_abs - 2])
                    sv.inc(vector.tensor_tensor(
                        outb[:, (tc_abs % 2) * D:(tc_abs % 2) * D + D],
                        x2b[:, (tc_abs % 2) * D:(tc_abs % 2) * D + D],
                        ps_t2[pt2][:, :], ALU.add))
                ev_oadd[tc_abs] = sv.n

                @block.sync
                def _(sync, tc_abs=tc_abs):
                    sync.wait_ge(sv.h, ev_oadd[tc_abs])
                    dys.dma(sync.dma_start(
                        out=y_tc[tc_abs],
                        in_=outb[:, (tc_abs % 2) * D:(tc_abs % 2) * D + D]),
                        tc_abs % 2)
                ev_ost[tc_abs] = dys.s[tc_abs % 2].n

        if dbg == 2:
            wdt_dbg = nc.dram_tensor("f_wdT", [128, NIC * 1024], BF16,
                                     kind="ExternalOutput").ap()
            hq_dbg = nc.dram_tensor("f_hq", [128, 2 * NIC * TG], BF16,
                                    kind="ExternalOutput").ap()
            dsc_dbg = nc.dram_tensor("f_dsc", [128, NKC * TG], F32,
                                     kind="ExternalOutput").ap()

            @block.sync
            def _(sync):
                sync.wait_ge(sv.h, sv.n)
                sync.wait_ge(st.h, st.n)
                qs.dma(sync.dma_start(out=wdt_dbg, in_=wdT[:, :]))
                qs.dma(sync.dma_start(out=hq_dbg, in_=hq[:, :]))
                qs.dma(sync.dma_start(out=dsc_dbg, in_=dsc[:, :]))

        @block.sync
        def _(sync):
            sync.wait_ge(qs.h, qs.n)
            for d_ in all_slotdma:
                d_.wait_all(sync)
        @block.gpsimd
        def _(g):
            g.wait_ge(qg.h, qg.n)

        p3.close(); open_stacks.remove(p3)
        p2c.close(); open_stacks.remove(p2c)

    return nc


def kernel(L0_attn_x, L0_attn_norm_w, L0_attn_w_forget, L0_attn_w_input,
           L0_attn_w_output, L0_ffn_clip_x, L0_ffn_clip_down, L0_ffn_w_gate,
           L0_ffn_w_up, L0_ffn_w_down):
    nc = build_program(dbg=False)
    x = np.ascontiguousarray(np.asarray(L0_attn_x, dtype=np.float32))
    shared = {
        "wf": np.ascontiguousarray(np.asarray(L0_attn_w_forget, np.float32)),
        "wi": np.ascontiguousarray(np.asarray(L0_attn_w_input, np.float32)),
        "wo": np.ascontiguousarray(np.asarray(L0_attn_w_output, np.float32)),
        "nw": np.ascontiguousarray(np.asarray(L0_attn_norm_w, np.float32)),
        "cx": np.ascontiguousarray(np.asarray(L0_ffn_clip_x, np.float32)),
        "cd": np.ascontiguousarray(np.asarray(L0_ffn_clip_down, np.float32)),
        "wg": np.ascontiguousarray(np.asarray(L0_ffn_w_gate, np.float32)),
        "wu": np.ascontiguousarray(np.asarray(L0_ffn_w_up, np.float32)),
        "wd": np.ascontiguousarray(np.asarray(L0_ffn_w_down, np.float32)),
    }
    in_maps = [dict(shared, x=np.ascontiguousarray(x[c])) for c in range(8)]
    res = run_bass_kernel_spmd(nc, in_maps, core_ids=list(range(8)))
    return np.stack([np.asarray(res.results[c]["y"]) for c in range(8)], axis=0)


# version 07
# speedup vs baseline: 1.6624x; 1.0198x over previous
import numpy as np
import ml_dtypes
from contextlib import ExitStack
import concourse.bass as bass
import concourse.mybir as mybir
from concourse.bass_utils import run_bass_kernel_spmd

F32 = mybir.dt.float32
BF16 = mybir.dt.bfloat16
AF = mybir.ActivationFunctionType
ALU = mybir.AluOpType
AX = mybir.AxisListType.X

T = 4096
D = 1024
NH = 8
NI = 2816
NIC = NI // 128
NKC = D // 128
TILE = 512
NT = T // TILE
NTC = T // 128
TG = 1024
NTG = T // TG
TWO23 = 12582912.0
QMAX = 127.0
LN_EPS = 1e-5
RMS_EPS = 1e-6
FORGET_LB = 0.9


class Sem:
    def __init__(self, handle):
        self.h = handle
        self.n = 0

    def inc(self, instr, k=1):
        self.n += k
        instr.then_inc(self.h, k)
        return self.n

    def dma(self, instr):
        return self.inc(instr, 16)


class SlotDma:

    def __init__(self, nc, e, name, nslots):
        self.s = [Sem(e(nc.semaphore(f"{name}{i}"))) for i in range(nslots)]

    def dma(self, instr, slot):
        return self.s[slot].dma(instr)

    def wait(self, eng, slot, val):
        eng.wait_ge(self.s[slot].h, val)

    def wait_all(self, eng):
        for s_ in self.s:
            eng.wait_ge(s_.h, s_.n)


class Chain:

    def __init__(self, eng, sem):
        self.eng = eng
        self.sem = sem

    def __call__(self, instr):
        self.sem.inc(instr)
        self.eng.wait_ge(self.sem.h, self.sem.n)
        return instr


def build_program(dbg=False, stage=7, nt_run=NT):
    nc = bass.Bass("TRN2", target_bir_lowering=False, debug=False)

    x_d = nc.dram_tensor("x", [T, D], F32, kind="ExternalInput").ap()
    wf_d = nc.dram_tensor("wf", [D, D], F32, kind="ExternalInput").ap()
    wi_d = nc.dram_tensor("wi", [D, D], F32, kind="ExternalInput").ap()
    wo_d = nc.dram_tensor("wo", [D, D], F32, kind="ExternalInput").ap()
    nw_d = nc.dram_tensor("nw", [D], F32, kind="ExternalInput").ap()
    cx_d = nc.dram_tensor("cx", [1], F32, kind="ExternalInput").ap()
    cd_d = nc.dram_tensor("cd", [1], F32, kind="ExternalInput").ap()
    wg_d = nc.dram_tensor("wg", [NI, D], F32, kind="ExternalInput").ap()
    wu_d = nc.dram_tensor("wu", [NI, D], F32, kind="ExternalInput").ap()
    wd_d = nc.dram_tensor("wd", [D, NI], F32, kind="ExternalInput").ap()
    y_d = nc.dram_tensor("y", [T, D], F32, kind="ExternalOutput").ap()
    if dbg:
        y2_d = nc.dram_tensor("y2", [T, D], F32, kind="ExternalOutput").ap()
        y2_t4 = y2_d.rearrange("(n c p) d -> n p c d", n=NT, c=4, p=128)

    x2_d = nc.dram_tensor("x2buf", [T, D], F32).ap()
    h_d = nc.dram_tensor("hbuf", [NI, T], F32).ap()
    ccxi_d = nc.dram_tensor("ccxi", [1, 128], F32)
    ccxo_d = nc.dram_tensor("ccxo", [1, 128], F32)
    cchi_d = nc.dram_tensor("cchi", [1, 128], F32)
    ccho_d = nc.dram_tensor("ccho", [1, 128], F32)

    x_t4 = x_d.rearrange("(n c p) d -> n p c d", n=NT, c=4, p=128)
    x2_t4 = x2_d.rearrange("(n c p) d -> n p c d", n=NT, c=4, p=128)
    x2_tc = x2_d.rearrange("(n p) d -> n p d", n=NTC, p=128)
    y_tc = y_d.rearrange("(n p) d -> n p d", n=NTC, p=128)
    h_ic = h_d.rearrange("(a p) t -> a p t", a=NIC, p=128)

    eye32 = nc.inline_tensor(np.eye(128, dtype=np.float32), "ceye32").ap()
    eyeb = nc.inline_tensor(np.eye(128, dtype=ml_dtypes.bfloat16), "ceyeb").ap()
    hones_np = np.zeros((128, NH * NH), dtype=np.float32)
    for hh in range(NH):
        hones_np[:, hh * NH + hh] = 1.0
    hones = nc.inline_tensor(hones_np, "chones").ap()
    selb_np = np.zeros((NH, NH * 128), dtype=np.float32)
    for hh in range(NH):
        selb_np[hh, hh * 128:(hh + 1) * 128] = 1.0
    selb = nc.inline_tensor(selb_np, "cselb").ap()
    ones1 = nc.inline_tensor(np.ones((1, 128), dtype=np.float32), "cones1").ap()

    es = ExitStack()
    with es:
        e = es.enter_context

        qs = Sem(e(nc.semaphore("qs")))
        qg = Sem(e(nc.semaphore("qg")))
        sv = Sem(e(nc.semaphore("sv")))
        ss = Sem(e(nc.semaphore("ss")))
        st = Sem(e(nc.semaphore("st")))
        sg = Sem(e(nc.semaphore("sg")))
        scc = Sem(e(nc.semaphore("scc")))

        dwt = SlotDma(nc, e, "dwt", 2)
        dxl = SlotDma(nc, e, "dxl", 2)
        dx2s = SlotDma(nc, e, "dx2s", 1)
        dqx = SlotDma(nc, e, "dqx", 2)
        dwd = SlotDma(nc, e, "dwd", 1)
        dwg = SlotDma(nc, e, "dwg", 2)
        dhs = SlotDma(nc, e, "dhs", 2)
        dhl = SlotDma(nc, e, "dhl", 3)
        dxb = SlotDma(nc, e, "dxb", 2)
        dys = SlotDma(nc, e, "dys", 2)
        all_slotdma = [dwt, dxl, dx2s, dqx, dwd, dwg, dhs, dhl, dxb, dys]

        eyet = e(nc.sbuf_tensor("eyet", [128, 128], F32))
        eyebt = e(nc.sbuf_tensor("eyebt", [128, 128], BF16))
        honet = e(nc.sbuf_tensor("honet", [128, NH * NH], F32))
        selbt = e(nc.sbuf_tensor("selbt", [NH, NH * 128], F32))
        ones1t = e(nc.sbuf_tensor("ones1t", [1, 128], F32))
        nwt = e(nc.sbuf_tensor("nwt", [128, NKC], F32))
        m2t = e(nc.sbuf_tensor("m2t", [128, NTC], F32))
        rs2t = e(nc.sbuf_tensor("rs2t", [128, NTC], F32))
        qsc = e(nc.sbuf_tensor("qsc", [128, NTC], F32))
        qbi = e(nc.sbuf_tensor("qbi", [128, NTC], F32))
        carry = e(nc.sbuf_tensor("carry", [128, NH], F32))
        lnmax = e(nc.sbuf_tensor("lnmax", [128, 1], F32))
        lnmc = e(nc.sbuf_tensor("lnmc", [128, NT], F32))
        hmax = e(nc.sbuf_tensor("hmax", [128, 1], F32))
        hmaxc = e(nc.sbuf_tensor("hmaxc", [128, NIC * NTG], F32))
        sxv = e(nc.sbuf_tensor("sxv", [128, 4], F32))
        shv = e(nc.sbuf_tensor("shv", [128, 4], F32))
        sgc = e(nc.sbuf_tensor("sgc", [1, 8], F32))
        ccrow = e(nc.sbuf_tensor("ccrow", [1, 128], F32))
        swgt = e(nc.sbuf_tensor("swgt", [128, NIC], F32))
        swut = e(nc.sbuf_tensor("swut", [128, NIC], F32))
        swdt = e(nc.sbuf_tensor("swdt", [128, NKC], F32))
        stat2 = e(nc.sbuf_tensor("stat2", [128, 10 * 128], F32))
        statS = e(nc.sbuf_tensor("statS", [128, 256], F32))

        def S2(r):
            return stat2[:, r * 128:(r + 1) * 128]

        def S2c(r, c0, c1=None):
            c1 = c0 + 1 if c1 is None else c1
            return stat2[:, r * 128 + c0:r * 128 + c1]

        R_MEAN, R_VARP, R_A2, R_R0, R_WRK, R_M2, R_W2, R_SQ2, R_AMAX, R_TMP = range(10)
        WG_AM, WU_AM, WG_R, WU_R, WG_SW, WU_SW, WD_AM, WD_R, WD_SW, WG_MU = range(10)
        c23t = e(nc.sbuf_tensor("c23t", [128, 1], F32))
        open_stacks = []
        p1w = ExitStack()
        open_stacks.append(p1w)
        p1we = p1w.enter_context
        x_t = p1we(nc.sbuf_tensor("x_t", [128, 2 * 4096], F32))
        wfThi = p1we(nc.sbuf_tensor("wfThi", [128, NKC * D], BF16))
        wfTlo = p1we(nc.sbuf_tensor("wfTlo", [128, NKC * D], BF16))
        woThi = p1we(nc.sbuf_tensor("woThi", [128, NKC * D], BF16))
        woTlo = p1we(nc.sbuf_tensor("woTlo", [128, NKC * D], BF16))
        uThi = p1we(nc.sbuf_tensor("uThi", [128, NKC * NH], BF16))
        uTlo = p1we(nc.sbuf_tensor("uTlo", [128, NKC * NH], BF16))

        def x_sl(p):
            return x_t[:, p * 4096:(p + 1) * 4096]

        block = e(nc.Block())

        pro = ExitStack()
        pe_ = pro.enter_context
        wtile = pe_(nc.sbuf_tensor("wtile", [128, 2 * D], F32))
        u_sb = pe_(nc.sbuf_tensor("u_sb", [NH, D], F32))
        ps_w = pe_(nc.psum_tensor("ps_w", [128, 1024], F32))
        ps_u = pe_(nc.psum_tensor("ps_u", [NH, 1024], F32))

        @block.sync
        def _(sync):
            for dst, src in ((eyet, eye32), (eyebt, eyeb), (honet, hones),
                             (selbt, selb), (ones1t, ones1)):
                qs.dma(sync.dma_start(out=dst[:, :], in_=src))
            qs.dma(sync.dma_start(out=nwt[:, :],
                                  in_=nw_d.rearrange("(c p) -> p c", p=128),
                                  allow_slow_non_contiguous=True))
            qs.dma(sync.dma_start(out=sgc[0:1, 0:1], in_=cx_d.rearrange("(a o) -> a o", a=1)))
            qs.dma(sync.dma_start(out=sgc[0:1, 1:2], in_=cd_d.rearrange("(a o) -> a o", a=1)))
        ev_const = qs.n

        ev_xload = []
        @block.sync
        def _(sync):
            dxl.dma(sync.dma_start(out=x_sl(0), in_=x_t4[0]), 0)
        ev_xload.append(dxl.s[0].n)
        @block.sync
        def _(sync):
            dxl.dma(sync.dma_start(out=x_sl(1), in_=x_t4[1]), 1)
        ev_xload.append(dxl.s[1].n)

        ev_wi = []
        ev_umm_h = []
        for hh in range(NH):
            @block.sync
            def _(sync, hh=hh):
                if hh >= 2:
                    sync.wait_ge(st.h, ev_umm_h[hh - 2])
                dwt.dma(sync.dma_start(
                    out=wtile[:, (hh % 2) * D:(hh % 2) * D + D],
                    in_=wi_d[hh * 128:(hh + 1) * 128, :]), hh % 2)
            ev_wi.append(dwt.s[hh % 2].n)

            @block.tensor
            def _(tensor, hh=hh):
                if hh == 0:
                    tensor.wait_ge(qs.h, ev_const)
                dwt.wait(tensor, hh % 2, ev_wi[hh])
                for half in range(2):
                    st.inc(tensor.matmul(
                        ps_u[:, half * 512:(half + 1) * 512],
                        honet[:, hh * NH:(hh + 1) * NH],
                        wtile[:, (hh % 2) * D + half * 512:(hh % 2) * D + (half + 1) * 512],
                        start=(hh == 0), stop=(hh == NH - 1)))
            ev_umm_h.append(st.n)
        ev_umm = st.n

        @block.vector
        def _(vector):
            vector.wait_ge(st.h, ev_umm)
            sv.inc(vector.tensor_copy(u_sb[:, :], ps_u[:, :]))
        ev_usb = sv.n

        @block.tensor
        def _(tensor):
            tensor.wait_ge(sv.h, ev_usb)
            for kc in range(NKC):
                st.inc(tensor.transpose(ps_w[:, kc * NH:(kc + 1) * NH],
                                        u_sb[:, kc * 128:(kc + 1) * 128],
                                        eyet[0:NH, 0:NH]))
        ev_ut = st.n

        @block.vector
        def _(vector):
            ch = Chain(vector, sv)
            vector.wait_ge(st.h, ev_ut)
            for kc in range(NKC):
                ch(vector.tensor_scalar(
                    uThi[:, kc * NH:(kc + 1) * NH], ps_w[:, kc * NH:(kc + 1) * NH],
                    nwt[:, kc:kc + 1], None, ALU.mult))
                ch(vector.scalar_tensor_tensor(
                    uTlo[:, kc * NH:(kc + 1) * NH], ps_w[:, kc * NH:(kc + 1) * NH],
                    nwt[:, kc:kc + 1], uThi[:, kc * NH:(kc + 1) * NH],
                    ALU.mult, ALU.subtract))
        ev_utsplit = sv.n

        def prep_weight(w_dram, whi, wlo, init_v, init_s, init_tr):
            ev_ld = []
            evac_v = [init_v]
            evac_s = [init_s]
            ev_tr_all = {-2: init_tr[0], -1: init_tr[1]}
            for oc in range(NKC):
                @block.sync
                def _(sync, oc=oc):
                    sync.wait_ge(st.h, ev_tr_all[oc - 2])
                    dwt.dma(sync.dma_start(
                        out=wtile[:, (oc % 2) * D:(oc % 2) * D + D],
                        in_=w_dram[oc * 128:(oc + 1) * 128, :]), oc % 2)
                ev_ld.append(dwt.s[oc % 2].n)
                ev_tr = []
                @block.tensor
                def _(tensor, oc=oc):
                    dwt.wait(tensor, oc % 2, ev_ld[oc])
                    tensor.wait_ge(sv.h, evac_v[-1])
                    tensor.wait_ge(ss.h, evac_s[-1])
                    for kc in range(NKC):
                        ev_tr.append(st.inc(tensor.transpose(
                            ps_w[:, kc * 128:(kc + 1) * 128],
                            wtile[:, (oc % 2) * D + kc * 128:(oc % 2) * D + (kc + 1) * 128],
                            eyet[:, :])))
                ev_tr_all[oc] = st.n
                ev_hi = []
                @block.scalar
                def _(scalar, oc=oc):
                    for kc in range(NKC):
                        scalar.wait_ge(st.h, ev_tr[3 if kc < 4 else 7])
                        ev_hi.append(ss.inc(scalar.activation(
                            whi[:, kc * D + oc * 128:kc * D + (oc + 1) * 128],
                            ps_w[:, kc * 128:(kc + 1) * 128],
                            AF.Identity, scale=nwt[:, kc:kc + 1])))
                @block.vector
                def _(vector, oc=oc):
                    for kc in range(NKC):
                        vector.wait_ge(ss.h, ev_hi[3 if kc < 4 else 7])
                        sv.inc(vector.scalar_tensor_tensor(
                            wlo[:, kc * D + oc * 128:kc * D + (oc + 1) * 128],
                            ps_w[:, kc * 128:(kc + 1) * 128],
                            nwt[:, kc:kc + 1],
                            whi[:, kc * D + oc * 128:kc * D + (oc + 1) * 128],
                            ALU.mult, ALU.subtract))
                evac_v.append(sv.n)
                evac_s.append(ss.n)
            return evac_v[-1], evac_s[-1], (ev_tr_all[NKC - 2], ev_tr_all[NKC - 1])

        wf_v, wf_s, wf_tr = prep_weight(wf_d, wfThi, wfTlo, ev_utsplit, ss.n,
                                        (ev_umm, ev_umm))
        ev_pro_v, ev_pro_s, _ = prep_weight(wo_d, woThi, woTlo, wf_v, wf_s, wf_tr)
        pro.close()

        def _finish():
            for stk in reversed(open_stacks):
                stk.close()
            @block.sync
            def _(sync):
                sync.wait_ge(qs.h, qs.n)
                for d_ in all_slotdma:
                    d_.wait_all(sync)
                sync.wait_ge(sv.h, sv.n)
                sync.wait_ge(ss.h, ss.n)
                sync.wait_ge(st.h, st.n)
            @block.gpsimd
            def _(g):
                g.wait_ge(qg.h, qg.n)
                g.wait_ge(sg.h, sg.n)

        if stage <= 0:
            _finish()
            return nc

        p1 = ExitStack()
        open_stacks.append(p1)
        p1e = p1.enter_context
        z_t = p1e(nc.sbuf_tensor("z_t", [128, 2 * 1024], F32))
        r_t = p1e(nc.sbuf_tensor("r_t", [128, 4096], F32))
        x2_t = p1e(nc.sbuf_tensor("x2_t", [128, 4096], F32))
        zThi = p1e(nc.sbuf_tensor("zThi", [128, NKC * TILE], BF16))
        zTlo = p1e(nc.sbuf_tensor("zTlo", [128, NKC * TILE], BF16))
        y_tile = p1e(nc.sbuf_tensor("y_tile", [128, 4096], F32))
        f_b = p1e(nc.sbuf_tensor("f_b", [128, 2 * TILE], F32))
        i_b = p1e(nc.sbuf_tensor("i_b", [128, 2 * TILE], F32))
        o_b = p1e(nc.sbuf_tensor("o_b", [128, 2 * TILE], F32))
        g_b = p1e(nc.sbuf_tensor("g_b", [128, 2 * TILE], F32))
        v_sb = p1e(nc.sbuf_tensor("v_sb", [NH, TILE], F32))
        tmp_t = p1e(nc.sbuf_tensor("tmp_t", [128, 1024], F32))
        trash = p1e(nc.sbuf_tensor("trash", [128, 1024], F32))
        ps_rot = [p1e(nc.psum_tensor("ps_rotA", [128, 1024], F32)),
                  p1e(nc.psum_tensor("ps_rotB", [128, 1024], F32))]
        ps_f = p1e(nc.psum_tensor("ps_f", [128, TILE], F32))
        ps_o = p1e(nc.psum_tensor("ps_o", [128, TILE], F32))
        ps_v = p1e(nc.psum_tensor("ps_v", [NH, TILE], F32))
        ps_vb = p1e(nc.psum_tensor("ps_vb", [128, TILE], F32))

        @block.vector
        def _(vector):
            vector.wait_ge(qs.h, ev_const)
            vector.memset(carry[:, :], 0.0)
            vector.memset(lnmax[:, :], 0.0)
            vector.memset(lnmc[:, :], 0.0)
            vector.memset(hmaxc[:, :], 0.0)
            vector.memset(c23t[:, :], TWO23)
            sv.inc(vector.memset(hmax[:, :], 0.0))

        rot_guard = [None, None]
        rot_idx = [0]

        def rot_next():
            s_ = rot_idx[0] % 2
            rot_idx[0] += 1
            return s_


        ev_z = {}; ev_r = {}; ev_zt = {}; ev_zhi = {}; ev_zlo = {}
        ev_vsb = {}; ev_fsig = {}; ev_osig = {}; ev_g = {}; ev_ymul = {}
        ev_ytile = {}; ev_yt = {}; ev_x2add = {}; ev_x2st = {}
        ev_sq2 = {}; ev_amax4 = {}; ev_cxn = {}; ev_xdone = {}; ev_rs2 = {}
        ev_fproj = {}; ev_oproj = {}; ev_vb = {}; ev_vmm = {}

        for i in range(nt_run + 1):
            par = i % 2
            la = i - 1

            if i + 1 < nt_run:
                @block.sync
                def _(sync, i=i):
                    if i - 1 >= 0:
                        sync.wait_ge(ss.h, ev_xdone[i - 1])
                    dxl.dma(sync.dma_start(out=x_sl((i + 1) % 2),
                                           in_=x_t4[i + 1]), (i + 1) % 2)
                ev_xload.append(dxl.s[(i + 1) % 2].n)

            if la >= 0:
                for tc in range(4):
                    slot = rot_next()
                    @block.tensor
                    def _(tensor, la=la, tc=tc, slot=slot):
                        tensor.wait_ge(sv.h, ev_ytile[la])
                        if rot_guard[slot] is not None:
                            kind, val = rot_guard[slot]
                            tensor.wait_ge(kind.h, val)
                        for oc in range(NH):
                            st.inc(tensor.transpose(
                                ps_rot[slot][:, oc * 128:(oc + 1) * 128],
                                y_tile[:, oc * TILE + tc * 128:oc * TILE + (tc + 1) * 128],
                                eyet[:, :]))
                    ev_yt[(la, tc)] = st.n

                    @block.vector
                    def _(vector, la=la, tc=tc, slot=slot):
                        vector.wait_ge(st.h, ev_yt[(la, tc)])
                        vector.wait_ge(ss.h, ev_r[la])
                        if la >= 1 and tc == 0:
                            vector.wait_ge(ss.h, ev_sq2[la - 1])
                            dx2s.wait(vector, 0, ev_x2st[la - 1])
                        ev_x2add[(la, tc)] = sv.inc(vector.tensor_tensor(
                            x2_t[:, tc * 1024:(tc + 1) * 1024],
                            r_t[:, tc * 1024:(tc + 1) * 1024],
                            ps_rot[slot][:, :], ALU.add))
                    rot_guard[slot] = (sv, ev_x2add[(la, tc)])

                @block.vector
                def _(vector, la=la):
                    ch = Chain(vector, sv)
                    ch(vector.tensor_reduce(
                        S2c(R_M2, 0, 4),
                        x2_t[:, :].rearrange("p (c d) -> p c d", c=4, d=1024),
                        AX, ALU.add))
                    vector.tensor_scalar(S2(R_M2), S2(R_M2), 1.0 / D, None,
                                         ALU.mult)
                    vector.tensor_copy(m2t[:, la * 4:(la + 1) * 4],
                                       S2c(R_M2, 0, 4))
                    for tc in range(4):
                        vector.tensor_scalar(
                            tmp_t[:, :], x2_t[:, tc * 1024:(tc + 1) * 1024],
                            S2c(R_M2, tc), None, ALU.subtract)
                        last = vector.tensor_reduce(
                            S2c(R_AMAX, tc), tmp_t[:, :], AX, ALU.max,
                            apply_absolute_value=True)
                    sv.inc(last)
                ev_amax4[la] = sv.n

                pass

            if True:
                @block.scalar
                def _(scalar, i=i, la=la, par=par):
                    if la >= 0:
                        scalar.wait_ge(sv.h, ev_x2add[(la, 3)])
                        if la >= 1:
                            scalar.wait_ge(sv.h, ev_rs2[la - 1])
                        scalar.memzero(statS[:, 128:132])
                        for tc in range(4):
                            last = scalar.activation(
                                trash[:, :], x2_t[:, tc * 1024:(tc + 1) * 1024],
                                AF.Square,
                                accum_out=statS[:, 128 + tc:129 + tc])
                        ss.inc(last)
                        ev_sq2[la] = ss.n
                    if i < nt_run:
                        dxl.wait(scalar, i % 2, ev_xload[i])
                        if i >= 1:
                            scalar.wait_ge(sv.h, ev_cxn[i - 1])
                        scalar.memzero(statS[:, 0:4])
                        for tc in range(4):
                            last = scalar.activation(
                                trash[:, :],
                                x_sl(par)[:, tc * 1024:(tc + 1) * 1024],
                                AF.Square, accum_out=statS[:, tc:tc + 1])
                        ss.inc(last)
                ev_sqx = ss.n

            if i < nt_run:

                @block.vector
                def _(vector, i=i, par=par, la=la):
                    ch = Chain(vector, sv)
                    dxl.wait(vector, i % 2, ev_xload[i])
                    ch(vector.tensor_reduce(
                        S2c(R_MEAN, 0, 4),
                        x_sl(par)[:, :].rearrange("p (c d) -> p c d", c=4, d=1024),
                        AX, ALU.add))
                    vector.tensor_scalar(S2(R_MEAN), S2(R_MEAN),
                                         1.0 / D, None, ALU.mult)
                    vector.wait_ge(ss.h, ev_sqx)
                    vector.tensor_scalar(S2(R_VARP), statS[:, 0:128],
                                         1.0 / D, None, ALU.mult)
                    vector.tensor_tensor(S2(R_W2), S2(R_MEAN),
                                         S2(R_MEAN), ALU.mult)
                    vector.tensor_tensor(S2(R_VARP), S2(R_VARP),
                                         S2(R_W2), ALU.subtract)
                    vector.tensor_scalar(S2(R_VARP), S2(R_VARP),
                                         LN_EPS, None, ALU.add)
                    vector.tensor_scalar(S2(R_A2), S2(R_VARP),
                                         1.0 + RMS_EPS, -LN_EPS, ALU.mult, ALU.add)
                    vector.reciprocal(S2(R_R0), S2(R_VARP))
                    last = vector.reciprocal(S2(R_WRK), S2(R_A2))
                    if la >= 0:
                        vector.wait_ge(ss.h, ev_sq2[la])
                        vector.tensor_scalar(S2(R_SQ2), statS[:, 128:256],
                                             1.0 / D, None, ALU.mult)
                        vector.tensor_tensor(S2(R_W2), S2(R_M2),
                                             S2(R_M2), ALU.mult)
                        vector.tensor_tensor(S2(R_SQ2), S2(R_SQ2),
                                             S2(R_W2), ALU.subtract)
                        vector.tensor_scalar(S2(R_SQ2), S2(R_SQ2),
                                             LN_EPS, None, ALU.add)
                        last = vector.reciprocal(S2(R_TMP), S2(R_SQ2))
                    sv.inc(last)
                ev_recip = sv.n

                @block.scalar
                def _(scalar, la=la):
                    scalar.wait_ge(sv.h, ev_recip)
                    scalar.activation(S2(R_R0), S2(R_R0), AF.Sqrt)
                    last = scalar.activation(S2(R_WRK), S2(R_WRK), AF.Sqrt)
                    if la >= 0:
                        last = scalar.activation(S2(R_TMP), S2(R_TMP), AF.Sqrt)
                    ss.inc(last)
                ev_sqrt = ss.n

                @block.vector
                def _(vector, i=i, la=la):
                    ch = Chain(vector, sv)
                    vector.wait_ge(ss.h, ev_sqrt)
                    if la >= 0:
                        vector.tensor_tensor(S2(R_W2), S2(R_TMP),
                                             S2(R_TMP), ALU.mult)
                        vector.tensor_tensor(S2(R_W2), S2(R_W2),
                                             S2(R_SQ2), ALU.mult)
                        vector.tensor_scalar(S2(R_W2), S2(R_W2),
                                             -0.5, 1.5, ALU.mult, ALU.add)
                        vector.tensor_tensor(S2(R_TMP), S2(R_TMP),
                                             S2(R_W2), ALU.mult)
                        vector.tensor_copy(rs2t[:, la * 4:(la + 1) * 4],
                                           S2c(R_TMP, 0, 4))
                        vector.wait_ge(sv.h, ev_amax4[la])
                        vector.tensor_tensor(S2(R_AMAX), S2(R_AMAX),
                                             S2(R_TMP), ALU.mult)
                        sv.inc(vector.tensor_reduce(lnmc[:, la:la + 1],
                                                    S2c(R_AMAX, 0, 4),
                                                    AX, ALU.max))
                        ev_rs2[la] = sv.n
                    vector.tensor_tensor(S2(R_W2), S2(R_R0),
                                         S2(R_R0), ALU.mult)
                    vector.tensor_tensor(S2(R_W2), S2(R_W2),
                                         S2(R_VARP), ALU.mult)
                    vector.tensor_scalar(S2(R_W2), S2(R_W2),
                                         -0.5, 1.5, ALU.mult, ALU.add)
                    vector.tensor_tensor(S2(R_R0), S2(R_R0),
                                         S2(R_W2), ALU.mult)
                    vector.tensor_tensor(S2(R_W2), S2(R_WRK),
                                         S2(R_WRK), ALU.mult)
                    vector.tensor_tensor(S2(R_W2), S2(R_W2),
                                         S2(R_A2), ALU.mult)
                    vector.tensor_scalar(S2(R_W2), S2(R_W2),
                                         -0.5, 1.5, ALU.mult, ALU.add)
                    vector.tensor_tensor(S2(R_WRK), S2(R_WRK),
                                         S2(R_W2), ALU.mult)
                    vector.tensor_tensor(S2(R_A2), S2(R_MEAN),
                                         S2(R_WRK), ALU.mult)
                    vector.tensor_scalar(S2(R_A2), S2(R_A2),
                                         -1.0, None, ALU.mult)
                    vector.tensor_scalar(S2(R_VARP), S2(R_R0),
                                         1.0, None, ALU.add)
                    vector.tensor_tensor(S2(R_W2), S2(R_MEAN),
                                         S2(R_R0), ALU.mult)
                    sv.inc(vector.tensor_scalar(S2(R_W2), S2(R_W2),
                                                -1.0, None, ALU.mult))
                ev_cxn[i] = sv.n

                for tc in range(4):
                    slot = rot_next()
                    @block.scalar
                    def _(scalar, i=i, tc=tc, par=par):
                        scalar.wait_ge(sv.h, ev_cxn[i])
                        zt_prev = ev_zt.get((i, tc - 2)) if tc >= 2 else \
                            (ev_zt.get((i - 1, tc + 2)) if i >= 1 else None)
                        if zt_prev is not None:
                            scalar.wait_ge(st.h, zt_prev)
                        if ev_x2add.get((i - 1, tc)) is not None:
                            scalar.wait_ge(sv.h, ev_x2add[(i - 1, tc)])
                        ss.inc(scalar.activation(
                            z_t[:, (tc % 2) * 1024:(tc % 2) * 1024 + 1024],
                            x_sl(par)[:, tc * 1024:(tc + 1) * 1024],
                            AF.Identity, scale=S2c(R_WRK, tc),
                            bias=S2c(R_A2, tc)))
                        ev_z[(i, tc)] = ss.n
                        ss.inc(scalar.activation(
                            r_t[:, tc * 1024:(tc + 1) * 1024],
                            x_sl(par)[:, tc * 1024:(tc + 1) * 1024],
                            AF.Identity, scale=S2c(R_VARP, tc),
                            bias=S2c(R_W2, tc)))
                        if tc == 3:
                            ev_r[i] = ss.n
                            ev_xdone[i] = ss.n

                    @block.tensor
                    def _(tensor, i=i, tc=tc, slot=slot):
                        if i == 0 and tc == 0:
                            tensor.wait_ge(sv.h, ev_pro_v)
                            tensor.wait_ge(ss.h, ev_pro_s)
                            tensor.wait_ge(sv.h, ev_utsplit)
                        tensor.wait_ge(ss.h, ev_z[(i, tc)])
                        if rot_guard[slot] is not None:
                            kind, val = rot_guard[slot]
                            tensor.wait_ge(kind.h, val)
                        for kc in range(NKC):
                            st.inc(tensor.transpose(
                                ps_rot[slot][:, kc * 128:(kc + 1) * 128],
                                z_t[:, (tc % 2) * 1024 + kc * 128:(tc % 2) * 1024 + (kc + 1) * 128],
                                eyet[:, :]))
                    ev_zt[(i, tc)] = st.n

                    @block.vector
                    def _(vector, i=i, tc=tc, slot=slot):
                        ch = Chain(vector, sv)
                        vector.wait_ge(st.h, ev_zt[(i, tc)])
                        if i >= 1 and tc == 0:
                            vector.wait_ge(st.h, ev_oproj[(i - 1, NH - 1)])
                        ch(vector.tensor_copy(
                            zThi[:, :].rearrange("p (k t) -> p k t", k=NKC, t=TILE)[:, :, tc * 128:(tc + 1) * 128],
                            ps_rot[slot][:, :].rearrange("p (k b) -> p k b", k=NKC, b=128)))
                        ev_zhi[(i, tc)] = sv.n
                        sv.inc(vector.tensor_tensor(
                            zTlo[:, :].rearrange("p (k t) -> p k t", k=NKC, t=TILE)[:, :, tc * 128:(tc + 1) * 128],
                            ps_rot[slot][:, :].rearrange("p (k b) -> p k b", k=NKC, b=128),
                            zThi[:, :].rearrange("p (k t) -> p k t", k=NKC, t=TILE)[:, :, tc * 128:(tc + 1) * 128],
                            ALU.subtract))
                    ev_zlo[(i, tc)] = sv.n
                    rot_guard[slot] = (sv, ev_zlo[(i, tc)])

                    @block.tensor
                    def _(tensor, i=i, tc=tc):
                        tensor.wait_ge(sv.h, ev_zlo[(i, tc)])
                        if tc == 0 and i > 0:
                            tensor.wait_ge(sv.h, ev_vsb[i - 1])
                        n = 0
                        for kc in range(NKC):
                            zh = zThi[:, kc * TILE + tc * 128:kc * TILE + (tc + 1) * 128]
                            zl = zTlo[:, kc * TILE + tc * 128:kc * TILE + (tc + 1) * 128]
                            uh = uThi[:, kc * NH:(kc + 1) * NH]
                            ul = uTlo[:, kc * NH:(kc + 1) * NH]
                            for (wop, zop) in ((uh, zh), (ul, zh), (uh, zl)):
                                st.inc(tensor.matmul(
                                    ps_v[:, tc * 128:(tc + 1) * 128], wop, zop,
                                    start=(n == 0), stop=(n == 3 * NKC - 1)))
                                n += 1
                    ev_vmm[(i, tc)] = st.n

                @block.vector
                def _(vector, i=i):
                    vector.wait_ge(st.h, ev_vmm[(i, 3)])
                    sv.inc(vector.tensor_copy(v_sb[:, :], ps_v[:, :]))
                ev_vsb[i] = sv.n

                for oc in range(NH):
                    op = oc % 2

                    @block.tensor
                    def _(tensor, i=i, oc=oc):
                        prev = ev_fsig.get((i, oc - 1)) if oc > 0 else \
                            ev_fsig.get((i - 1, NH - 1))
                        if prev is not None:
                            tensor.wait_ge(ss.h, prev)
                        n = 0
                        for kc in range(NKC):
                            zh = zThi[:, kc * TILE:(kc + 1) * TILE]
                            zl = zTlo[:, kc * TILE:(kc + 1) * TILE]
                            wh = wfThi[:, kc * D + oc * 128:kc * D + (oc + 1) * 128]
                            wl = wfTlo[:, kc * D + oc * 128:kc * D + (oc + 1) * 128]
                            for (wop, zop) in ((wh, zh), (wl, zh), (wh, zl)):
                                st.inc(tensor.matmul(
                                    ps_f[:, :], wop, zop,
                                    start=(n == 0), stop=(n == 3 * NKC - 1)))
                                n += 1
                    ev_fproj[(i, oc)] = st.n

                    @block.tensor
                    def _(tensor, i=i, oc=oc):
                        prev = ev_osig.get((i, oc - 1)) if oc > 0 else \
                            ev_osig.get((i - 1, NH - 1))
                        if prev is not None:
                            tensor.wait_ge(ss.h, prev)
                        n = 0
                        for kc in range(NKC):
                            zh = zThi[:, kc * TILE:(kc + 1) * TILE]
                            zl = zTlo[:, kc * TILE:(kc + 1) * TILE]
                            wh = woThi[:, kc * D + oc * 128:kc * D + (oc + 1) * 128]
                            wl = woTlo[:, kc * D + oc * 128:kc * D + (oc + 1) * 128]
                            for (wop, zop) in ((wh, zh), (wl, zh), (wh, zl)):
                                st.inc(tensor.matmul(
                                    ps_o[:, :], wop, zop,
                                    start=(n == 0), stop=(n == 3 * NKC - 1)))
                                n += 1
                        ev_oproj[(i, oc)] = st.n
                        tensor.wait_ge(sv.h, ev_vsb[i])
                        prevg = ev_g.get((i, oc - 1)) if oc > 0 else \
                            ev_g.get((i - 1, NH - 1))
                        if prevg is not None:
                            tensor.wait_ge(sv.h, prevg)
                        st.inc(tensor.matmul(
                            ps_vb[:, :], selbt[:, oc * 128:(oc + 1) * 128],
                            v_sb[:, :], start=True, stop=True))
                    ev_vb[(i, oc)] = st.n

                    @block.scalar
                    def _(scalar, i=i, oc=oc, op=op):
                        scalar.wait_ge(st.h, ev_fproj[(i, oc)])
                        prevy = ev_ymul.get((i, oc - 2)) if oc >= 2 else \
                            (ev_ymul.get((i - 1, NH - 2 + oc)) if i >= 1 else None)
                        if prevy is not None:
                            scalar.wait_ge(sv.h, prevy)
                        ss.inc(scalar.activation(
                            f_b[:, op * TILE:(op + 1) * TILE], ps_f[:, :],
                            AF.Sigmoid))
                        ev_fsig[(i, oc)] = ss.n
                        scalar.wait_ge(st.h, ev_oproj[(i, oc)])
                        ss.inc(scalar.activation(
                            o_b[:, op * TILE:(op + 1) * TILE], ps_o[:, :],
                            AF.Sigmoid))
                        ev_osig[(i, oc)] = ss.n

                    @block.vector
                    def _(vector, i=i, oc=oc, op=op):
                        fsl = f_b[:, op * TILE:(op + 1) * TILE]
                        isl = i_b[:, op * TILE:(op + 1) * TILE]
                        gsl = g_b[:, op * TILE:(op + 1) * TILE]
                        osl = o_b[:, op * TILE:(op + 1) * TILE]
                        if oc == 0 and i >= 1 and (i - 1, 3) in ev_yt:
                            vector.wait_ge(st.h, ev_yt[(i - 1, 3)])
                        vector.wait_ge(ss.h, ev_fsig[(i, oc)])
                        vector.tensor_scalar(fsl, fsl, FORGET_LB, None, ALU.max)
                        vector.tensor_scalar(isl, fsl, -1.0, 1.0, ALU.mult, ALU.add)
                        vector.wait_ge(st.h, ev_vb[(i, oc)])
                        ev_g[(i, oc)] = sv.inc(vector.tensor_tensor(
                            gsl, isl, ps_vb[:, :], ALU.mult))
                        sv.inc(vector.tensor_tensor_scan(
                            fsl, fsl, gsl, carry[:, oc:oc + 1], ALU.mult, ALU.add))
                        vector.wait_ge(sv.h, sv.n)
                        vector.tensor_copy(carry[:, oc:oc + 1],
                                           fsl[:, TILE - 1:TILE])
                        vector.wait_ge(ss.h, ev_osig[(i, oc)])
                        sv.inc(vector.tensor_tensor(
                            y_tile[:, oc * TILE:(oc + 1) * TILE], osl, fsl,
                            ALU.mult))
                    ev_ymul[(i, oc)] = sv.n
                ev_ytile[i] = ev_ymul[(i, NH - 1)]

            if i == nt_run and la >= 0:
                @block.vector
                def _(vector, la=la):
                    vector.wait_ge(ss.h, ev_sq2[la])
                    vector.tensor_scalar(S2(R_SQ2), statS[:, 128:256],
                                         1.0 / D, None, ALU.mult)
                    vector.tensor_tensor(S2(R_W2), S2(R_M2),
                                         S2(R_M2), ALU.mult)
                    vector.tensor_tensor(S2(R_SQ2), S2(R_SQ2),
                                         S2(R_W2), ALU.subtract)
                    vector.tensor_scalar(S2(R_SQ2), S2(R_SQ2),
                                         LN_EPS, None, ALU.add)
                    sv.inc(vector.reciprocal(S2(R_TMP), S2(R_SQ2)))
                ev_rc2 = sv.n

                @block.scalar
                def _(scalar):
                    scalar.wait_ge(sv.h, ev_rc2)
                    ss.inc(scalar.activation(S2(R_TMP), S2(R_TMP), AF.Sqrt))
                ev_sq2s = ss.n

                @block.vector
                def _(vector, la=la):
                    ch = Chain(vector, sv)
                    vector.wait_ge(ss.h, ev_sq2s)
                    vector.tensor_tensor(S2(R_W2), S2(R_TMP),
                                         S2(R_TMP), ALU.mult)
                    vector.tensor_tensor(S2(R_W2), S2(R_W2),
                                         S2(R_SQ2), ALU.mult)
                    vector.tensor_scalar(S2(R_W2), S2(R_W2),
                                         -0.5, 1.5, ALU.mult, ALU.add)
                    vector.tensor_tensor(S2(R_TMP), S2(R_TMP),
                                         S2(R_W2), ALU.mult)
                    vector.tensor_copy(rs2t[:, la * 4:(la + 1) * 4],
                                       S2c(R_TMP, 0, 4))
                    vector.wait_ge(sv.h, ev_amax4[la])
                    vector.tensor_tensor(S2(R_AMAX), S2(R_AMAX),
                                         S2(R_TMP), ALU.mult)
                    sv.inc(vector.tensor_reduce(lnmc[:, la:la + 1],
                                                S2c(R_AMAX, 0, 4),
                                                AX, ALU.max))
                ev_rs2[la] = sv.n

            if la >= 0:
                @block.sync
                def _(sync, la=la):
                    sync.wait_ge(sv.h, ev_rs2[la])
                    sync.wait_ge(ss.h, ev_sq2[la])
                    dx2s.dma(sync.dma_start(out=x2_t4[la], in_=x2_t[:, :]), 0)
                    if dbg:
                        qs.dma(sync.dma_start(out=y2_t4[la], in_=x2_t[:, :]))
                ev_x2st[la] = dx2s.s[0].n

        if stage <= 1:
            if dbg == 2:
                dmp = {}
                for nm, tile_ in (("d_r", r_t), ("d_y", y_tile), ("d_x2", x2_t),
                                  ("d_f", f_b), ("d_o", o_b), ("d_g", g_b),
                                  ("d_stat", stat), ("d_m2", m2t), ("d_rs2", rs2t),
                                  ("d_carry", carry), ("d_zhi", zThi),
                                  ("d_uhi", uThi), ("d_ulo", uTlo)):
                    shp = [tile_.shape[0], tile_.shape[1]]
                    dt_ = tile_.dtype
                    dmp[nm] = (nc.dram_tensor(nm, shp, dt_, kind="ExternalOutput").ap(), tile_)
                vdmp = nc.dram_tensor("d_v", [NH, TILE], F32, kind="ExternalOutput").ap()

                @block.sync
                def _(sync):
                    sync.wait_ge(sv.h, sv.n)
                    sync.wait_ge(ss.h, ss.n)
                    sync.wait_ge(st.h, st.n)
                    for nm, (ap_, tile_) in dmp.items():
                        qs.dma(sync.dma_start(out=ap_, in_=tile_[:, :]))
                    qs.dma(sync.dma_start(out=vdmp, in_=v_sb[:, :]))
            _finish()
            return nc

        @block.vector
        def _(vector):
            ch = Chain(vector, sv)
            vector.wait_ge(sv.h, ev_rs2[nt_run - 1])
            ch(vector.tensor_reduce(lnmax[:, 0:1], lnmc[:, 0:nt_run],
                                    AX, ALU.max))
        ev_lnf = sv.n

        @block.tensor
        def _(tensor):
            tensor.wait_ge(sv.h, ev_lnf)
            st.inc(tensor.transpose(ps_v[0:1, 0:128], lnmax[:, 0:1], eyet[:, :]))
        ev_lt = st.n

        @block.vector
        def _(vector):
            vector.wait_ge(st.h, ev_lt)
            sv.inc(vector.tensor_copy(ccrow[0:1, :], ps_v[0:1, 0:128]))
        ev_lrow = sv.n

        @block.scalar
        def _(scalar):
            scalar.wait_ge(qs.h, ev_const)
            scalar.activation(sgc[0:1, 2:3], sgc[0:1, 0:1], AF.Exp, scale=-1.0)
            ss.inc(scalar.activation(sgc[0:1, 3:4], sgc[0:1, 1:2], AF.Exp, scale=-1.0))
        ev_exp = ss.n

        @block.gpsimd
        def _(g):
            g.wait_ge(sv.h, ev_lrow)
            qg.dma(g.dma_start(out=ccxi_d.ap(), in_=ccrow[0:1, :]))
            g.wait_ge(qg.h, qg.n)
            scc.inc(g.collective_compute(
                "AllReduce", ALU.max, replica_groups=[list(range(8))],
                ins=[ccxi_d.ap()], outs=[ccxo_d.ap()]))
            g.wait_ge(scc.h, scc.n)
            qg.dma(g.dma_start(out=ccrow[0:1, :], in_=ccxo_d.ap()))
        ev_ccx = qg.n

        @block.vector
        def _(vector):
            ch = Chain(vector, sv)
            vector.wait_ge(ss.h, ev_exp)
            ch(vector.tensor_scalar(sgc[0:1, 2:4], sgc[0:1, 2:4], 1.0, None, ALU.add))
            ch(vector.reciprocal(sgc[0:1, 2:4], sgc[0:1, 2:4]))
            vector.wait_ge(qg.h, ev_ccx)
            ch(vector.tensor_reduce(sgc[0:1, 4:5], ccrow[0:1, :], AX, ALU.max))
            ch(vector.tensor_tensor(sgc[0:1, 4:5], sgc[0:1, 4:5],
                                    sgc[0:1, 2:3], ALU.mult))
        ev_gm = sv.n

        @block.tensor
        def _(tensor):
            tensor.wait_ge(sv.h, ev_gm)
            st.inc(tensor.matmul(ps_vb[:, 0:1], ones1t[:, :], sgc[0:1, 4:5],
                                 start=True, stop=True))
        ev_gb = st.n

        @block.vector
        def _(vector):
            ch = Chain(vector, sv)
            vector.wait_ge(st.h, ev_gb)
            ch(vector.tensor_scalar(sxv[:, 1:2], ps_vb[:, 0:1], 1.0 / QMAX,
                                    None, ALU.mult))
            ch(vector.reciprocal(sxv[:, 0:1], sxv[:, 1:2]))
            ch(vector.tensor_scalar(qsc[:, :], rs2t[:, :], sxv[:, 0:1], None,
                                    ALU.mult))
            ch(vector.tensor_tensor(qbi[:, :], m2t[:, :], qsc[:, :], ALU.mult))
            ch(vector.tensor_scalar(qbi[:, :], qbi[:, :], -1.0, None, ALU.mult))
        ev_qsc = sv.n

        p1.close(); open_stacks.remove(p1)
        p1w.close(); open_stacks.remove(p1w)

        if stage <= 2:
            _finish()
            return nc

        p2c = ExitStack()
        open_stacks.append(p2c)
        wdT = p2c.enter_context(nc.sbuf_tensor("wdT", [128, NIC * 1024], BF16))

        p2 = ExitStack()
        open_stacks.append(p2)
        p2e = p2.enter_context
        xqT = p2e(nc.sbuf_tensor("xqT", [128, NKC * T], BF16))
        wdst = p2e(nc.sbuf_tensor("wdst", [128, NI], F32))
        wdqn = p2e(nc.sbuf_tensor("wdqn", [128, NI], BF16))
        wgst = p2e(nc.sbuf_tensor("wgst", [128, 2 * D], F32))
        wust = p2e(nc.sbuf_tensor("wust", [128, 2 * D], F32))
        wgqn = p2e(nc.sbuf_tensor("wgqn", [128, 2 * D], BF16))
        wuqn = p2e(nc.sbuf_tensor("wuqn", [128, 2 * D], BF16))
        wgqT = p2e(nc.sbuf_tensor("wgqT", [128, 2 * D], BF16))
        wuqT = p2e(nc.sbuf_tensor("wuqT", [128, 2 * D], BF16))
        q_s = p2e(nc.sbuf_tensor("q_s", [128, 2 * D], F32))
        xqn = p2e(nc.sbuf_tensor("xqn", [128, 2 * D], BF16))
        wrnd = p2e(nc.sbuf_tensor("wrnd", [128, 2 * D], F32))
        gsil = p2e(nc.sbuf_tensor("gsil", [128, 2 * TG], F32))
        h_t = p2e(nc.sbuf_tensor("h_t", [128, 2 * TG], F32))
        mgt = p2e(nc.sbuf_tensor("mgt", [128, NIC], F32))
        mut = p2e(nc.sbuf_tensor("mut", [128, NIC], F32))
        ps_gate = [p2e(nc.psum_tensor("ps_gateA", [128, TG], F32))]
        ps_up = p2e(nc.psum_tensor("ps_up", [128, TG], F32))
        ps_wt = p2e(nc.psum_tensor("ps_wt", [128, 1024], BF16))
        ps_wtg = p2e(nc.psum_tensor("ps_wtg", [128, 1024], BF16))
        ps_gate.append(p2e(nc.psum_tensor("ps_gateB", [128, TG], F32)))

        PH1_SS, PH1_SV, PH1_ST, PH1_SG = ss.n, sv.n, st.n, sg.n

        ev_x2ld = {}; ev_aff = {}; ev_xq = {}; ev_xqtr = {}; ev_xqevac = {}
        for tc in range(NTC):
            parq = tc % 2
            @block.sync
            def _(sync, tc=tc, parq=parq):
                if tc == 0:
                    sync.wait_ge(sv.h, ev_qsc)
                    sync.wait_ge(ss.h, PH1_SS)
                    sync.wait_ge(sg.h, PH1_SG)
                    sync.wait_ge(st.h, PH1_ST)
                    dx2s.wait(sync, 0, ev_x2st[nt_run - 1])
                if tc >= 2:
                    sync.wait_ge(sv.h, ev_xq[tc - 2])
                dqx.dma(sync.dma_start(out=q_s[:, parq * D:(parq + 1) * D],
                                       in_=x2_tc[tc]), parq)
            ev_x2ld[tc] = dqx.s[parq].n

            @block.vector
            def _(vector, tc=tc, parq=parq):
                dqx.wait(vector, tc % 2, ev_x2ld[tc])
                if tc >= 2:
                    vector.wait_ge(st.h, ev_xqtr[tc - 2])
                vector.tensor_scalar(q_s[:, parq * D:(parq + 1) * D],
                                     q_s[:, parq * D:(parq + 1) * D],
                                     qsc[:, tc:tc + 1], qbi[:, tc:tc + 1],
                                     ALU.mult, ALU.add)
                vector.tensor_scalar(q_s[:, parq * D:(parq + 1) * D],
                                     q_s[:, parq * D:(parq + 1) * D],
                                     TWO23, -TWO23, ALU.add, ALU.add)
                sv.inc(vector.tensor_scalar(
                    xqn[:, parq * D:(parq + 1) * D],
                    q_s[:, parq * D:(parq + 1) * D],
                    QMAX, -QMAX - 1.0, ALU.min, ALU.max))
            ev_xq[tc] = sv.n

            @block.tensor
            def _(tensor, tc=tc, parq=parq):
                tensor.wait_ge(sv.h, ev_xq[tc])
                if tc >= 1:
                    tensor.wait_ge(ss.h, ev_xqevac[tc - 1])
                for kc in range(NKC):
                    st.inc(tensor.transpose(
                        ps_wt[:, kc * 128:(kc + 1) * 128],
                        xqn[:, parq * D + kc * 128:parq * D + (kc + 1) * 128],
                        eyebt[:, :]))
            ev_xqtr[tc] = st.n

            @block.scalar
            def _(scalar, tc=tc):
                scalar.wait_ge(st.h, ev_xqtr[tc])
                ss.inc(scalar.copy(
                    xqT[:, :].rearrange("p (k t) -> p k t", k=NKC, t=T)[:, :, tc * 128:(tc + 1) * 128],
                    ps_wt[:, :].rearrange("p (k b) -> p k b", k=NKC, b=128)))
            ev_xqevac[tc] = ss.n
        ev_xqT_done = ev_xqevac[NTC - 1]

        if stage <= 3:
            _finish()
            return nc

        ev_wdld = {}; ev_wdam = {}; ev_wdr = {}; ev_wdq = {}
        ev_wdtr = {}; ev_wdevac = {}; ev_wdafs = {}; ev_wdq_part = [0]
        for hc in range(NKC):
            @block.sync
            def _(sync, hc=hc):
                if hc >= 1:
                    sync.wait_ge(sv.h, ev_wdafs[hc - 1])
                dwd.dma(sync.dma_start(out=wdst[:, :],
                                       in_=wd_d[hc * 128:(hc + 1) * 128, :]), 0)
            ev_wdld[hc] = dwd.s[0].n

            @block.vector
            def _(vector, hc=hc):
                dwd.wait(vector, 0, ev_wdld[hc])
                chd = Chain(vector, sv)
                chd(vector.tensor_reduce(S2c(WD_AM, 0), wdst[:, :], AX, ALU.max,
                                         apply_absolute_value=True))
                vector.tensor_scalar(S2(WD_SW), S2(WD_AM),
                                     1.0 / QMAX, None, ALU.mult)
                vector.tensor_copy(swdt[:, hc:hc + 1], S2c(WD_SW, 0))
                sv.inc(vector.reciprocal(S2(WD_R), S2(WD_SW)))
            ev_wdr[hc] = sv.n

            for cofs, clen in ((0, 2048), (2048, NI - 2048)):
                @block.vector
                def _(vector, hc=hc, cofs=cofs, clen=clen):
                    if hc >= 1 and cofs == 0:
                        vector.wait_ge(st.h, ev_wdtr[hc - 1])
                    vector.tensor_scalar(wrnd[:, 0:clen], wdst[:, cofs:cofs + clen],
                                         S2c(WD_R, 0), TWO23, ALU.mult, ALU.add)
                    vector.tensor_scalar(wrnd[:, 0:clen], wrnd[:, 0:clen],
                                         -TWO23, QMAX, ALU.add, ALU.min)
                    sv.inc(vector.tensor_scalar(
                        wdqn[:, cofs:cofs + clen], wrnd[:, 0:clen],
                        -QMAX - 1.0, None, ALU.max))
                ev_wdq_part.append(sv.n)
            ev_wdq[hc] = sv.n
            ev_wdafs[hc] = sv.n

            for grp in range(3):
                lo = grp * 8
                hi_ = min(lo + 8, NIC)
                @block.tensor
                def _(tensor, hc=hc, lo=lo, hi_=hi_, grp=grp):
                    tensor.wait_ge(sv.h, ev_wdq[hc])
                    if hc == 0 and grp == 0:
                        tensor.wait_ge(ss.h, ev_xqT_done)
                    if grp >= 1:
                        tensor.wait_ge(ss.h, ev_wdevac[(hc, grp - 1)])
                    elif hc >= 1:
                        tensor.wait_ge(ss.h, ev_wdevac[(hc - 1, 2)])
                    for icc in range(lo, hi_):
                        st.inc(tensor.transpose(
                            ps_wt[:, (icc - lo) * 128:(icc - lo + 1) * 128],
                            wdqn[:, icc * 128:(icc + 1) * 128],
                            eyebt[:, :]))
                ev_wdtr_g = st.n

                @block.scalar
                def _(scalar, hc=hc, lo=lo, hi_=hi_, ev=ev_wdtr_g):
                    scalar.wait_ge(st.h, ev)
                    for icc in range(lo, hi_):
                        last = scalar.copy(
                            wdT[:, icc * 1024 + hc * 128:icc * 1024 + (hc + 1) * 128],
                            ps_wt[:, (icc - lo) * 128:(icc - lo + 1) * 128])
                    ss.inc(last)
                ev_wdevac[(hc, grp)] = ss.n
            ev_wdtr[hc] = st.n
        ev_wd_done = ev_wdevac[(NKC - 1, 2)]

        if stage <= 4:
            _finish()
            return nc

        ev_wgld = {}; ev_am2 = {}; ev_r2 = {}; ev_wgq = {}; ev_wgafs = {}
        ev_wgtr = {}; ev_wgte1 = {}; ev_wutr = {}; ev_wgte = {}
        ev_gmm = {}; ev_umm = {}; ev_sil = {}; ev_hmul = {}
        ev_hmax = {}; ev_hst = {}
        for ic in range(NIC):
            parw = ic % 2
            @block.sync
            def _(sync, ic=ic, parw=parw):
                if ic >= 2:
                    sync.wait_ge(sv.h, ev_wgafs[ic - 2])
                dwg.dma(sync.dma_start(out=wgst[:, parw * D:(parw + 1) * D],
                                       in_=wg_d[ic * 128:(ic + 1) * 128, :]), parw)
                dwg.dma(sync.dma_start(out=wust[:, parw * D:(parw + 1) * D],
                                       in_=wu_d[ic * 128:(ic + 1) * 128, :]), parw)
            ev_wgld[ic] = dwg.s[parw].n

            @block.vector
            def _(vector, ic=ic, parw=parw):
                dwg.wait(vector, ic % 2, ev_wgld[ic])
                chw = Chain(vector, sv)
                vector.tensor_reduce(S2c(WG_AM, 0),
                                     wgst[:, parw * D:(parw + 1) * D],
                                     AX, ALU.max, apply_absolute_value=True)
                chw(vector.tensor_reduce(S2c(WU_AM, 0),
                                         wust[:, parw * D:(parw + 1) * D],
                                         AX, ALU.max, apply_absolute_value=True))
                vector.tensor_scalar(S2(WG_SW), S2(WG_AM),
                                     1.0 / QMAX, None, ALU.mult)
                vector.tensor_scalar(S2(WU_SW), S2(WU_AM),
                                     1.0 / QMAX, None, ALU.mult)
                vector.tensor_copy(swgt[:, ic:ic + 1], S2c(WG_SW, 0))
                vector.tensor_copy(swut[:, ic:ic + 1], S2c(WU_SW, 0))
                vector.reciprocal(S2(WG_R), S2(WG_SW))
                vector.reciprocal(S2(WU_R), S2(WU_SW))
                vector.tensor_scalar(S2(WG_MU), S2(WG_SW),
                                     sxv[:, 1:2], None, ALU.mult)
                vector.tensor_copy(mgt[:, ic:ic + 1], S2c(WG_MU, 0))
                vector.tensor_scalar(S2(WG_MU), S2(WU_SW),
                                     sxv[:, 1:2], None, ALU.mult)
                sv.inc(vector.tensor_copy(mut[:, ic:ic + 1], S2c(WG_MU, 0)))
            ev_r2[ic] = sv.n

            @block.vector
            def _(vector, ic=ic, parw=parw):
                if ic >= 2:
                    vector.wait_ge(st.h, ev_wutr[ic - 2])
                for half, wst_, rr, wqn in ((0, wgst, WG_R, wgqn),
                                            (1, wust, WU_R, wuqn)):
                    vector.tensor_scalar(wrnd[:, half * D:(half + 1) * D],
                                         wst_[:, parw * D:(parw + 1) * D],
                                         S2c(rr, 0), TWO23, ALU.mult, ALU.add)
                    vector.tensor_scalar(wrnd[:, half * D:(half + 1) * D],
                                         wrnd[:, half * D:(half + 1) * D],
                                         -TWO23, QMAX, ALU.add, ALU.min)
                    sv.inc(vector.tensor_scalar(
                        wqn[:, parw * D:(parw + 1) * D],
                        wrnd[:, half * D:(half + 1) * D],
                        -QMAX - 1.0, None, ALU.max))
            ev_wgq[ic] = sv.n
            ev_wgafs[ic] = sv.n

            @block.tensor
            def _(tensor, ic=ic, parw=parw):
                tensor.wait_ge(sv.h, ev_wgq[ic])
                if ic >= 2:
                    tensor.wait_ge(sv.h, ev_wgte[ic - 2])
                for kc in range(NKC):
                    st.inc(tensor.transpose(
                        ps_wtg[:, kc * 128:(kc + 1) * 128],
                        wgqn[:, parw * D + kc * 128:parw * D + (kc + 1) * 128],
                        eyebt[:, :]))
            ev_wgtr[ic] = st.n

            @block.vector
            def _(vector, ic=ic, parw=parw):
                vector.wait_ge(st.h, ev_wgtr[ic])
                sv.inc(vector.tensor_copy(wgqT[:, parw * D:(parw + 1) * D],
                                          ps_wtg[:, :]))
            ev_wgte1[ic] = sv.n

            @block.tensor
            def _(tensor, ic=ic, parw=parw):
                tensor.wait_ge(sv.h, ev_wgte1[ic])
                for kc in range(NKC):
                    st.inc(tensor.transpose(
                        ps_wtg[:, kc * 128:(kc + 1) * 128],
                        wuqn[:, parw * D + kc * 128:parw * D + (kc + 1) * 128],
                        eyebt[:, :]))
            ev_wutr[ic] = st.n

            @block.vector
            def _(vector, ic=ic, parw=parw):
                vector.wait_ge(st.h, ev_wutr[ic])
                sv.inc(vector.tensor_copy(wuqT[:, parw * D:(parw + 1) * D],
                                          ps_wtg[:, :]))
            ev_wgte[ic] = sv.n

            for ts_ in range(NTG):
                parh = ts_ % 2

                @block.tensor
                def _(tensor, ic=ic, ts_=ts_, parw=parw):
                    tensor.wait_ge(sv.h, ev_wgte[ic])
                    tensor.wait_ge(ss.h, ev_xqevac[(ts_ + 1) * NKC - 1])
                    pk = (ic, ts_ - 2) if ts_ >= 2 else \
                        ((ic - 1, NTG - 2 + ts_) if ic > 0 else None)
                    if pk is not None and pk in ev_sil:
                        tensor.wait_ge(ss.h, ev_sil[pk])
                    for kc in range(NKC):
                        for half in range(2):
                            st.inc(tensor.matmul(
                                ps_gate[ts_ % 2][:, half * 512:(half + 1) * 512],
                                wgqT[:, parw * D + kc * 128:parw * D + (kc + 1) * 128],
                                xqT[:, kc * T + ts_ * TG + half * 512:kc * T + ts_ * TG + (half + 1) * 512],
                                start=(kc == 0), stop=(kc == NKC - 1)))
                    ev_gmm[(ic, ts_)] = st.n
                    if pk in ev_hmul:
                        tensor.wait_ge(sv.h, ev_hmul[pk])
                    for kc in range(NKC):
                        for half in range(2):
                            st.inc(tensor.matmul(
                                ps_up[:, half * 512:(half + 1) * 512],
                                wuqT[:, parw * D + kc * 128:parw * D + (kc + 1) * 128],
                                xqT[:, kc * T + ts_ * TG + half * 512:kc * T + ts_ * TG + (half + 1) * 512],
                                start=(kc == 0), stop=(kc == NKC - 1)))
                ev_umm[(ic, ts_)] = st.n

                @block.scalar
                def _(scalar, ic=ic, ts_=ts_, parh=parh):
                    scalar.wait_ge(st.h, ev_gmm[(ic, ts_)])
                    pk = (ic, ts_ - 2) if ts_ >= 2 else \
                        ((ic - 1, NTG - 2 + ts_) if ic > 0 else None)
                    if pk is not None and pk in ev_hmul:
                        scalar.wait_ge(sv.h, ev_hmul[pk])
                    ss.inc(scalar.activation(
                        gsil[:, parh * TG:(parh + 1) * TG],
                        ps_gate[ts_ % 2][:, :],
                        AF.Silu, scale=mgt[:, ic:ic + 1]))
                ev_sil[(ic, ts_)] = ss.n

                @block.vector
                def _(vector, ic=ic, ts_=ts_, parh=parh):
                    vector.wait_ge(st.h, ev_umm[(ic, ts_)])
                    vector.wait_ge(ss.h, ev_sil[(ic, ts_)])
                    pk = (ic, ts_ - 2) if ts_ >= 2 else \
                        ((ic - 1, NTG - 2 + ts_) if ic > 0 else None)
                    if pk is not None and pk in ev_hst:
                        dhs.wait(vector, ts_ % 2, ev_hst[pk])
                    sv.inc(vector.scalar_tensor_tensor(
                        h_t[:, parh * TG:(parh + 1) * TG], ps_up[:, :],
                        mut[:, ic:ic + 1], gsil[:, parh * TG:(parh + 1) * TG],
                        ALU.mult, ALU.mult))
                ev_hmul[(ic, ts_)] = sv.n

                @block.vector
                def _(vector, ic=ic, ts_=ts_, parh=parh):
                    sv.inc(vector.tensor_reduce(
                        hmaxc[:, ic * NTG + ts_:ic * NTG + ts_ + 1],
                        h_t[:, parh * TG:(parh + 1) * TG],
                        AX, ALU.max, apply_absolute_value=True))
                ev_hmax[(ic, ts_)] = sv.n

                @block.gpsimd
                def _(g, ic=ic, ts_=ts_, parh=parh):
                    g.wait_ge(sv.h, ev_hmul[(ic, ts_)])
                    dhs.dma(g.dma_start(
                        out=h_ic[ic][:, ts_ * TG:(ts_ + 1) * TG],
                        in_=h_t[:, parh * TG:(parh + 1) * TG]), parh)
                ev_hst[(ic, ts_)] = dhs.s[parh].n

        if stage <= 5:
            _finish()
            return nc

        @block.vector
        def _(vector):
            ch = Chain(vector, sv)
            vector.wait_ge(sv.h, ev_hmax[(NIC - 1, NTG - 1)])
            ch(vector.tensor_reduce(hmax[:, 0:1], hmaxc[:, :], AX, ALU.max))
        ev_hmf = sv.n

        @block.tensor
        def _(tensor):
            tensor.wait_ge(sv.h, ev_hmf)
            st.inc(tensor.transpose(ps_up[0:1, 0:128], hmax[:, 0:1], eyet[:, :]))
        ev_ht = st.n

        @block.vector
        def _(vector):
            vector.wait_ge(st.h, ev_ht)
            sv.inc(vector.tensor_copy(ccrow[0:1, :], ps_up[0:1, 0:128]))
        ev_hrow = sv.n

        @block.gpsimd
        def _(g):
            g.wait_ge(sv.h, ev_hrow)
            qg.dma(g.dma_start(out=cchi_d.ap(), in_=ccrow[0:1, :]))
            g.wait_ge(qg.h, qg.n)
            scc.inc(g.collective_compute(
                "AllReduce", ALU.max, replica_groups=[list(range(8))],
                ins=[cchi_d.ap()], outs=[ccho_d.ap()]))
            g.wait_ge(scc.h, scc.n)
            qg.dma(g.dma_start(out=ccrow[0:1, :], in_=ccho_d.ap()))
        ev_cch = qg.n

        @block.vector
        def _(vector):
            ch = Chain(vector, sv)
            vector.wait_ge(qg.h, ev_cch)
            ch(vector.tensor_reduce(sgc[0:1, 5:6], ccrow[0:1, :], AX, ALU.max))
            ch(vector.tensor_tensor(sgc[0:1, 5:6], sgc[0:1, 5:6],
                                    sgc[0:1, 3:4], ALU.mult))
        ev_gh = sv.n

        @block.tensor
        def _(tensor):
            tensor.wait_ge(sv.h, ev_gh)
            st.inc(tensor.matmul(ps_up[:, 0:1], ones1t[:, :], sgc[0:1, 5:6],
                                 start=True, stop=True))
        ev_ghb = st.n

        @block.vector
        def _(vector):
            ch = Chain(vector, sv)
            vector.wait_ge(st.h, ev_ghb)
            ch(vector.tensor_scalar(shv[:, 1:2], ps_up[:, 0:1], 1.0 / QMAX,
                                    None, ALU.mult))
            ch(vector.reciprocal(shv[:, 0:1], shv[:, 1:2]))
            ch(vector.tensor_scalar(swdt[:, :], swdt[:, :], shv[:, 1:2],
                                    None, ALU.mult))
        ev_md = sv.n

        if dbg == 2:
            dmp2 = {}
            for nm, tile_ in (("e_sxv", sxv), ("e_shv", shv), ("e_qsc", qsc),
                              ("e_qbi", qbi), ("e_swg", swgt), ("e_swu", swut),
                              ("e_swd", swdt), ("e_mgt", mgt), ("e_mut", mut),
                              ("e_xqT", xqT), ("e_lnm", lnmax),
                              ("e_ccr", ccrow), ("e_sgc", sgc),
                              ("e_hmx", hmax), ("e_m2", m2t), ("e_rs2", rs2t)):
                shp = [tile_.shape[0], tile_.shape[1]]
                dmp2[nm] = (nc.dram_tensor(nm, shp, tile_.dtype,
                                           kind="ExternalOutput").ap(), tile_)
            hdbg = nc.dram_tensor("e_h", [NI, T], F32, kind="ExternalOutput").ap()

            @block.sync
            def _(sync):
                sync.wait_ge(sv.h, sv.n)
                for nm, (ap_, tile_) in dmp2.items():
                    qs.dma(sync.dma_start(out=ap_, in_=tile_[:, :]))
                qs.dma(sync.dma_start(out=hdbg, in_=h_d))

        p2.close(); open_stacks.remove(p2)

        if stage <= 6:
            _finish()
            return nc

        p3 = ExitStack()
        open_stacks.append(p3)
        p3e = p3.enter_context
        hq = p3e(nc.sbuf_tensor("hq", [128, 2 * NIC * TG], BF16))
        hst_b = p3e(nc.sbuf_tensor("hst_b", [128, 3 * TG], F32))
        dsc = p3e(nc.sbuf_tensor("dsc", [128, NKC * TG], F32))
        x2b = p3e(nc.sbuf_tensor("x2b", [128, 2 * D], F32))
        outb = p3e(nc.sbuf_tensor("outb", [128, 2 * D], F32))
        ps_d = [p3e(nc.psum_tensor("ps_dA", [128, TG], F32)),
                p3e(nc.psum_tensor("ps_dB", [128, TG], F32))]
        ps_t2_single = p3e(nc.psum_tensor("ps_t2A", [128, 1024], F32))
        ps_t2 = [ps_t2_single, ps_t2_single]

        PH2_SS, PH2_SV, PH2_ST, PH2_SG = ss.n, sv.n, st.n, sg.n

        ev_hld = {}; ev_hq = {}; ev_hqs = {}; ev_dmm = {}; ev_dsc = {}
        ev_x2bld = {}; ev_dtr = {}; ev_oadd = {}; ev_ost = {}

        def emit_quant(tg):
            ptg = tg % 2
            for icc in range(NIC):
                pl = (tg * NIC + icc) % 3
                k = tg * NIC + icc
                if icc % 2 == 0:
                    @block.sync
                    def _(sync, tg=tg, icc=icc, pl=pl, k=k):
                        if tg == 0 and icc == 0:
                            sync.wait_ge(sv.h, ev_md)
                            sync.wait_ge(ss.h, PH2_SS)
                            sync.wait_ge(sg.h, PH2_SG)
                            sync.wait_ge(st.h, PH2_ST)
                            dhs.wait_all(sync)
                        if k >= 3:
                            pk = divmod(k - 3, NIC)
                            sync.wait_ge(sv.h, ev_hqs[(pk[0], pk[1])])
                        dhl.dma(sync.dma_start(
                            out=hst_b[:, pl * TG:(pl + 1) * TG],
                            in_=h_ic[icc][:, tg * TG:(tg + 1) * TG]), pl)
                    ev_hld[(tg, icc)] = (pl, dhl.s[pl].n)
                else:
                    @block.gpsimd
                    def _(g, tg=tg, icc=icc, pl=pl, k=k):
                        if tg == 0 and icc == 1:
                            g.wait_ge(sv.h, ev_md)
                            g.wait_ge(ss.h, PH2_SS)
                            g.wait_ge(sg.h, PH2_SG)
                            g.wait_ge(st.h, PH2_ST)
                            dhs.wait_all(g)
                        if k >= 3:
                            pk = divmod(k - 3, NIC)
                            g.wait_ge(sv.h, ev_hqs[(pk[0], pk[1])])
                        dhl.dma(g.dma_start(
                            out=hst_b[:, pl * TG:(pl + 1) * TG],
                            in_=h_ic[icc][:, tg * TG:(tg + 1) * TG]), pl)
                    ev_hld[(tg, icc)] = (pl, dhl.s[pl].n)

                @block.vector
                def _(vector, tg=tg, icc=icc, pl=pl, ptg=ptg):
                    pl_, v_ = ev_hld[(tg, icc)]
                    dhl.wait(vector, pl_, v_)
                    if tg >= 2 and icc == 0:
                        vector.wait_ge(st.h, ev_dmm[(tg - 2, NKC - 1)])
                    vector.tensor_scalar(
                        hst_b[:, pl * TG:(pl + 1) * TG],
                        hst_b[:, pl * TG:(pl + 1) * TG],
                        shv[:, 0:1], TWO23, ALU.mult, ALU.add)
                    vector.tensor_scalar(
                        hst_b[:, pl * TG:(pl + 1) * TG],
                        hst_b[:, pl * TG:(pl + 1) * TG],
                        -TWO23, QMAX, ALU.add, ALU.min)
                    sv.inc(vector.tensor_scalar(
                        hq[:, (ptg * NIC + icc) * TG:(ptg * NIC + icc + 1) * TG],
                        hst_b[:, pl * TG:(pl + 1) * TG],
                        -QMAX - 1.0, None, ALU.max))
                ev_hq[(tg, icc)] = sv.n
                ev_hqs[(tg, icc)] = ev_hq[(tg, icc)]

        emit_quant(0)
        for tg in range(NTG):
            ptg = tg % 2
            if tg + 1 < NTG:
                emit_quant(tg + 1)

            for hc in range(NKC):
                pd = hc % 2
                @block.tensor
                def _(tensor, tg=tg, hc=hc, pd=pd, ptg=ptg):
                    if tg == 0 and hc == 0:
                        tensor.wait_ge(sv.h, ev_md)
                    tensor.wait_ge(sv.h, ev_hq[(tg, NIC - 1)])
                    pk = (tg, hc - 2) if hc >= 2 else \
                        ((tg - 1, NKC - 2 + hc) if tg > 0 else None)
                    if pk is not None and pk in ev_dsc:
                        tensor.wait_ge(sv.h, ev_dsc[pk])
                    for icc in range(NIC):
                        for half in range(2):
                            st.inc(tensor.matmul(
                                ps_d[pd][:, half * 512:(half + 1) * 512],
                                wdT[:, icc * 1024 + hc * 128:icc * 1024 + (hc + 1) * 128],
                                hq[:, (ptg * NIC + icc) * TG + half * 512:(ptg * NIC + icc) * TG + (half + 1) * 512],
                                start=(icc == 0), stop=(icc == NIC - 1)))
                ev_dmm[(tg, hc)] = st.n

                @block.vector
                def _(vector, tg=tg, hc=hc, pd=pd):
                    vector.wait_ge(st.h, ev_dmm[(tg, hc)])
                    if tg >= 1:
                        vector.wait_ge(st.h, ev_dtr[(tg - 1, 7)])
                    sv.inc(vector.tensor_scalar(
                        dsc[:, hc * TG:(hc + 1) * TG], ps_d[pd][:, :],
                        swdt[:, hc:hc + 1], None, ALU.mult))
                ev_dsc[(tg, hc)] = sv.n

            for tcg in range(NKC):
                pt2 = tcg % 2
                tc_abs = tg * NKC + tcg
                @block.sync
                def _(sync, tg=tg, tcg=tcg, tc_abs=tc_abs):
                    k = tc_abs
                    if k == 0:
                        dx2s.wait(sync, 0, ev_x2st[nt_run - 1])
                    if k >= 2:
                        sync.wait_ge(sv.h, ev_oadd[k - 2])
                    dxb.dma(sync.dma_start(
                        out=x2b[:, (k % 2) * D:(k % 2) * D + D],
                        in_=x2_tc[tc_abs]), k % 2)
                ev_x2bld[tc_abs] = dxb.s[tc_abs % 2].n

                @block.tensor
                def _(tensor, tg=tg, tcg=tcg, pt2=pt2):
                    tensor.wait_ge(sv.h, ev_dsc[(tg, NKC - 1)])
                    k = tg * NKC + tcg
                    if k >= 1:
                        tensor.wait_ge(sv.h, ev_oadd[k - 1])
                    for hc in range(NKC):
                        st.inc(tensor.transpose(
                            ps_t2[pt2][:, hc * 128:(hc + 1) * 128],
                            dsc[:, hc * TG + tcg * 128:hc * TG + (tcg + 1) * 128],
                            eyet[:, :]))
                ev_dtr[(tg, tcg)] = st.n

                @block.vector
                def _(vector, tg=tg, tcg=tcg, pt2=pt2, tc_abs=tc_abs):
                    vector.wait_ge(st.h, ev_dtr[(tg, tcg)])
                    dxb.wait(vector, tc_abs % 2, ev_x2bld[tc_abs])
                    if tc_abs >= 2:
                        dys.wait(vector, tc_abs % 2, ev_ost[tc_abs - 2])
                    sv.inc(vector.tensor_tensor(
                        outb[:, (tc_abs % 2) * D:(tc_abs % 2) * D + D],
                        x2b[:, (tc_abs % 2) * D:(tc_abs % 2) * D + D],
                        ps_t2[pt2][:, :], ALU.add))
                ev_oadd[tc_abs] = sv.n

                @block.sync
                def _(sync, tc_abs=tc_abs):
                    sync.wait_ge(sv.h, ev_oadd[tc_abs])
                    dys.dma(sync.dma_start(
                        out=y_tc[tc_abs],
                        in_=outb[:, (tc_abs % 2) * D:(tc_abs % 2) * D + D]),
                        tc_abs % 2)
                ev_ost[tc_abs] = dys.s[tc_abs % 2].n

        if dbg == 2:
            wdt_dbg = nc.dram_tensor("f_wdT", [128, NIC * 1024], BF16,
                                     kind="ExternalOutput").ap()
            hq_dbg = nc.dram_tensor("f_hq", [128, 2 * NIC * TG], BF16,
                                    kind="ExternalOutput").ap()
            dsc_dbg = nc.dram_tensor("f_dsc", [128, NKC * TG], F32,
                                     kind="ExternalOutput").ap()

            @block.sync
            def _(sync):
                sync.wait_ge(sv.h, sv.n)
                sync.wait_ge(st.h, st.n)
                qs.dma(sync.dma_start(out=wdt_dbg, in_=wdT[:, :]))
                qs.dma(sync.dma_start(out=hq_dbg, in_=hq[:, :]))
                qs.dma(sync.dma_start(out=dsc_dbg, in_=dsc[:, :]))

        @block.sync
        def _(sync):
            sync.wait_ge(qs.h, qs.n)
            for d_ in all_slotdma:
                d_.wait_all(sync)
        @block.gpsimd
        def _(g):
            g.wait_ge(qg.h, qg.n)

        p3.close(); open_stacks.remove(p3)
        p2c.close(); open_stacks.remove(p2c)

    return nc


def kernel(L0_attn_x, L0_attn_norm_w, L0_attn_w_forget, L0_attn_w_input,
           L0_attn_w_output, L0_ffn_clip_x, L0_ffn_clip_down, L0_ffn_w_gate,
           L0_ffn_w_up, L0_ffn_w_down):
    nc = build_program(dbg=False)
    x = np.ascontiguousarray(np.asarray(L0_attn_x, dtype=np.float32))
    shared = {
        "wf": np.ascontiguousarray(np.asarray(L0_attn_w_forget, np.float32)),
        "wi": np.ascontiguousarray(np.asarray(L0_attn_w_input, np.float32)),
        "wo": np.ascontiguousarray(np.asarray(L0_attn_w_output, np.float32)),
        "nw": np.ascontiguousarray(np.asarray(L0_attn_norm_w, np.float32)),
        "cx": np.ascontiguousarray(np.asarray(L0_ffn_clip_x, np.float32)),
        "cd": np.ascontiguousarray(np.asarray(L0_ffn_clip_down, np.float32)),
        "wg": np.ascontiguousarray(np.asarray(L0_ffn_w_gate, np.float32)),
        "wu": np.ascontiguousarray(np.asarray(L0_ffn_w_up, np.float32)),
        "wd": np.ascontiguousarray(np.asarray(L0_ffn_w_down, np.float32)),
    }
    in_maps = [dict(shared, x=np.ascontiguousarray(x[c])) for c in range(8)]
    res = run_bass_kernel_spmd(nc, in_maps, core_ids=list(range(8)))
    return np.stack([np.asarray(res.results[c]["y"]) for c in range(8)], axis=0)
